# version 12
# baseline (speedup 1.0000x reference)
import numpy as np
import ml_dtypes
import concourse.bass as bass
import concourse.mybir as mybir
from concourse.bass_utils import run_bass_kernel_spmd
from contextlib import ExitStack

F32 = mybir.dt.float32
BF16 = mybir.dt.bfloat16
AF = mybir.ActivationFunctionType
ALU = mybir.AluOpType
AX = mybir.AxisListType
BF = ml_dtypes.bfloat16

NCORES = 8
NB = 2
NT = 16
D = 1024
SEQ = 2048
HID2 = 5460
HID = 2730
ALL_STAGES = ("A", "B0", "B1", "C", "D", "E")
FFN_ON = True
RELAX_INCW = False
PRECAST_ON = True


class Ctr:
    def __init__(self, h):
        self.h = h
        self.n = 0

    def inc(self, instr, k=1):
        self.n += k
        instr.then_inc(self.h, k)
        return instr

    def wait(self, eng, n=None):
        n = self.n if n is None else n
        if n > 0:
            eng.wait_ge(self.h, n)

    def incx(self, eng, instr, k=1):
        return self.inc(instr, k)

    def incw(self, eng, instr, k=1):
        self.inc(instr, k)
        if not RELAX_INCW:
            eng.wait_ge(self.h, self.n)
        return instr


class Slot:

    def __init__(self, handle):
        self.b = handle
        self.free = None

    def acquire(self, eng):
        if self.free is not None:
            c, n = self.free
            if n > 0:
                eng.wait_ge(c.h, n)

    def release(self, ctr):
        self.free = (ctr, ctr.n)


def bcast_last(ap3, nf):
    return bass.AP(ap3.tensor, ap3.offset, [ap3.ap[0], ap3.ap[1], [0, nf]])


def bcast_mid(ap2, nh):
    return bass.AP(ap2.tensor, ap2.offset, [ap2.ap[0], [0, nh], ap2.ap[1]])


class P:
    def __init__(self, nc, stages=ALL_STAGES):
        self.nc = nc
        self.stages = stages
        self.t, self.v, self.s, self.g, self.y = nc.tensor, nc.vector, nc.scalar, nc.gpsimd, nc.sync
        self.ctx = ExitStack()
        self.mm_i = 0
        self.cc_world = NCORES

    def nt(self, name):
        self.uid = getattr(self, "uid", 0) + 1
        return f"{name}_{self.uid}"

    def sb(self, name, shape, dt):
        return self.ctx.enter_context(self.nc.sbuf_tensor(self.nt(name), shape, dt))

    def next_mm(self):
        m = self.mm[self.mm_i % 4]
        self.mm_i += 1
        return m

    def powx(self, out, in_ap, bias_ap, pw):
        s = self.s
        self.sv.wait(s)
        self.ss.incw(s, s.activation(out=out, in_=in_ap, func=AF.Ln, bias=bias_ap, scale=1.0))
        self.ss.incw(s, s.activation(out=out, in_=out, func=AF.Exp, bias=self.zero1[:], scale=float(pw)))

    def build(self):
        nc = self.nc
        dt = nc.dram_tensor
        self.x_tok = dt("x_tok", [NB, 128, NT, D], F32, kind="ExternalInput").ap()
        self.x_feaT = dt("x_feaT", [NB, 128, 8, SEQ], F32, kind="ExternalInput").ap()
        self.out = dt("out", [NB, 128, NT, D], F32, kind="ExternalOutput").ap()
        self.wa_qkvT = dt("wa_qkvT", [128, 8, 3072], F32, kind="ExternalInput").ap()
        self.wa_outT = dt("wa_outT", [128, 8, D], F32, kind="ExternalInput").ap()
        self.ba_qkv = dt("ba_qkv", [128, 16], F32, kind="ExternalInput").ap()
        self.ba_bv = dt("ba_bv", [128, 8], F32, kind="ExternalInput").ap()
        self.ba_bout = dt("ba_bout", [1, D], F32, kind="ExternalInput").ap()
        self.wf_qkv = dt("wf_qkv", [2, 128, 8, 1536], F32, kind="ExternalInput").ap()
        self.wf_wo = dt("wf_wo", [2, 128, 4, D], F32, kind="ExternalInput").ap()
        self.wf_w1 = dt("wf_w1", [2, 128, 8, HID2], F32, kind="ExternalInput").ap()
        self.wf_w2 = dt("wf_w2", [2, 128, 22, D], F32, kind="ExternalInput").ap()
        self.wf_qs = dt("wf_qs", [1, 2, 64], F32, kind="ExternalInput").ap()
        self.wf_ks = dt("wf_ks", [1, 2, 64], F32, kind="ExternalInput").ap()
        self.wf_lng = dt("wf_lng", [128, 2, 8], F32, kind="ExternalInput").ap()
        self.wf_lnb = dt("wf_lnb", [128, 2, 8], F32, kind="ExternalInput").ap()
        self.wf_b1 = dt("wf_b1", [1, 2, HID2], F32, kind="ExternalInput").ap()
        self.wf_b2 = dt("wf_b2", [1, 2, D], F32, kind="ExternalInput").ap()
        self.w1_qT = dt("w1_qT", [128, 8, 256], F32, kind="ExternalInput").ap()
        self.w1_kT = dt("w1_kT", [128, 8, 256], F32, kind="ExternalInput").ap()
        self.w1_vT = dt("w1_vT", [128, 8, D], F32, kind="ExternalInput").ap()
        self.w1_oT = dt("w1_oT", [128, 8, D], F32, kind="ExternalInput").ap()
        self.w1_g = dt("w1_g", [1, 16], F32, kind="ExternalInput").ap()
        self.w1_b = dt("w1_b", [1, 16], F32, kind="ExternalInput").ap()
        self.we_wT = dt("we_wT", [128, 8, D], F32, kind="ExternalInput").ap()
        self.we_g = dt("we_g", [1, D], F32, kind="ExternalInput").ap()
        self.we_b = dt("we_b", [1, D], F32, kind="ExternalInput").ap()
        self.c_ident_bf = dt("c_ident_bf", [128, 128], BF16, kind="ExternalInput").ap()
        self.c_ident_f = dt("c_ident_f", [128, 128], F32, kind="ExternalInput").ap()
        self.c_ones_bf = dt("c_ones_bf", [1, 128], BF16, kind="ExternalInput").ap()
        self.c_ones_f = dt("c_ones_f", [1, 128], F32, kind="ExternalInput").ap()
        self.c_maskA = dt("c_maskA", [128, 256], BF16, kind="ExternalInput").ap()
        self.c_maskA0 = dt("c_maskA0", [128, 128], BF16, kind="ExternalInput").ap()
        self.c_maskF = dt("c_maskF", [128, 640], BF16, kind="ExternalInput").ap()
        self.c_triu = dt("c_triu", [128, 128], BF16, kind="ExternalInput").ap()
        self.c_psub = dt("c_psub", [128, 128], F32, kind="ExternalInput").ap()
        self.c_e0 = dt("c_e0", [1, 128], F32, kind="ExternalInput").ap()
        self.c_p127 = dt("c_p127", [128, 128], F32, kind="ExternalInput").ap()
        self.c_rot = dt("c_rot", [128, 4, NT, 64], F32, kind="ExternalInput").ap()
        self.x1_spill = dt("x1_spill", [128, NT, D], F32)
        self.w1_bf = dt("w1_bf", [2, 128, 8, 12, 512], BF16)
        self.x3_spill = dt("x3_spill", [128, NT, D], F32)
        self.ar_in = dt("ar_in", [1, 128], F32)
        self.ar_out = dt("ar_out", [1, 128], F32)

        E = self.ctx.enter_context
        self.sd = Ctr(E(nc.semaphore("sd")))
        self.sy_ = Ctr(E(nc.semaphore("sy")))
        self.sv = Ctr(E(nc.semaphore("sv")))
        self.ss = Ctr(E(nc.semaphore("ss")))
        self.st = Ctr(E(nc.semaphore("st")))
        self.scc = Ctr(E(nc.semaphore("scc")))

        self.X = self.sb("X", [128, NT, D], F32)
        self.XT = self.sb("XT", [128, 8, SEQ + 1], BF16)
        self.ident_bf = self.sb("ident_bf", [128, 128], BF16)
        self.ident_f = self.sb("ident_f", [128, 128], F32)
        self.ones_bf = self.sb("ones_bf", [1, 128], BF16)
        self.ones_f = self.sb("ones_f", [1, 128], F32)
        self.zero1 = self.sb("zero1", [128, 1], F32)
        self.eps1 = self.sb("eps1", [128, 1], F32)
        self.magic = self.sb("magic", [128, 1], F32)
        self.nmagic = self.sb("nmagic", [128, 1], F32)
        self.clipmax = self.sb("clipmax", [128, 1], F32)
        self.stats_e = self.sb("stats_e", [128, NB, NT, 2], F32)
        self.gmax = self.sb("gmax", [128, 1], F32)
        self.scale_b = self.sb("scale_b", [128, 1], F32)
        self.iscale_b = self.sb("iscale_b", [128, 1], F32)
        self.wf_lng_sb = self.sb("wf_lng_sb", [128, 2, 8], F32)
        self.wf_lnb_sb = self.sb("wf_lnb_sb", [128, 2, 8], F32)

        self.mm = [Slot(E(nc.psum_tensor(f"mm{i}", [128, 512], F32))) for i in range(4)]
        self.tsp = Slot(E(nc.psum_tensor("tsp", [128, 5, 128], BF16)))
        self.tsp2 = Slot(E(nc.psum_tensor("tsp2", [128, 5, 128], BF16)))
        self.po = Slot(E(nc.psum_tensor("po", [128, 512], F32)))
        self.po2 = Slot(E(nc.psum_tensor("po2", [128, 512], F32)))

        v, s, t, g, y = self.v, self.s, self.t, self.g, self.y
        self.sd.inc(g.dma_start(out=self.ident_bf[:], in_=self.c_ident_bf), 16)
        self.sd.inc(g.dma_start(out=self.ident_f[:], in_=self.c_ident_f), 16)
        self.sd.inc(g.dma_start(out=self.ones_bf[:], in_=self.c_ones_bf), 16)
        self.sd.inc(g.dma_start(out=self.ones_f[:], in_=self.c_ones_f), 16)
        self.sd.inc(g.dma_start(out=self.wf_lng_sb[:], in_=self.wf_lng), 16)
        self.sd.inc(g.dma_start(out=self.wf_lnb_sb[:], in_=self.wf_lnb), 16)
        self.sv.incw(v, v.memset(self.zero1[:], 0.0))
        self.sv.incw(v, v.memset(self.eps1[:], 1e-5))
        self.sv.incw(v, v.memset(self.magic[:], 12582912.0))
        self.sv.incw(v, v.memset(self.nmagic[:], -12582912.0))
        self.sv.incw(v, v.memset(self.clipmax[:], 1e12))
        self.sv.incw(v, v.memset(self.gmax[:], 0.0))
        for b in range(NB):
            self.load_x(b)
            if b == 0 and PRECAST_ON and ("B0" in self.stages or "B1" in self.stages):
                self.precast_w1()
            if "A" in self.stages:
                self.stage_A(b)
            for li, stg in ((0, "B0"), (1, "B1")):
                if stg in self.stages:
                    self.stage_B(b, li)
            if "C" in self.stages:
                self.stage_C(b)
            if "D" in self.stages:
                self.stage_D(b)
                if "E" in self.stages:
                    self.stage_D_eprep(b)
            if "E" in self.stages and b != NB - 1:
                self.ss.wait(y)
                self.sv.wait(y)
                for tc in range(4):
                    self.sy_.inc(y.dma_start(out=self.x3_spill.ap()[:, 4 * tc:4 * tc + 4, :],
                                             in_=self.X[:, 4 * tc:4 * tc + 4, :]), 16)
            if "E" not in self.stages:
                self.ss.wait(y)
                self.sv.wait(y)
                for tc in range(4):
                    self.sy_.inc(y.dma_start(out=self.out[b, :, 4 * tc:4 * tc + 4, :],
                                             in_=self.X[:, 4 * tc:4 * tc + 4, :]), 16)
        if "E" in self.stages:
            self.stage_E_all()
        y.wait_ge(self.sy_.h, self.sy_.n)
        self.ctx.close()
        return nc

    def load_x(self, b):
        y, v, s, t, g = self.y, self.v, self.s, self.t, self.g
        self.sv.wait(y)
        self.ss.wait(y)
        self.st.wait(y)
        self.sv.wait(g)
        self.ss.wait(g)
        self.st.wait(g)
        for tc in range(4):
            self.sy_.inc(y.dma_start(out=self.X[:, 4 * tc:4 * tc + 4, :],
                                     in_=self.x_tok[b, :, 4 * tc:4 * tc + 4, :]), 16)
        self.sy_.wait(v)
        self.sv.incw(v, v.memset(self.XT[:, :, 0:1], 0.0))
        self.sd.inc(g.dma_start(out=self.XT[:, :, 1:SEQ + 1], in_=self.x_feaT[b]), 16)

    def precast_w1(self):
        nc, g, y = self.nc, self.g, self.y
        with ExitStack() as c:
            E = c.enter_context
            scr = [E(nc.sbuf_tensor(self.nt("pc_scr"), [128, 8, 512], BF16)) for _ in range(2)]
            slots = [Slot(s_) for s_ in scr]
            for li in range(2):
                for cc in range(12):
                    p_, half = cc // 2, cc % 2
                    a0 = 512 * p_
                    a1 = min(a0 + 512, HID)
                    w = a1 - a0
                    c0 = a0 if half == 0 else HID + a0
                    i = cc % 2
                    slots[i].acquire(g)
                    self.sd.inc(g.dma_start(out=scr[i][:, :, 0:w], in_=self.wf_w1[li, :, :, c0:c0 + w]), 16)
                    self.sd.wait(y)
                    self.sy_.inc(y.dma_start(out=self.w1_bf.ap()[li, :, :, cc, :], in_=scr[i][:]), 16)
                    slots[i].release(self.sy_)
            g.wait_ge(self.sy_.h, self.sy_.n)
            y.wait_ge(self.sy_.h, self.sy_.n)

    def emit_w2_part(self, it, p_, ti, wch, sv_hid, dacc, w2, hTb, hT_s):
        t, v, s = self.t, self.v, self.s
        nhb = (wch + 127) // 128
        hT_s[it].acquire(t)
        self.sv.wait(t, sv_hid)
        banks = (self.tsp, self.tsp2)

        def evac_mm(hl, n_t):
            hw = min(128, wch - 128 * hl)
            sl = banks[hl % 2]
            if hl % 2 == 0:
                v.wait_ge(self.st.h, n_t)
                self.sv.incx(v, v.tensor_copy(out=hTb[0:hw, it, hl, :], in_=sl.b[0:hw, 2 + hl // 2, :]))
                sl.release(self.sv)
                t.wait_ge(self.sv.h, self.sv.n)
            else:
                s.wait_ge(self.st.h, n_t)
                self.ss.incx(s, s.copy(out=hTb[0:hw, it, hl, :], in_=sl.b[0:hw, 2 + hl // 2, :]))
                sl.release(self.ss)
                t.wait_ge(self.ss.h, self.ss.n)
            hb = 4 * p_ + hl
            for oc in range(2):
                dm = dacc[2 * ti + oc]
                self.st.inc(t.matmul(dm.b[:], hTb[0:hw, it, hl, :], w2[0:hw, hb, 512 * oc:512 * oc + 512],
                                     start=(p_ == 0 and hl == 0), stop=False, skip_group_check=True))

        pend = None
        for hl in range(nhb):
            hw = min(128, wch - 128 * hl)
            sl = banks[hl % 2]
            sl.acquire(t)
            self.st.inc(t.transpose(sl.b[0:hw, 2 + hl // 2, :],
                                    self.hidc_cur[it][:, 128 * hl:128 * hl + hw], self.ident_bf[:]))
            n_t = self.st.n
            if pend is not None:
                evac_mm(*pend)
            pend = (hl, n_t)
        self.hid_s_cur[it].release(self.st)
        if pend is not None:
            evac_mm(*pend)
        hT_s[it].release(self.st)

    def lazy_xt(self, T):
        t, v, s = self.t, self.v, self.s
        xbf = self.xbf_s
        xbf.acquire(s)
        self.sv.wait(s)
        self.ss.incx(s, s.copy(out=xbf.b[:], in_=self.X[:, T, :]))
        for db in range(8):
            self.tsp.acquire(t)
            self.ss.wait(t)
            self.st.inc(t.transpose(self.tsp.b[:, db % 5, :], xbf.b[:, 128 * db:128 * db + 128],
                                    self.ident_bf[:]))
            self.tsp.release(self.st)
            self.st.wait(v)
            self.sv.incw(v, v.tensor_copy(out=self.XT[:, db, 1 + 128 * T:1 + 128 * T + 128],
                                      in_=self.tsp.b[:, db % 5, :]))
            self.tsp.release(self.sv)
        xbf.release(self.st)

    def build_gbc(self, gbc):
        t, v, g = self.t, self.v, self.g
        with self.nc.sbuf_tensor(self.nt("grow"), [1, 2, D], F32) as grow:
            self.sv.wait(g)
            self.sd.inc(g.dma_start(out=grow[0:1, 0, :], in_=self.we_g), 16)
            self.sd.inc(g.dma_start(out=grow[0:1, 1, :], in_=self.we_b), 16)
            for which in range(2):
                for oc in range(2):
                    m = self.next_mm()
                    m.acquire(t)
                    t.wait_ge(self.sd.h, self.sd.n)
                    self.st.inc(t.matmul(m.b[:], self.ones_f[:], grow[0:1, which, 512 * oc:512 * oc + 512],
                                         start=True, stop=True, skip_group_check=True))
                    self.st.wait(v)
                    self.sv.incw(v, v.tensor_copy(out=gbc[:, which, 512 * oc:512 * oc + 512], in_=m.b[:]))
                    m.release(self.sv)

    def ln_combine(self, bn):
        v = self.v
        self.sv.incw(v, v.tensor_tensor(out=bn[:, 16:18], in0=bn[:, 12:14], in1=bn[:, 14:16], op=ALU.add))
        self.sv.incw(v, v.tensor_scalar(out=bn[:, 16:18], in0=bn[:, 16:18], scalar1=0.5, scalar2=None,
                                    op0=ALU.mult, op1=ALU.bypass))
        self.sv.incw(v, v.tensor_tensor(out=bn[:, 18:19], in0=bn[:, 12:13], in1=bn[:, 14:15], op=ALU.subtract))
        self.sv.incw(v, v.tensor_tensor(out=bn[:, 18:19], in0=bn[:, 18:19], in1=bn[:, 18:19], op=ALU.mult))
        self.sv.incw(v, v.scalar_tensor_tensor(out=bn[:, 17:18], in0=bn[:, 18:19], scalar=0.25,
                                           in1=bn[:, 17:18], op0=ALU.mult, op1=ALU.add))

    def ln_apply_tile(self, T):
        v, s = self.v, self.s
        bn = self.bn
        self.ss.wait(v)
        self.sv.incw(v, v.bn_stats(bn[:, 0:6], self.X[:, T, 0:512]))
        self.sv.incw(v, v.bn_stats(bn[:, 6:12], self.X[:, T, 512:1024]))
        self.sv.incw(v, v.bn_aggr(bn[:, 12:14], bn[:, 0:6]))
        self.sv.incw(v, v.bn_aggr(bn[:, 14:16], bn[:, 6:12]))
        self.ln_combine(bn)
        rstd, nmr = bn[:, 19:20], bn[:, 20:21]
        self.powx(rstd, bn[:, 17:18], self.eps1[:], -0.5)
        self.ss.wait(v)
        self.sv.incw(v, v.tensor_tensor(out=nmr, in0=bn[:, 16:17], in1=rstd, op=ALU.mult))
        self.sv.incw(v, v.tensor_scalar(out=nmr, in0=nmr, scalar1=-1.0, scalar2=None,
                                    op0=ALU.mult, op1=ALU.bypass))
        self.sv.wait(s)
        self.ss.incw(s, s.activation(out=self.X[:, T, :], in_=self.X[:, T, :], func=AF.Identity,
                                 bias=nmr, scale=rstd))

    def softmax_block(self, psum_aps, mask_aps, scf_slot, probs_slot, sm, ncols,
                      scf_ap=None, probs_ap=None):
        v, s = self.v, self.s
        scf_b = scf_ap if scf_ap is not None else scf_slot.b[:]
        probs_b = probs_ap if probs_ap is not None else probs_slot.b[:]
        c0 = 0
        for pap, map_ in zip(psum_aps, mask_aps):
            w = pap.shape[-1]
            self.st.wait(v)
            self.sv.incw(v, v.tensor_tensor(out=scf_b[:, c0:c0 + w], in0=pap, in1=map_, op=ALU.add))
            c0 += w
        assert c0 == ncols
        self.sv.incx(v, v.reduce_max(out=sm[:, 0:1], in_=scf_b[:, 0:ncols], axis=AX.X, negate=True))
        self.sv.wait(s)
        self.ss.incx(s, s.activation(out=probs_b[:, 0:ncols], in_=scf_b[:, 0:ncols], func=AF.Exp,
                                 bias=sm[:, 0:1], accum_out=sm[:, 1:2]))
        scf_slot.release(self.ss)
        self.ss.wait(v)
        self.sv.incw(v, v.reciprocal(out=sm[:, 2:3], in_=sm[:, 1:2]))
        self.sv.incx(v, v.tensor_scalar(out=probs_b[:, 0:ncols], in0=probs_b[:, 0:ncols],
                                    scalar1=sm[:, 2:3], scalar2=None, op0=ALU.mult, op1=ALU.bypass))

    def softmax_psum(self, psum_ap, probs_slot, sm, ncols, probs_ap=None):
        v, s = self.v, self.s
        probs_b = probs_ap if probs_ap is not None else probs_slot.b[:]
        self.st.wait(v)
        self.sv.incx(v, v.reduce_max(out=sm[:, 0:1], in_=psum_ap, axis=AX.X, negate=True))
        self.sv.wait(s)
        self.ss.incx(s, s.activation(out=probs_b[:, 0:ncols], in_=psum_ap, func=AF.Exp,
                                     bias=sm[:, 0:1], accum_out=sm[:, 1:2]))
        self.ss.wait(v)
        self.sv.incw(v, v.reciprocal(out=sm[:, 2:3], in_=sm[:, 1:2]))
        self.sv.incx(v, v.tensor_scalar(out=probs_b[:, 0:ncols], in0=probs_b[:, 0:ncols],
                                        scalar1=sm[:, 2:3], scalar2=None, op0=ALU.mult, op1=ALU.bypass))

    def softmax_psum2(self, papA, papB, probs_slot, sm, ncols, probs_ap):
        v, s = self.v, self.s
        if papA is None:
            return self.softmax_psum(papB, probs_slot, sm, ncols, probs_ap)
        wA = papA.shape[-1]
        self.st.wait(v)
        self.sv.incx(v, v.reduce_max(out=sm[:, 0:1], in_=papA, axis=AX.X, negate=True))
        self.sv.incw(v, v.reduce_max(out=sm[:, 3:4], in_=papB, axis=AX.X, negate=True))
        self.sv.incx(v, v.tensor_tensor(out=sm[:, 0:1], in0=sm[:, 0:1], in1=sm[:, 3:4], op=ALU.min))
        self.sv.wait(s)
        self.ss.incx(s, s.activation(out=probs_ap[:, 0:wA], in_=papA, func=AF.Exp,
                                     bias=sm[:, 0:1], accum_out=sm[:, 1:2]))
        self.ss.incx(s, s.activation(out=probs_ap[:, wA:ncols], in_=papB, func=AF.Exp,
                                     bias=sm[:, 0:1], accum_out=sm[:, 3:4]))
        self.ss.wait(v)
        self.sv.incw(v, v.tensor_tensor(out=sm[:, 1:2], in0=sm[:, 1:2], in1=sm[:, 3:4], op=ALU.add))
        self.sv.incw(v, v.reciprocal(out=sm[:, 2:3], in_=sm[:, 1:2]))
        self.sv.incx(v, v.tensor_scalar(out=probs_ap[:, 0:ncols], in0=probs_ap[:, 0:ncols],
                                        scalar1=sm[:, 2:3], scalar2=None, op0=ALU.mult, op1=ALU.bypass))

    def transpose_to(self, src_ap_list, dst_ap_list, ident):
        t, v, s = self.t, self.v, self.s
        banks = (self.tsp, self.tsp2)
        pend = None
        items = list(zip(src_ap_list, dst_ap_list))
        for i, (sap, dap) in enumerate(items):
            b = banks[i % 2]
            sl = (i // 2) % 5
            b.acquire(t)
            self.st.inc(t.transpose(b.b[0:sap.shape[-1], sl, :], sap, ident))
            n_t = self.st.n
            if pend is not None:
                self._tp_evac(*pend)
            pend = (b, (i // 2) % 5, items[i][0].shape[-1], dap, n_t, i)
        if pend is not None:
            self._tp_evac(*pend)

    def _tp_evac(self, b, sl, w, dap, n_t, i):
        v, s = self.v, self.s
        if i % 2 == 0:
            v.wait_ge(self.st.h, n_t)
            self.sv.incx(v, v.tensor_copy(out=dap, in_=b.b[0:w, sl, :]))
            b.release(self.sv)
        else:
            s.wait_ge(self.st.h, n_t)
            self.ss.incx(s, s.copy(out=dap, in_=b.b[0:w, sl, :]))
            b.release(self.ss)

    def stage_A(self, b):
        nc, t, v, s, g, y = self.nc, self.t, self.v, self.s, self.g, self.y
        with ExitStack() as c:
            E = c.enter_context
            wqkv = E(nc.sbuf_tensor(self.nt("a_wqkv"), [128, 8, 3072], BF16))
            wout = E(nc.sbuf_tensor(self.nt("a_wout"), [128, 8, D], BF16))
            bqk = E(nc.sbuf_tensor(self.nt("a_bqk"), [128, 16], F32))
            bv = E(nc.sbuf_tensor(self.nt("a_bv"), [128, 8], BF16))
            brow = E(nc.sbuf_tensor(self.nt("a_brow"), [1, D], BF16))
            browf = E(nc.sbuf_tensor(self.nt("a_browf"), [1, D], F32))
            qk_c = E(nc.sbuf_tensor(self.nt("a_qk"), [128, 16, 512], BF16))
            kcar = E(nc.sbuf_tensor(self.nt("a_kcar"), [128, 8, 128], BF16))
            v_c = E(nc.sbuf_tensor(self.nt("a_v"), [128, 4, D], BF16))
            vcar = E(nc.sbuf_tensor(self.nt("a_vcar"), [128, D], BF16))
            probs = E(nc.sbuf_tensor(self.nt("a_probs"), [128, 3, 256], BF16))
            pT = E(nc.sbuf_tensor(self.nt("a_pT"), [128, 3, 2, 128], BF16))
            oT = E(nc.sbuf_tensor(self.nt("a_oT"), [128, 8, 128], BF16))
            sm = E(nc.sbuf_tensor(self.nt("a_sm"), [128, 3, 4], F32))
            self.bn = E(nc.sbuf_tensor(self.nt("a_bn"), [128, 24], F32))
            maskA = E(nc.sbuf_tensor(self.nt("a_maskA"), [128, 256], BF16))
            maskA0 = E(nc.sbuf_tensor(self.nt("a_maskA0"), [128, 128], BF16))

            qk_s, v_s = Slot(qk_c), Slot(v_c)
            oT_s = Slot(oT)
            probs_s = [Slot(None), Slot(None), Slot(None)]
            pT_s = [Slot(None), Slot(None), Slot(None)]
            kcar_s, vcar_s = Slot(kcar), Slot(vcar)

            self.st.wait(g)
            self.sd.inc(g.dma_start(out=maskA[:], in_=self.c_maskA), 16)
            self.sd.inc(g.dma_start(out=maskA0[:], in_=self.c_maskA0), 16)
            self.sd.inc(g.dma_start(out=wqkv[:], in_=self.wa_qkvT), 16)
            self.sd.inc(g.dma_start(out=wout[:], in_=self.wa_outT), 16)
            self.sd.inc(g.dma_start(out=bqk[:], in_=self.ba_qkv), 16)
            self.sd.inc(g.dma_start(out=bv[:], in_=self.ba_bv), 16)
            self.sd.inc(g.dma_start(out=browf[:], in_=self.ba_bout), 16)

            self.sd.wait(s)
            self.ss.incw(s, s.mul(out=bqk[:, 0:8], in_=bqk[:, 0:8], mul=0.125))

            t.wait_ge(self.sd.h, self.sd.n)
            for oc in range(2):
                m = self.next_mm()
                m.acquire(t)
                for db in range(8):
                    self.st.inc(t.matmul(m.b[0:1, :], bv[:, db:db + 1], wout[:, db, 512 * oc:512 * oc + 512],
                                         start=(db == 0), stop=(db == 7), skip_group_check=True))
                self.st.wait(v)
                self.sv.incw(v, v.tensor_tensor(out=browf[0:1, 512 * oc:512 * oc + 512],
                                            in0=m.b[0:1, :], in1=browf[0:1, 512 * oc:512 * oc + 512], op=ALU.add))
                m.release(self.sv)
            self.sv.incw(v, v.tensor_copy(out=brow[:], in_=browf[:]))

            self.sd.wait(t)
            self.sv.wait(t)
            self.sy_.wait(t)
            self.sy_.wait(v)

            for tc in range(4):
                qk_s.acquire(s)
                qk_s.acquire(t)
                for ob in range(16):
                    m = self.next_mm()
                    m.acquire(t)
                    for db in range(8):
                        self.st.inc(t.matmul(m.b[:], wqkv[:, db, 128 * ob:128 * ob + 128],
                                             self.XT[:, db, 1 + 512 * tc:1 + 512 * tc + 512],
                                             start=(db == 0), stop=(db == 7), skip_group_check=True))
                    self.st.wait(s)
                    self.ss.incw(s, s.activation(out=qk_c[:, ob, :], in_=m.b[:], func=AF.Identity,
                                             bias=bqk[:, ob:ob + 1], scale=(0.125 if ob < 8 else 1.0)))
                    m.release(self.ss)
                v_s.acquire(v)
                for tt in range(4):
                    T = 4 * tc + tt
                    for vc in range(2):
                        m = self.next_mm()
                        m.acquire(t)
                        for db in range(8):
                            self.st.inc(t.matmul(m.b[:], self.XT[:, db, 1 + 128 * T:1 + 128 * T + 128],
                                                 wqkv[:, db, 2048 + 512 * vc:2048 + 512 * vc + 512],
                                                 start=(db == 0), stop=(db == 7), skip_group_check=True))
                        self.st.wait(v)
                        self.sv.incw(v, v.tensor_copy(out=v_c[:, tt, 512 * vc:512 * vc + 512], in_=m.b[:]))
                        m.release(self.sv)

                for tt in range(4):
                    T = 4 * tc + tt
                    ncols = 128 if T == 0 else 256
                    nkt = ncols // 128

                    def a_tail(h, par, sv_n):
                        hp, hh = h // 2, h % 2
                        opo = self.po if hp % 2 == 0 else self.po2
                        pT_s[par].acquire(v)
                        pT_s[par].acquire(s)
                        t.wait_ge(self.sv.h, sv_n)
                        self.transpose_to([probs[:, par, 128 * kt:128 * kt + 128] for kt in range(nkt)],
                                          [pT[:, par, kt, :] for kt in range(nkt)], self.ident_bf[:])
                        probs_s[par].release(self.st)
                        t.wait_ge(self.sv.h, self.sv.n)
                        t.wait_ge(self.ss.h, self.ss.n)
                        if hh == 0:
                            opo.acquire(t)
                        tp = (0, 64 * hh)
                        od = opo.b[64 * hh:64 * hh + 64, 0:128]
                        if T == 0:
                            self.st.inc(t.matmul(od, v_c[:, tt, 64 * h:64 * h + 64], pT[:, par, 0, :],
                                                 start=True, stop=True, tile_position=tp,
                                                 skip_group_check=True))
                        else:
                            vprev = vcar[:, 64 * h:64 * h + 64] if tt == 0 else v_c[:, tt - 1, 64 * h:64 * h + 64]
                            self.st.inc(t.matmul(od, vprev, pT[:, par, 0, :],
                                                 start=True, stop=False, tile_position=tp,
                                                 skip_group_check=True))
                            self.st.inc(t.matmul(od, v_c[:, tt, 64 * h:64 * h + 64], pT[:, par, 1, :],
                                                 start=False, stop=True, tile_position=tp,
                                                 skip_group_check=True))
                        pT_s[par].release(self.st)
                        if hh == 1:
                            oT_s.acquire(v)
                            v.wait_ge(self.st.h, self.st.n)
                            self.sv.incw(v, v.tensor_copy(out=oT[:, hp, :], in_=opo.b[:, 0:128]))
                            opo.release(self.sv)

                    from collections import deque
                    pendq = deque()
                    self.ss.wait(t)
                    self.sv.wait(t)
                    for h in range(16):
                        bp, pbq, pbk = 64 * (h % 2), h // 2, 8 + h // 2
                        par = h % 3
                        msc = self.next_mm()
                        msc.acquire(t)
                        if T == 0:
                            self.st.inc(t.matmul(msc.b[:, 0:128], qk_c[bp:bp + 64, pbq, 0:128],
                                                 qk_c[bp:bp + 64, pbk, 0:128], start=True, stop=False,
                                                 skip_group_check=True))
                        elif tt == 0:
                            self.st.inc(t.matmul(msc.b[:, 0:128], qk_c[bp:bp + 64, pbq, 0:128],
                                                 kcar[bp:bp + 64, h // 2, :], start=True, stop=False,
                                                 skip_group_check=True))
                            self.st.inc(t.matmul(msc.b[:, 128:256], qk_c[bp:bp + 64, pbq, 0:128],
                                                 qk_c[bp:bp + 64, pbk, 0:128], start=False, stop=False,
                                                 skip_group_check=True))
                        else:
                            self.st.inc(t.matmul(msc.b[:, 0:256], qk_c[bp:bp + 64, pbq, 128 * tt:128 * tt + 128],
                                                 qk_c[bp:bp + 64, pbk, 128 * (tt - 1):128 * (tt - 1) + 256],
                                                 start=True, stop=False, skip_group_check=True))
                        mask = maskA0[:, 0:128] if T == 0 else maskA[:, 0:256]
                        self.st.inc(t.matmul(msc.b[:, 0:ncols], self.ident_bf[:], mask,
                                             start=False, stop=True, skip_group_check=True))
                        if len(pendq) >= 3:
                            a_tail(*pendq.popleft())
                        probs_s[par].acquire(s)
                        self.softmax_psum(msc.b[:, 0:ncols], probs_s[par],
                                          sm[:, par, :], ncols, probs[:, par, :])
                        msc.release(self.ss)
                        pendq.append((h, par, self.sv.n))
                    while pendq:
                        a_tail(*pendq.popleft())

                    for oc in range(2):
                        m = self.next_mm()
                        m.acquire(t)
                        self.sv.wait(t)
                        for fb in range(8):
                            self.st.inc(t.matmul(m.b[:], oT[:, fb, :], wout[:, fb, 512 * oc:512 * oc + 512],
                                                 start=(fb == 0), stop=False, skip_group_check=True))
                        self.st.inc(t.matmul(m.b[:], self.ones_bf[:], brow[0:1, 512 * oc:512 * oc + 512],
                                             start=False, stop=True, skip_group_check=True))
                        self.st.wait(v)
                        self.sv.incw(v, v.tensor_tensor(out=self.X[:, T, 512 * oc:512 * oc + 512],
                                                    in0=m.b[:], in1=self.X[:, T, 512 * oc:512 * oc + 512],
                                                    op=ALU.add))
                        m.release(self.sv)
                    oT_s.release(self.st)
                    self.ln_apply_tile(T)
                    if "B0" in self.stages or "C" in self.stages:
                        self.ss.wait(y)
                        self.sy_.inc(y.dma_start(out=self.x1_spill.ap()[:, T, :], in_=self.X[:, T, :]), 16)
                if tc < 3:
                    kcar_s.acquire(v)
                    vcar_s.acquire(v)
                    self.st.wait(v)
                    self.sv.incw(v, v.tensor_copy(out=kcar[:], in_=qk_c[:, 8:16, 384:512]))
                    self.sv.incw(v, v.tensor_copy(out=vcar[:], in_=v_c[:, 3, :]))
                    kcar_s.release(self.sv)
                    vcar_s.release(self.sv)
                qk_s.release(self.st)
                v_s.release(self.st)

    def stage_B(self, b, li):
        nc, t, v, s, g, y = self.nc, self.t, self.v, self.s, self.g, self.y
        with ExitStack() as c:
            E = c.enter_context
            self.bn = E(nc.sbuf_tensor(self.nt("b_bn"), [128, 24], F32))
            with ExitStack() as p1:
                E1 = p1.enter_context
                self.xbf_s = Slot(E1(nc.sbuf_tensor(self.nt("b_xbf"), [128, D], BF16)))
                wqkv = E1(nc.sbuf_tensor(self.nt("b_wqkv"), [128, 8, 1536], BF16))
                wo = E1(nc.sbuf_tensor(self.nt("b_wo"), [128, 4, D], BF16))
                qsb = E1(nc.sbuf_tensor(self.nt("b_qsb"), [128, 64], F32))
                ksb = E1(nc.sbuf_tensor(self.nt("b_ksb"), [128, 64], F32))
                qsrow = E1(nc.sbuf_tensor(self.nt("b_qsrow"), [1, 128], F32))
                kT = E1(nc.sbuf_tensor(self.nt("b_kT"), [128, 4, SEQ], BF16))
                vfull = E1(nc.sbuf_tensor(self.nt("b_v"), [128, NT, 512], BF16))
                qf = E1(nc.sbuf_tensor(self.nt("b_qf"), [128, 512], F32))
                kf = E1(nc.sbuf_tensor(self.nt("b_kf"), [128, 512], F32))
                rtmp = E1(nc.sbuf_tensor(self.nt("b_rtmp"), [128, 512], F32))
                qe = E1(nc.sbuf_tensor(self.nt("b_qe"), [128, 512], BF16))
                ke = E1(nc.sbuf_tensor(self.nt("b_ke"), [128, 512], BF16))
                qT_t = E1(nc.sbuf_tensor(self.nt("b_qT"), [128, 2, 4, 128], BF16))
                scf = E1(nc.sbuf_tensor(self.nt("b_scf"), [128, 1, 640], F32))
                probs = E1(nc.sbuf_tensor(self.nt("b_probs"), [128, 3, 640], BF16))
                pT = E1(nc.sbuf_tensor(self.nt("b_pT"), [128, 3, 5, 128], BF16))
                oT = E1(nc.sbuf_tensor(self.nt("b_oT"), [128, 4, 128], BF16))
                sm = E1(nc.sbuf_tensor(self.nt("b_sm"), [128, 3, 4], F32))
                nrm = E1(nc.sbuf_tensor(self.nt("b_nrm"), [128, 2, 8], F32))
                rot = E1(nc.sbuf_tensor(self.nt("b_rot"), [128, 4, NT, 64], F32))
                rotscr = E1(nc.sbuf_tensor(self.nt("b_rotscr"), [128, 8, 64], F32))
                maskF = E1(nc.sbuf_tensor(self.nt("b_maskF"), [128, 640], BF16))

                self.st.wait(g)
                self.ss.wait(g)
                self.sv.wait(g)
                self.sd.inc(g.dma_start(out=rot[:], in_=self.c_rot), 16)
                self.sd.inc(g.dma_start(out=maskF[:], in_=self.c_maskF), 16)
                self.sd.inc(g.dma_start(out=wqkv[:], in_=self.wf_qkv[li]), 16)
                self.sd.inc(g.dma_start(out=wo[:], in_=self.wf_wo[li]), 16)
                self.sd.inc(g.dma_start(out=qsrow[0:1, 0:64], in_=self.wf_qs[0:1, li, :]), 16)
                self.sd.inc(g.dma_start(out=qsrow[0:1, 64:128], in_=self.wf_ks[0:1, li, :]), 16)
                m = self.next_mm()
                m.acquire(t)
                t.wait_ge(self.sd.h, self.sd.n)
                self.st.inc(t.matmul(m.b[:, 0:128], self.ones_f[:], qsrow[:], start=True, stop=True,
                                     skip_group_check=True))
                self.st.wait(v)
                self.sv.incw(v, v.tensor_scalar(out=qsb[:], in0=m.b[:, 0:64], scalar1=8.0, scalar2=None,
                                            op0=ALU.mult, op1=ALU.bypass))
                self.sv.incw(v, v.tensor_copy(out=ksb[:], in_=m.b[:, 64:128]))
                m.release(self.sv)

                qf_s, kf_s, rtmp_s = Slot(qf), Slot(kf), Slot(rtmp)
                qe_s, ke_s = Slot(qe), Slot(ke)
                qT_s = [Slot(None), Slot(None)]
                oT_s = Slot(oT)
                scf_s = [Slot(None), Slot(None)]
                probs_s = [Slot(None), Slot(None), Slot(None)]
                pT_s = [Slot(None), Slot(None), Slot(None)]
                self.st.wait(v)
                self.st.wait(s)
                prep_marks = {}

                def prep(T):
                    pr = T % 2
                    self.lazy_xt(T)

                    def hsl(db, T=T):
                        off = 1 if db < 4 else 0
                        return self.XT[:, db, off + 128 * T: off + 128 * T + 128]

                    for part in range(3):
                        m = self.next_mm()
                        m.acquire(t)
                        self.sv.wait(t)
                        for db in range(8):
                            self.st.inc(t.matmul(m.b[:], hsl(db), wqkv[:, db, 512 * part:512 * part + 512],
                                                 start=(db == 0), stop=(db == 7), skip_group_check=True))
                        if part == 0:
                            qf_s.acquire(s)
                            self.st.wait(s)
                            self.ss.incw(s, s.copy(out=qf[:], in_=m.b[:]))
                            m.release(self.ss)
                        elif part == 1:
                            kf_s.acquire(v)
                            self.st.wait(v)
                            self.sv.incw(v, v.tensor_copy(out=kf[:], in_=m.b[:]))
                            m.release(self.sv)
                        else:
                            self.st.wait(v)
                            self.sv.incw(v, v.tensor_copy(out=vfull[:, T, :], in_=m.b[:]))
                            m.release(self.sv)

                    for which in range(2):
                        src = qf if which == 0 else kf
                        src_s = qf_s if which == 0 else kf_s
                        dst = qe if which == 0 else ke
                        dst_s = qe_s if which == 0 else ke_s
                        scl = qsb if which == 0 else ksb
                        ci, si = (0, 1) if which == 0 else (2, 3)
                        src3 = src[:].rearrange("p (h f) -> p h f", h=8)
                        scf_s[0].acquire(v)
                        self.ss.wait(v)
                        sqv = scf[:, 0, 0:512].rearrange("p (h f) -> p h f", h=8)
                        self.sv.incw(v, v.tensor_tensor(out=sqv, in0=src3, in1=src3, op=ALU.mult))
                        self.sv.incw(v, v.reduce_sum(out=nrm[:, 0, :], in_=sqv, axis=AX.X))
                        self.powx(nrm[:, 1, :], nrm[:, 0, :], self.zero1[:], -0.5)
                        self.ss.wait(v)
                        self.sv.incw(v, v.tensor_scalar(out=nrm[:, 1, :], in0=nrm[:, 1, :],
                                                    scalar1=self.clipmax[:, 0:1], scalar2=None,
                                                    op0=ALU.min, op1=ALU.bypass))
                        qh = sqv
                        rb = bcast_last(nrm[:, 1, :].rearrange("p (h a) -> p h a", a=1), 64)
                        self.sv.incw(v, v.tensor_tensor(out=qh, in0=src3, in1=rb, op=ALU.mult))
                        src_s.release(self.sv)
                        self.sv.incw(v, v.tensor_tensor(out=qh, in0=qh, in1=bcast_mid(scl[:], 8), op=ALU.mult))
                        cosb = bcast_mid(rot[:, ci, T, :], 8)
                        sin_lo = bcast_mid(rot[:, si, T, 0:32], 8)
                        sin_hi = bcast_mid(rot[:, si, T, 32:64], 8)
                        rtmp_s.acquire(v)
                        tmp3 = rtmp[:].rearrange("p (h f) -> p h f", h=8)
                        self.sv.incw(v, v.tensor_tensor(out=tmp3, in0=qh, in1=cosb, op=ALU.mult))
                        self.sv.incw(v, v.tensor_tensor(out=rotscr[:, :, 0:32], in0=qh[:, :, 0:32],
                                                    in1=sin_hi, op=ALU.mult))
                        self.sv.incw(v, v.tensor_tensor(out=rotscr[:, :, 32:64], in0=qh[:, :, 32:64],
                                                    in1=sin_lo, op=ALU.mult))
                        scf_s[0].release(self.sv)
                        self.sv.incw(v, v.tensor_tensor(out=tmp3[:, :, 32:64], in0=tmp3[:, :, 32:64],
                                                    in1=rotscr[:, :, 0:32], op=ALU.add))
                        self.sv.incw(v, v.tensor_tensor(out=tmp3[:, :, 0:32], in0=tmp3[:, :, 0:32],
                                                    in1=rotscr[:, :, 32:64], op=ALU.subtract))
                        dst_s.acquire(v)
                        self.sv.incw(v, v.tensor_copy(out=dst[:], in_=rtmp[:]))
                        rtmp_s.release(self.sv)
                        if which == 0:
                            qT_s[pr].acquire(v)
                            qT_s[pr].acquire(s)
                            self.sv.wait(t)
                            self.transpose_to([qe[:, 128 * cc:128 * cc + 128] for cc in range(4)],
                                              [qT_t[:, pr, cc, :] for cc in range(4)], self.ident_bf[:])
                        else:
                            self.sv.wait(t)
                            self.transpose_to([ke[:, 128 * cc:128 * cc + 128] for cc in range(4)],
                                              [kT[:, cc, 128 * T:128 * T + 128] for cc in range(4)],
                                              self.ident_bf[:])
                        dst_s.release(self.st)
                    prep_marks[T] = (self.sv.n, self.ss.n)

                prep(0)
                for T in range(NT):
                    if T + 1 < NT:
                        prep(T + 1)
                    pr = T % 2

                    nk = min(T, 4)
                    ncols = 128 * (nk + 1)

                    def attn_tail(h, par, sv_n):
                        hp, hh = h // 2, h % 2
                        opo = self.po if hp % 2 == 0 else self.po2
                        pT_s[par].acquire(v)
                        pT_s[par].acquire(s)
                        t.wait_ge(self.sv.h, sv_n)
                        self.transpose_to([probs[:, par, 128 * kt:128 * kt + 128] for kt in range(nk + 1)],
                                          [pT[:, par, kt, :] for kt in range(nk + 1)], self.ident_bf[:])
                        probs_s[par].release(self.st)
                        t.wait_ge(self.sv.h, self.sv.n)
                        t.wait_ge(self.ss.h, self.ss.n)
                        if hh == 0:
                            opo.acquire(t)
                        tp = (0, 64 * hh)
                        od = opo.b[64 * hh:64 * hh + 64, 0:128]
                        for kt in range(nk + 1):
                            ktabs = T - nk + kt
                            self.st.inc(t.matmul(od, vfull[:, ktabs, 64 * h:64 * h + 64], pT[:, par, kt, :],
                                                 start=(kt == 0), stop=(kt == nk),
                                                 tile_position=tp, skip_group_check=True))
                        pT_s[par].release(self.st)
                        if hh == 1:
                            oT_s.acquire(v)
                            v.wait_ge(self.st.h, self.st.n)
                            self.sv.incw(v, v.tensor_copy(out=oT[:, hp, :], in_=opo.b[:, 0:128]))
                            opo.release(self.sv)

                    from collections import deque
                    pendq = deque()
                    self.sv.wait(t, prep_marks[T][0])
                    self.ss.wait(t, prep_marks[T][1])
                    for h in range(8):
                        bp, pb = 64 * (h % 2), h // 2
                        par = h % 3
                        mA = self.next_mm()
                        mB = self.next_mm()
                        mA.acquire(t)
                        mB.acquire(t)
                        if nk > 0:
                            self.st.inc(t.matmul(mA.b[:, 0:128 * nk], qT_t[bp:bp + 64, pr, pb, :],
                                                 kT[bp:bp + 64, pb, 128 * (T - nk):128 * T],
                                                 start=True, stop=False, skip_group_check=True))
                            self.st.inc(t.matmul(mA.b[:, 0:128 * nk], self.ident_bf[:],
                                                 maskF[:, 512 - 128 * nk:512],
                                                 start=False, stop=True, skip_group_check=True))
                        self.st.inc(t.matmul(mB.b[:, 0:128], qT_t[bp:bp + 64, pr, pb, :],
                                             kT[bp:bp + 64, pb, 128 * T:128 * T + 128],
                                             start=True, stop=False, skip_group_check=True))
                        self.st.inc(t.matmul(mB.b[:, 0:128], self.ident_bf[:], maskF[:, 512:640],
                                             start=False, stop=True, skip_group_check=True))
                        if len(pendq) >= 3:
                            attn_tail(*pendq.popleft())
                        probs_s[par].acquire(s)
                        self.softmax_psum2(mA.b[:, 0:128 * nk] if nk > 0 else None, mB.b[:, 0:128],
                                           probs_s[par], sm[:, par, :], ncols, probs[:, par, :])
                        mA.release(self.ss)
                        mB.release(self.ss)
                        pendq.append((h, par, self.sv.n))
                    while pendq:
                        attn_tail(*pendq.popleft())
                    for oc in range(2):
                        m = self.next_mm()
                        m.acquire(t)
                        self.sv.wait(t)
                        for fb in range(4):
                            self.st.inc(t.matmul(m.b[:], oT[:, fb, :], wo[:, fb, 512 * oc:512 * oc + 512],
                                                 start=(fb == 0), stop=(fb == 3), skip_group_check=True))
                        self.st.wait(v)
                        self.sv.incw(v, v.tensor_tensor(out=self.X[:, T, 512 * oc:512 * oc + 512],
                                                    in0=m.b[:], in1=self.X[:, T, 512 * oc:512 * oc + 512],
                                                    op=ALU.add))
                        m.release(self.sv)
                    oT_s.release(self.st)
                    qT_s[pr].release(self.st)
            if FFN_ON:
                self.stage_B_ffn(b, li)

    def stage_B_ffn(self, b, li):
        nc, t, v, s, g, y = self.nc, self.t, self.v, self.s, self.g, self.y
        with ExitStack() as c:
            E = c.enter_context
            rows_s = E(nc.sbuf_tensor(self.nt("b_rows_s"), [1, SEQ], BF16))
            rows_nm = E(nc.sbuf_tensor(self.nt("b_rows_nm"), [1, SEQ], BF16))
            rstd_all = E(nc.sbuf_tensor(self.nt("b_rstd"), [128, NT], F32))
            lgw1 = E(nc.sbuf_tensor(self.nt("b_lgw1"), [1, HID2], BF16))
            lb1 = E(nc.sbuf_tensor(self.nt("b_lb1"), [1, HID2], BF16))
            b2r = E(nc.sbuf_tensor(self.nt("b_b2r"), [1, D], BF16))
            lglb = E(nc.sbuf_tensor(self.nt("b_lglb"), [128, 8, 2], BF16))
            histat = E(nc.sbuf_tensor(self.nt("b_histat"), [128, 2, 2], F32))
            psub = E(nc.sbuf_tensor(self.nt("b_psub"), [128, 128], F32))
            p127 = E(nc.sbuf_tensor(self.nt("b_p127"), [128, 128], F32))
            stt3 = E(nc.sbuf_tensor(self.nt("b_stt3"), [128, 2], F32))
            bn = self.bn

            self.st.wait(g)
            self.ss.wait(g)
            self.sv.wait(g)
            self.sd.inc(g.dma_start(out=psub[:], in_=self.c_psub), 16)
            self.sd.inc(g.dma_start(out=p127[:], in_=self.c_p127), 16)
            self.sd.inc(g.dma_start(out=lb1[:], in_=self.wf_b1[0:1, li, :]), 16)
            self.sd.inc(g.dma_start(out=b2r[:], in_=self.wf_b2[0:1, li, :]), 16)
            self.sd.wait(s)
            for db in range(8):
                self.ss.incw(s, s.copy(out=lglb[:, db, 0:1], in_=self.wf_lng_sb[:, li, db:db + 1]))
                self.ss.incw(s, s.copy(out=lglb[:, db, 1:2], in_=self.wf_lnb_sb[:, li, db:db + 1]))

            self.sv.incw(v, v.memset(self.XT[:, :, 0:1], 0.0))
            for T in range(NT):
                for db in range(8):
                    m = self.next_mm()
                    m.acquire(t)
                    self.sv.wait(t)
                    self.st.inc(t.transpose(m.b[:, 0:128], self.X[:, T, 128 * db:128 * db + 128],
                                            self.ident_f[:]))
                    self.st.wait(s)
                    self.ss.incx(s, s.activation(out=self.XT[:, db, 1 + 128 * T:1 + 128 * T + 128],
                                             in_=m.b[:, 0:128], func=AF.Identity,
                                             bias=self.zero1[:], scale=self.wf_lng_sb[:, li, db:db + 1]))
                    m.release(self.ss)

            hist_s, stt3_s = Slot(histat), Slot(stt3)
            for T in range(NT):
                self.ss.wait(v)
                self.sv.incw(v, v.bn_stats(bn[:, 0:6], self.X[:, T, 0:512]))
                self.sv.incw(v, v.bn_stats(bn[:, 6:12], self.X[:, T, 512:1024]))
                self.sv.incw(v, v.bn_aggr(bn[:, 12:14], bn[:, 0:6]))
                hist_s.acquire(v)
                self.sv.incw(v, v.bn_aggr(histat[:, T % 2, :], bn[:, 6:12]))
                self.po.acquire(t)
                self.sv.wait(t)
                self.st.inc(t.matmul(self.po.b[:, 0:2], psub[:], histat[:, T % 2, :],
                                     start=True, stop=(T == 0), skip_group_check=True))
                if T > 0:
                    self.st.inc(t.matmul(self.po.b[:, 0:2], p127[:], histat[:, (T - 1) % 2, :],
                                         start=False, stop=True, skip_group_check=True))
                hist_s.release(self.st)
                self.st.wait(v)
                self.sv.incw(v, v.tensor_copy(out=bn[:, 14:16], in_=self.po.b[:, 0:2]))
                self.po.release(self.sv)
                self.ln_combine(bn)
                stt3_s.acquire(s)
                self.powx(stt3[:, 0:1], bn[:, 17:18], self.eps1[:], +0.5)
                self.ss.wait(v)
                self.sv.incw(v, v.reciprocal(out=rstd_all[:, T:T + 1], in_=stt3[:, 0:1]))
                self.sv.incw(v, v.tensor_scalar(out=stt3[:, 1:2], in0=bn[:, 16:17], scalar1=-1.0, scalar2=None,
                                            op0=ALU.mult, op1=ALU.bypass))
                for which in range(2):
                    m = self.next_mm()
                    m.acquire(t)
                    self.sv.wait(t)
                    self.st.inc(t.transpose(m.b[0:1, 0:128], stt3[:, which:which + 1], self.ident_f[:]))
                    self.st.wait(s)
                    dst = rows_s if which == 0 else rows_nm
                    self.ss.incw(s, s.copy(out=dst[0:1, 128 * T:128 * T + 128], in_=m.b[0:1, 0:128]))
                    m.release(self.ss)
                stt3_s.release(self.st)

            w2 = E(nc.sbuf_tensor(self.nt("b_w2"), [128, 22, D], BF16))
            w1c = [E(nc.sbuf_tensor(self.nt("b_w1c"), [128, 8, 512], BF16)) for _ in range(3)]
            hidc = [E(nc.sbuf_tensor(self.nt("b_hidc"), [128, 512], BF16)) for _ in range(3)]
            hTb = E(nc.sbuf_tensor(self.nt("b_hTb"), [128, 3, 4, 128], BF16))
            gel = E(nc.sbuf_tensor(self.nt("b_gel"), [128, 3, 512], BF16))
            w1_s = [Slot(w) for w in w1c]
            hid_s = [Slot(h) for h in hidc]
            self.hidc_cur = hidc
            self.hid_s_cur = hid_s
            hT_s = [Slot(None), Slot(None), Slot(None)]
            gel_s = [Slot(None), Slot(None), Slot(None)]
            self.sy_.wait(g)
            self.st.wait(g)
            self.sd.inc(g.dma_start(out=w2[:], in_=self.wf_w2[li]), 16)
            w2_ready = self.sd.n
            rows_done = [False] * 6
            CH = [(512 * p_, min(512 * p_ + 512, HID)) for p_ in range(6)]
            for sub in range(8):
                tiles = (2 * sub, 2 * sub + 1)
                dacc = [self.next_mm() for _ in range(4)]
                for dm in dacc:
                    dm.acquire(t)
                from collections import deque
                pendq = deque()
                for p_, (a0, a1) in enumerate(CH):
                    wch = a1 - a0
                    ia, ig = (2 * p_) % 3, (2 * p_ + 1) % 3
                    w1_s[ia].acquire(g)
                    self.sd.inc(g.dma_start(out=w1c[ia][:], in_=self.w1_bf.ap()[li, :, :, 2 * p_, :]), 16)
                    w1_s[ig].acquire(g)
                    self.sd.inc(g.dma_start(out=w1c[ig][:], in_=self.w1_bf.ap()[li, :, :, 2 * p_ + 1, :]), 16)
                    ld_n = self.sd.n
                    if sub == 0 and not rows_done[p_]:
                        t.wait_ge(self.sd.h, ld_n)
                        self.ss.wait(t)
                        for which in range(2):
                            m2 = self.po if which == 0 else self.po2
                            m2.acquire(t)
                            for db in range(8):
                                self.st.inc(t.matmul(m2.b[0:1, 0:wch], lglb[:, db, 0:1],
                                                     w1c[ia if which == 0 else ig][:, db, 0:wch],
                                                     start=(db == 0), stop=(db == 7), skip_group_check=True))
                        self.st.wait(v)
                        self.sv.incw(v, v.tensor_copy(out=lgw1[0:1, a0:a0 + wch], in_=self.po.b[0:1, 0:wch]))
                        self.sv.incw(v, v.tensor_copy(out=lgw1[0:1, HID + a0:HID + a0 + wch],
                                                      in_=self.po2.b[0:1, 0:wch]))
                        self.po.release(self.sv)
                        self.po2.release(self.sv)
                        for which in range(2):
                            m2 = self.po if which == 0 else self.po2
                            m2.acquire(t)
                            for db in range(8):
                                self.st.inc(t.matmul(m2.b[0:1, 0:wch], lglb[:, db, 1:2],
                                                     w1c[ia if which == 0 else ig][:, db, 0:wch],
                                                     start=(db == 0), stop=(db == 7), skip_group_check=True))
                        self.st.wait(v)
                        self.sv.incw(v, v.tensor_tensor(out=lb1[0:1, a0:a0 + wch], in0=self.po.b[0:1, 0:wch],
                                                        in1=lb1[0:1, a0:a0 + wch], op=ALU.add))
                        self.sv.incw(v, v.tensor_tensor(out=lb1[0:1, HID + a0:HID + a0 + wch],
                                                        in0=self.po2.b[0:1, 0:wch],
                                                        in1=lb1[0:1, HID + a0:HID + a0 + wch], op=ALU.add))
                        self.po.release(self.sv)
                        self.po2.release(self.sv)
                        rows_done[p_] = True
                    for ti, T in enumerate(tiles):
                        it = (2 * p_ + ti) % 3

                        def hsl2(db, T=T):
                            off = 1 if db < 4 else 0
                            return self.XT[:, db, off + 128 * T: off + 128 * T + 128]
                        self.po.acquire(t)
                        self.po2.acquire(t)
                        t.wait_ge(self.sd.h, ld_n)
                        self.sv.wait(t)
                        self.ss.wait(t)
                        for which, mmx in ((0, self.po), (1, self.po2)):
                            c0 = a0 if which == 0 else HID + a0
                            for db in range(8):
                                self.st.inc(t.matmul(mmx.b[:, 0:wch], hsl2(db),
                                                     w1c[ia if which == 0 else ig][:, db, 0:wch],
                                                     start=(db == 0), stop=False, skip_group_check=True))
                            self.st.inc(t.matmul(mmx.b[:, 0:wch], rows_nm[0:1, 128 * T:128 * T + 128],
                                                 lgw1[0:1, c0:c0 + wch], start=False, stop=False,
                                                 skip_group_check=True))
                            self.st.inc(t.matmul(mmx.b[:, 0:wch], rows_s[0:1, 128 * T:128 * T + 128],
                                                 lb1[0:1, c0:c0 + wch], start=False, stop=True,
                                                 skip_group_check=True))
                        st_w1 = self.st.n
                        gel_s[it].acquire(s)
                        s.wait_ge(self.st.h, st_w1)
                        self.ss.incx(s, s.activation(out=gel[:, it, 0:wch], in_=self.po2.b[:, 0:wch],
                                                     func=AF.Gelu, bias=self.zero1[:],
                                                     scale=rstd_all[:, T:T + 1]))
                        self.po2.release(self.ss)
                        hid_s[it].acquire(v)
                        self.ss.wait(v)
                        self.sv.incx(v, v.scalar_tensor_tensor(out=hidc[it][:, 0:wch], in0=self.po.b[:, 0:wch],
                                                               scalar=rstd_all[:, T:T + 1], in1=gel[:, it, 0:wch],
                                                               op0=ALU.mult, op1=ALU.mult))
                        self.po.release(self.sv)
                        gel_s[it].release(self.sv)
                        sv_hid = self.sv.n
                        if len(pendq) > 1:
                            self.emit_w2_part(*pendq.popleft(), dacc, w2, hTb, hT_s)
                        pendq.append((it, p_, ti, wch, sv_hid))
                    w1_s[ia].release(self.st)
                    w1_s[ig].release(self.st)
                while pendq:
                    self.emit_w2_part(*pendq.popleft(), dacc, w2, hTb, hT_s)
                for ti, T in enumerate(tiles):
                    for oc in range(2):
                        dm = dacc[2 * ti + oc]
                        self.st.inc(t.matmul(dm.b[:], self.ones_bf[:], b2r[0:1, 512 * oc:512 * oc + 512],
                                             start=False, stop=True, skip_group_check=True))
                        self.st.wait(v)
                        self.sv.incw(v, v.tensor_tensor(out=self.X[:, T, 512 * oc:512 * oc + 512],
                                                        in0=dm.b[:], in1=self.X[:, T, 512 * oc:512 * oc + 512],
                                                        op=ALU.add))
                        dm.release(self.sv)

    def stage_C(self, b):
        nc, t, v, s, g, y = self.nc, self.t, self.v, self.s, self.g, self.y
        with ExitStack() as c:
            E = c.enter_context
            x1t = E(nc.sbuf_tensor(self.nt("c_x1"), [128, 4, D], F32))
            self.bn = E(nc.sbuf_tensor(self.nt("c_bn"), [128, 24], F32))
            x1_s = Slot(x1t)
            self.sv.wait(y)
            self.ss.wait(y)
            self.st.wait(y)
            for tc in range(4):
                x1_s.acquire(y)
                self.sy_.inc(y.dma_start(out=x1t[:], in_=self.x1_spill.ap()[:, 4 * tc:4 * tc + 4, :]), 16)
                self.sy_.wait(v)
                for tt in range(4):
                    T = 4 * tc + tt
                    self.ss.wait(v)
                    self.sv.incw(v, v.tensor_tensor(out=self.X[:, T, :], in0=self.X[:, T, :],
                                                in1=x1t[:, tt, :], op=ALU.add))
                    self.ln_apply_tile(T)
                x1_s.release(self.sv)

    def stage_D(self, b):
        nc, t, v, s, g, y = self.nc, self.t, self.v, self.s, self.g, self.y
        with ExitStack() as c:
            E = c.enter_context
            self.bn = E(nc.sbuf_tensor(self.nt("d_bn"), [128, 24], F32))
            self.xbf_s = Slot(E(nc.sbuf_tensor(self.nt("d_xbf"), [128, D], BF16)))
            qT_pad = E(nc.sbuf_tensor(self.nt("d_qT"), [128, 4, SEQ], BF16))
            kT_pad = E(nc.sbuf_tensor(self.nt("d_kT"), [128, 4, SEQ], BF16))
            v65 = E(nc.sbuf_tensor(self.nt("d_v65"), [128, NT, 16, 65], BF16))
            zz = E(nc.sbuf_tensor(self.nt("d_zz"), [128, 2], F32))
            eu = E(nc.sbuf_tensor(self.nt("d_eu"), [128, D], F32))
            eu_s = Slot(eu)
            qTp_s, kTp_s, v65_s = Slot(qT_pad), Slot(kT_pad), Slot(v65)

            self.st.wait(g)
            self.ss.wait(g)
            self.sv.wait(g)
            with ExitStack() as dproj:
                E1 = dproj.enter_context
                gb = E1(nc.sbuf_tensor(self.nt("d_gb"), [128, 4, 16], F32))
                gbrow = E1(nc.sbuf_tensor(self.nt("d_gbrow"), [1, 128], F32))
                qf = E1(nc.sbuf_tensor(self.nt("d_qf"), [128, 256], F32))
                q2 = E1(nc.sbuf_tensor(self.nt("d_q2"), [128, 256], F32))
                fmst = E1(nc.sbuf_tensor(self.nt("d_fmst"), [128, 4, 16], F32))
                qpad = E1(nc.sbuf_tensor(self.nt("d_qpad"), [128, 16, 32], BF16))
                wq_sb = E1(nc.sbuf_tensor(self.nt("d_wq"), [128, 8, 256], BF16))
                wk_sb = E1(nc.sbuf_tensor(self.nt("d_wk"), [128, 8, 256], BF16))
                wv_sb = E1(nc.sbuf_tensor(self.nt("d_wv"), [128, 8, D], BF16))
                self.sd.inc(g.dma_start(out=wq_sb[:], in_=self.w1_qT), 16)
                self.sd.inc(g.dma_start(out=wk_sb[:], in_=self.w1_kT), 16)
                self.sd.inc(g.dma_start(out=wv_sb[:], in_=self.w1_vT), 16)
                self.sd.inc(g.dma_start(out=gbrow[0:1, 0:16], in_=self.w1_g), 16)
                self.sd.inc(g.dma_start(out=gbrow[0:1, 16:32], in_=self.w1_b), 16)
                m = self.next_mm()
                m.acquire(t)
                t.wait_ge(self.sd.h, self.sd.n)
                self.st.inc(t.matmul(m.b[:, 0:32], self.ones_f[:], gbrow[0:1, 0:32], start=True, stop=True,
                                     skip_group_check=True))
                self.st.wait(v)
                self.sv.incw(v, v.tensor_scalar(out=gb[:, 0, :], in0=m.b[:, 0:16], scalar1=0.25, scalar2=None,
                                            op0=ALU.mult, op1=ALU.bypass))
                self.sv.incw(v, v.tensor_scalar(out=gb[:, 1, :], in0=m.b[:, 16:32], scalar1=0.25, scalar2=None,
                                            op0=ALU.mult, op1=ALU.bypass))
                self.sv.incw(v, v.tensor_copy(out=gb[:, 2, :], in_=m.b[:, 0:16]))
                self.sv.incw(v, v.tensor_copy(out=gb[:, 3, :], in_=m.b[:, 16:32]))
                m.release(self.sv)
                self.sv.incw(v, v.memset(qpad[:], 0.0))
                self.sv.incw(v, v.memset(v65[:], 1.0))

                qf_s, q2_s, qp_s = Slot(qf), Slot(q2), Slot(qpad)

                for T in range(NT):
                    self.lazy_xt(T)
                    for which in range(2):
                        wsb = wq_sb if which == 0 else wk_sb
                        m = self.next_mm()
                        m.acquire(t)
                        self.sv.wait(t)
                        for db in range(8):
                            self.st.inc(t.matmul(m.b[:, 0:256], self.XT[:, db, 1 + 128 * T:1 + 128 * T + 128],
                                                 wsb[:, db, :], start=(db == 0), stop=(db == 7),
                                                 skip_group_check=True))
                        qf_s.acquire(v)
                        self.st.wait(v)
                        self.sv.incw(v, v.tensor_copy(out=qf[:], in_=m.b[:, 0:256]))
                        m.release(self.sv)
                        q3 = qf[:].rearrange("p (h f) -> p h f", h=16)
                        q2_s.acquire(s)
                        self.sv.wait(s)
                        self.ss.incw(s, s.square(out=q2[:], in_=qf[:]))
                        q23 = q2[:].rearrange("p (h f) -> p h f", h=16)
                        self.ss.wait(v)
                        self.sv.incw(v, v.reduce_sum(out=fmst[:, 0, :], in_=q3, axis=AX.X))
                        self.sv.incw(v, v.reduce_sum(out=fmst[:, 1, :], in_=q23, axis=AX.X))
                        q2_s.release(self.sv)
                        self.sv.incw(v, v.tensor_scalar(out=fmst[:, 0, :], in0=fmst[:, 0, :], scalar1=1.0 / 16,
                                                    scalar2=None, op0=ALU.mult, op1=ALU.bypass))
                        self.sv.incw(v, v.tensor_scalar(out=fmst[:, 1, :], in0=fmst[:, 1, :], scalar1=1.0 / 16,
                                                    scalar2=None, op0=ALU.mult, op1=ALU.bypass))
                        self.sv.incw(v, v.tensor_tensor(out=fmst[:, 2, :], in0=fmst[:, 0, :], in1=fmst[:, 0, :],
                                                    op=ALU.mult))
                        self.sv.incw(v, v.tensor_tensor(out=fmst[:, 2, :], in0=fmst[:, 1, :], in1=fmst[:, 2, :],
                                                    op=ALU.subtract))
                        self.powx(fmst[:, 3, :], fmst[:, 2, :], self.eps1[:], -0.5)
                        mu_b = bcast_last(fmst[:, 0, :].rearrange("p (h a) -> p h a", a=1), 16)
                        rs_b = bcast_last(fmst[:, 3, :].rearrange("p (h a) -> p h a", a=1), 16)
                        g_b = bcast_mid(gb[:, 0 if which == 0 else 2, :], 16)
                        b_b = bcast_mid(gb[:, 1 if which == 0 else 3, :], 16)
                        self.ss.wait(v)
                        self.sv.incw(v, v.tensor_tensor(out=q3, in0=q3, in1=mu_b, op=ALU.subtract))
                        self.sv.incw(v, v.tensor_tensor(out=q3, in0=q3, in1=rs_b, op=ALU.mult))
                        self.sv.incw(v, v.tensor_tensor(out=q3, in0=q3, in1=g_b, op=ALU.mult))
                        qp_s.acquire(v)
                        self.sv.incw(v, v.tensor_tensor(out=qpad[:, :, 0:16], in0=q3, in1=b_b, op=ALU.add))
                        qf_s.release(self.sv)
                        qpf = qpad[:].rearrange("p h f -> p (h f)")
                        dstT = qT_pad if which == 0 else kT_pad
                        (qTp_s if which == 0 else kTp_s).acquire(v)
                        (qTp_s if which == 0 else kTp_s).acquire(s)
                        self.sv.wait(t)
                        self.transpose_to([qpf[:, 128 * cc:128 * cc + 128] for cc in range(4)],
                                          [dstT[:, cc, 128 * T:128 * T + 128] for cc in range(4)],
                                          self.ident_bf[:])
                        qp_s.release(self.st)
                    v65_s.acquire(v)
                    for vc in range(2):
                        m = self.next_mm()
                        m.acquire(t)
                        self.sv.wait(t)
                        for db in range(8):
                            self.st.inc(t.matmul(m.b[:], self.XT[:, db, 1 + 128 * T:1 + 128 * T + 128],
                                                 wv_sb[:, db, 512 * vc:512 * vc + 512],
                                                 start=(db == 0), stop=(db == 7), skip_group_check=True))
                        self.st.wait(v)
                        dst = v65[:, T, 8 * vc:8 * vc + 8, 0:64]
                        self.sv.incw(v, v.tensor_copy(out=dst, in_=m.b[:].rearrange("p (h f) -> p h f", h=8)))
                        m.release(self.sv)

            s2T = E(nc.sbuf_tensor(self.nt("d_s2T"), [128, 3, 512], BF16))
            triu = E(nc.sbuf_tensor(self.nt("d_triu"), [128, 128], BF16))
            onorm = E(nc.sbuf_tensor(self.nt("d_on"), [128, 4, 16, 64], BF16))
            oev = E(nc.sbuf_tensor(self.nt("d_oev"), [128, 4, 65], F32))
            rz4 = E(nc.sbuf_tensor(self.nt("d_rz4"), [128, 4, 1], F32))
            oTd = E(nc.sbuf_tensor(self.nt("d_oT"), [128, 8, 128], BF16))
            wo_sb = E(nc.sbuf_tensor(self.nt("d_wo"), [128, 8, D], BF16))
            s2_s = [Slot(None), Slot(None), Slot(None)]
            on_s, oT_s, oev_s = Slot(onorm), Slot(oTd), Slot(oev)
            self.st.wait(g)
            self.sd.inc(g.dma_start(out=triu[:], in_=self.c_triu), 16)
            self.sd.inc(g.dma_start(out=wo_sb[:], in_=self.w1_oT), 16)
            self.sd.wait(t)
            for qc in range(4):
                on_s.acquire(v)
                self.sv.wait(t)
                self.ss.wait(t)
                for h in range(16):
                    bp, pb = 32 * (h % 4), h // 4
                    nkt = 4 * qc + 4
                    opo = [self.mm[j] for j in range(4)]
                    for j in range(4):
                        opo[j].acquire(t)
                    def omm_d(kt, was_diag):
                        sb_i = kt % 3
                        if was_diag:
                            self.sv.wait(t)
                        else:
                            self.ss.wait(t)
                        for j in range(4):
                            qtile = 4 * qc + j
                            if kt > qtile:
                                continue
                            self.st.inc(t.matmul(opo[j].b[:, 0:65], s2T[:, sb_i, 128 * j:128 * j + 128],
                                                 v65[:, kt, h, :], start=(kt == 0), stop=(kt == qtile),
                                                 skip_group_check=True))
                        s2_s[sb_i].release(self.st)

                    from collections import deque
                    pq = deque()
                    for kt in range(nkt):
                        sps = self.po if kt % 2 == 0 else self.po2
                        sps.acquire(t)
                        self.st.inc(t.matmul(sps.b[:], kT_pad[bp:bp + 32, pb, 128 * kt:128 * kt + 128],
                                             qT_pad[bp:bp + 32, pb, 512 * qc:512 * qc + 512],
                                             start=True, stop=True, tile_position=(bp, 0),
                                             skip_group_check=True))
                        sb_i = kt % 3
                        s2_s[sb_i].acquire(s)
                        self.st.wait(s)
                        self.ss.incx(s, s.square(out=s2T[:, sb_i, :], in_=sps.b[:]))
                        sps.release(self.ss)
                        jd = kt - 4 * qc
                        if 0 <= jd < 4:
                            self.ss.wait(v)
                            self.sv.incx(v, v.tensor_tensor(out=s2T[:, sb_i, 128 * jd:128 * jd + 128],
                                                            in0=s2T[:, sb_i, 128 * jd:128 * jd + 128],
                                                            in1=triu[:], op=ALU.mult))
                        pq.append((kt, 0 <= jd < 4))
                        if len(pq) > 2:
                            omm_d(*pq.popleft())
                    while pq:
                        omm_d(*pq.popleft())
                    oev_s.acquire(v)
                    self.st.wait(v)
                    for j in range(4):
                        self.sv.incw(v, v.tensor_copy(out=oev[:, j, :], in_=opo[j].b[:, 0:65]))
                        opo[j].release(self.sv)
                    self.sv.incw(v, v.tensor_scalar(out=rz4[:], in0=oev[:, :, 64:65], scalar1=1e-5,
                                                    scalar2=None, op0=ALU.add, op1=ALU.bypass))
                    self.sv.incw(v, v.reciprocal(out=rz4[:], in_=rz4[:]))
                    self.sv.incw(v, v.tensor_tensor(out=onorm[:, :, h, :], in0=oev[:, :, 0:64],
                                                    in1=bcast_last(rz4[:], 64), op=ALU.mult))
                    oev_s.release(self.sv)
                for j in range(4):
                    T = 4 * qc + j
                    onf = onorm[:, j].rearrange("p h f -> p (h f)")
                    oT_s.acquire(v)
                    oT_s.acquire(s)
                    self.sv.wait(t)
                    self.transpose_to([onf[:, 128 * cc:128 * cc + 128] for cc in range(8)],
                                      [oTd[:, cc, :] for cc in range(8)], self.ident_bf[:])
                    for oc in range(2):
                        m = self.next_mm()
                        m.acquire(t)
                        self.sv.wait(t)
                        self.ss.wait(t)
                        for fb in range(8):
                            self.st.inc(t.matmul(m.b[:], oTd[:, fb, :], wo_sb[:, fb, 512 * oc:512 * oc + 512],
                                                 start=(fb == 0), stop=(fb == 7), skip_group_check=True))
                        self.st.wait(v)
                        self.sv.incw(v, v.tensor_tensor(out=self.X[:, T, 512 * oc:512 * oc + 512],
                                                        in0=m.b[:], in1=self.X[:, T, 512 * oc:512 * oc + 512],
                                                        op=ALU.add))
                        m.release(self.sv)
                    oT_s.release(self.st)
                    self.ln_apply_tile(T)
                on_s.release(self.st)

    def stage_D_eprep(self, b):
        nc, t, v, s, g, y = self.nc, self.t, self.v, self.s, self.g, self.y
        with ExitStack() as c:
            E = c.enter_context
            self.bn = E(nc.sbuf_tensor(self.nt("de_bn"), [128, 24], F32))
            gbc = E(nc.sbuf_tensor(self.nt("de_gbc"), [128, 2, D], F32))
            eu = E(nc.sbuf_tensor(self.nt("de_eu"), [128, D], F32))
            zz = E(nc.sbuf_tensor(self.nt("de_zz"), [128, 2], F32))
            self.sv.wait(g)
            self.ss.wait(g)
            self.st.wait(g)
            self.build_gbc(gbc)
            eu_s = Slot(eu)
            bn = self.bn
            for T in range(NT):
                self.ss.wait(v)
                self.sv.incw(v, v.bn_stats(bn[:, 0:6], self.X[:, T, 0:512]))
                self.sv.incw(v, v.bn_stats(bn[:, 6:12], self.X[:, T, 512:1024]))
                self.sv.incw(v, v.bn_aggr(bn[:, 12:14], bn[:, 0:6]))
                self.sv.incw(v, v.bn_aggr(bn[:, 14:16], bn[:, 6:12]))
                self.ln_combine(bn)
                rstd3 = self.stats_e[:, b, T, 0:1]
                nmr3 = self.stats_e[:, b, T, 1:2]
                self.powx(rstd3, bn[:, 17:18], self.eps1[:], -0.5)
                self.ss.wait(v)
                self.sv.incw(v, v.tensor_tensor(out=nmr3, in0=bn[:, 16:17], in1=rstd3, op=ALU.mult))
                self.sv.incw(v, v.tensor_scalar(out=nmr3, in0=nmr3, scalar1=-1.0, scalar2=None,
                                            op0=ALU.mult, op1=ALU.bypass))
                eu_s.acquire(s)
                self.sv.wait(s)
                self.ss.incw(s, s.activation(out=eu[:], in_=self.X[:, T, :], func=AF.Identity,
                                         bias=nmr3, scale=rstd3))
                self.ss.wait(v)
                self.sv.incw(v, v.tensor_tensor(out=eu[:], in0=eu[:], in1=gbc[:, 0, :], op=ALU.mult))
                self.sv.incw(v, v.tensor_tensor(out=eu[:], in0=eu[:], in1=gbc[:, 1, :], op=ALU.add))
                self.sv.incw(v, v.reduce_max(out=zz[:, 0:1], in_=eu[:], axis=AX.X, apply_absolute_value=True))
                eu_s.release(self.sv)
                self.sv.incw(v, v.tensor_tensor(out=self.gmax[:], in0=self.gmax[:], in1=zz[:, 0:1], op=ALU.max))

    def stage_E_all(self):
        nc, t, v, s, g, y = self.nc, self.t, self.v, self.s, self.g, self.y
        with ExitStack() as c:
            E = c.enter_context
            self.bn = E(nc.sbuf_tensor(self.nt("e_bn"), [128, 24], F32))
            wtf = E(nc.sbuf_tensor(self.nt("e_wtf"), [128, 8, D], F32))
            absc = E(nc.sbuf_tensor(self.nt("e_absc"), [128, D], F32))
            maskc = E(nc.sbuf_tensor(self.nt("e_maskc"), [128, D], F32))
            sgn = E(nc.sbuf_tensor(self.nt("e_sgn"), [128, D], F32))
            sums = E(nc.sbuf_tensor(self.nt("e_sums"), [128, 8], F32))
            sm2 = E(nc.sbuf_tensor(self.nt("e_sm2"), [128, 2], F32))
            smt = E(nc.sbuf_tensor(self.nt("e_smt"), [2, 1], F32))
            al = E(nc.sbuf_tensor(self.nt("e_al"), [1, 4], F32))
            alb = E(nc.sbuf_tensor(self.nt("e_alb"), [128, 2], F32))
            u = E(nc.sbuf_tensor(self.nt("e_u"), [128, D], F32))
            xq = E(nc.sbuf_tensor(self.nt("e_xq"), [128, D], BF16))
            xqT = E(nc.sbuf_tensor(self.nt("e_xqT"), [128, 8, 128], BF16))
            arr = E(nc.sbuf_tensor(self.nt("e_arr"), [1, 128], F32))
            tmpc = E(nc.sbuf_tensor(self.nt("e_tmpc"), [128, 1], F32))
            self.wq_ns = E(nc.sbuf_tensor(self.nt("wq_ns"), [128, 8, D], BF16))

            self.st.wait(g)
            self.ss.wait(g)
            self.sv.wait(g)
            self.gbc = E(nc.sbuf_tensor(self.nt("e_gbc"), [128, 2, D], F32))
            self.build_gbc(self.gbc)
            self.sd.inc(g.dma_start(out=wtf[:], in_=self.we_wT), 16)
            self.sd.wait(s)
            self.sd.wait(v)

            for db in range(8):
                self.sv.wait(s)
                self.ss.incw(s, s.activation(out=absc[:], in_=wtf[:, db, :], func=AF.Abs,
                                         bias=self.zero1[:], scale=1.0))
                self.ss.wait(v)
                self.sv.incw(v, v.reduce_sum(out=sums[:, db:db + 1], in_=absc[:], axis=AX.X))
            self.sv.incw(v, v.reduce_sum(out=sm2[:, 0:1], in_=sums[:], axis=AX.X))
            m = self.next_mm()
            m.acquire(t)
            self.sv.wait(t)
            self.st.inc(t.transpose(m.b[0:1, 0:128], sm2[:, 0:1], self.ident_f[:]))
            self.st.wait(v)
            self.sv.incw(v, v.reduce_sum(out=al[0:1, 0:1], in_=m.b[0:1, 0:128], axis=AX.X))
            m.release(self.sv)
            self.sv.incw(v, v.tensor_scalar(out=al[0:1, 0:1], in0=al[0:1, 0:1],
                                        scalar1=0.7 / (1024.0 * 1024.0), scalar2=None,
                                        op0=ALU.mult, op1=ALU.bypass))
            m = self.next_mm()
            m.acquire(t)
            self.sv.wait(t)
            self.st.inc(t.matmul(m.b[:, 0:1], self.ones_f[:], al[0:1, 0:1], start=True, stop=True,
                                 skip_group_check=True))
            self.st.wait(v)
            self.sv.incw(v, v.tensor_copy(out=alb[:, 0:1], in_=m.b[:, 0:1]))
            m.release(self.sv)
            self.sv.incw(v, v.memset(sm2[:], 0.0))
            for db in range(8):
                self.sv.wait(s)
                self.ss.incw(s, s.activation(out=absc[:], in_=wtf[:, db, :], func=AF.Abs,
                                         bias=self.zero1[:], scale=1.0))
                self.ss.wait(v)
                self.sv.incw(v, v.tensor_scalar(out=maskc[:], in0=absc[:], scalar1=alb[:, 0:1], scalar2=None,
                                            op0=ALU.is_gt, op1=ALU.bypass))
                self.sv.incw(v, v.scalar_tensor_tensor(out=absc[:], in0=absc[:], scalar=1.0, in1=maskc[:],
                                                   op0=ALU.mult, op1=ALU.mult, accum_out=tmpc[:]))
                self.sv.incw(v, v.tensor_tensor(out=sm2[:, 0:1], in0=sm2[:, 0:1], in1=tmpc[:], op=ALU.add))
                self.sv.incw(v, v.tensor_scalar(out=maskc[:], in0=maskc[:], scalar1=1.0, scalar2=0.0,
                                            op0=ALU.mult, op1=ALU.add, accum_out=tmpc[:]))
                self.sv.incw(v, v.tensor_tensor(out=sm2[:, 1:2], in0=sm2[:, 1:2], in1=tmpc[:], op=ALU.add))
            m = self.next_mm()
            m.acquire(t)
            self.sv.wait(t)
            self.st.inc(t.transpose(m.b[0:2, 0:128], sm2[:], self.ident_f[:]))
            self.st.wait(v)
            self.sv.incw(v, v.reduce_sum(out=smt[0:2, 0:1], in_=m.b[0:2, 0:128], axis=AX.X))
            m.release(self.sv)
            m = self.next_mm()
            m.acquire(t)
            self.sv.wait(t)
            self.st.inc(t.transpose(m.b[0:1, 0:2], smt[0:2, 0:1], self.ident_f[0:2, 0:2]))
            self.st.wait(v)
            self.sv.incw(v, v.tensor_scalar(out=al[0:1, 1:2], in0=m.b[0:1, 1:2], scalar1=1.0,
                                        scalar2=None, op0=ALU.max, op1=ALU.bypass))
            self.sv.incw(v, v.reciprocal(out=al[0:1, 1:2], in_=al[0:1, 1:2]))
            self.sv.incw(v, v.tensor_tensor(out=al[0:1, 2:3], in0=m.b[0:1, 0:1], in1=al[0:1, 1:2],
                                        op=ALU.mult))
            m.release(self.sv)
            m = self.next_mm()
            m.acquire(t)
            self.sv.wait(t)
            self.st.inc(t.matmul(m.b[:, 0:1], self.ones_f[:], al[0:1, 2:3], start=True, stop=True,
                                 skip_group_check=True))
            self.st.wait(v)
            self.sv.incw(v, v.tensor_copy(out=alb[:, 1:2], in_=m.b[:, 0:1]))
            m.release(self.sv)
            for db in range(8):
                self.sv.wait(s)
                self.ss.incw(s, s.activation(out=absc[:], in_=wtf[:, db, :], func=AF.Abs,
                                         bias=self.zero1[:], scale=1.0))
                self.ss.incw(s, s.sign(out=sgn[:], in_=wtf[:, db, :]))
                self.ss.wait(v)
                self.sv.incw(v, v.tensor_scalar(out=maskc[:], in0=absc[:], scalar1=alb[:, 0:1], scalar2=None,
                                            op0=ALU.is_gt, op1=ALU.bypass))
                self.sv.incw(v, v.tensor_tensor(out=sgn[:], in0=sgn[:], in1=maskc[:], op=ALU.mult))
                self.sv.incw(v, v.tensor_scalar(out=self.wq_ns[:, db, :], in0=sgn[:], scalar1=alb[:, 1:2],
                                            scalar2=None, op0=ALU.mult, op1=ALU.bypass))

            m = self.next_mm()
            m.acquire(t)
            self.sv.wait(t)
            self.st.inc(t.transpose(m.b[0:1, 0:128], self.gmax[:], self.ident_f[:]))
            self.st.wait(v)
            self.sv.incw(v, v.memset(arr[:], 0.0))
            self.sv.incw(v, v.reduce_max(out=arr[0:1, 0:1], in_=m.b[0:1, 0:128], axis=AX.X))
            m.release(self.sv)
            self.sv.wait(g)
            self.sd.inc(g.dma_start(out=self.ar_in.ap(), in_=arr[:]), 16)
            self.sd.wait(g)
            self.scc.inc(g.collective_compute("AllReduce", ALU.max, replica_groups=[list(range(self.cc_world))],
                                              ins=[self.ar_in.ap().opt()], outs=[self.ar_out.ap().opt()]))
            self.scc.wait(g)
            self.sd.inc(g.dma_start(out=arr[0:1, 0:1], in_=self.ar_out.ap()[0:1, 0:1]), 16)
            self.sd.wait(v)
            self.sv.incw(v, v.tensor_scalar(out=al[0:1, 3:4], in0=arr[0:1, 0:1], scalar1=1e-8,
                                        scalar2=1.0 / 127.0, op0=ALU.max, op1=ALU.mult))
            m = self.next_mm()
            m.acquire(t)
            self.sv.wait(t)
            self.st.inc(t.matmul(m.b[:, 0:1], self.ones_f[:], al[0:1, 3:4], start=True, stop=True,
                                 skip_group_check=True))
            self.st.wait(v)
            self.sv.incw(v, v.tensor_copy(out=self.scale_b[:], in_=m.b[:, 0:1]))
            m.release(self.sv)
            self.sv.incw(v, v.reciprocal(out=self.iscale_b[:], in_=self.scale_b[:]))

            u_s, xq_s, xqT_s = Slot(u), Slot(xq), Slot(xqT)
            for b in range(NB - 1, -1, -1):
                if b != NB - 1:
                    self.sv.wait(y)
                    self.ss.wait(y)
                    self.st.wait(y)
                    for tc in range(4):
                        self.sy_.inc(y.dma_start(out=self.X[:, 4 * tc:4 * tc + 4, :],
                                                 in_=self.x3_spill.ap()[:, 4 * tc:4 * tc + 4, :]), 16)
                    self.sy_.wait(v)
                    self.sy_.wait(s)
                for T in range(NT):
                    st_ap = self.stats_e[:, b, T, :]
                    u_s.acquire(s)
                    self.sv.wait(s)
                    self.ss.incw(s, s.activation(out=u[:], in_=self.X[:, T, :], func=AF.Identity,
                                             bias=st_ap[:, 1:2], scale=st_ap[:, 0:1]))
                    self.ss.wait(v)
                    self.sv.incw(v, v.tensor_tensor(out=u[:], in0=u[:], in1=self.gbc[:, 0, :], op=ALU.mult))
                    self.sv.incw(v, v.tensor_tensor(out=u[:], in0=u[:], in1=self.gbc[:, 1, :], op=ALU.add))
                    self.sv.incw(v, v.tensor_scalar(out=u[:], in0=u[:], scalar1=self.iscale_b[:, 0:1],
                                                scalar2=None, op0=ALU.mult, op1=ALU.bypass))
                    self.sv.incw(v, v.tensor_scalar(out=u[:], in0=u[:], scalar1=127.0, scalar2=-127.0,
                                                op0=ALU.min, op1=ALU.max))
                    xq_s.acquire(s)
                    self.sv.wait(s)
                    self.ss.incw(s, s.activation(out=u[:], in_=u[:], func=AF.Identity,
                                             bias=self.magic[:], scale=1.0))
                    self.ss.incw(s, s.activation(out=xq[:], in_=u[:], func=AF.Identity,
                                             bias=self.nmagic[:], scale=1.0))
                    u_s.release(self.ss)
                    xqT_s.acquire(v)
                    xqT_s.acquire(s)
                    self.ss.wait(t)
                    self.transpose_to([xq[:, 128 * db:128 * db + 128] for db in range(8)],
                                      [xqT[:, db, :] for db in range(8)], self.ident_bf[:])
                    xq_s.release(self.st)
                    for oc in range(2):
                        m = self.next_mm()
                        m.acquire(t)
                        self.sv.wait(t)
                        self.ss.wait(t)
                        for db in range(8):
                            self.st.inc(t.matmul(m.b[:], xqT[:, db, :],
                                                 self.wq_ns[:, db, 512 * oc:512 * oc + 512],
                                                 start=(db == 0), stop=(db == 7), skip_group_check=True))
                        self.st.wait(v)
                        self.sv.incw(v, v.scalar_tensor_tensor(out=self.X[:, T, 512 * oc:512 * oc + 512],
                                                           in0=m.b[:], scalar=self.scale_b[:, 0:1],
                                                           in1=self.X[:, T, 512 * oc:512 * oc + 512],
                                                           op0=ALU.mult, op1=ALU.add))
                        m.release(self.sv)
                    xqT_s.release(self.st)
                    self.ln_apply_tile(T)
                    self.ss.wait(y)
                    self.sy_.inc(y.dma_start(out=self.out[b, :, T, :], in_=self.X[:, T, :]), 16)


def _host_prep(inputs):
    f32 = np.float32
    x = np.ascontiguousarray(inputs["L0_attn_x"], dtype=f32)

    def tokmaj(xb):
        return np.ascontiguousarray(xb.reshape(NT, 128, D).transpose(1, 0, 2))

    def feamaj(xb):
        return np.ascontiguousarray(xb.T.reshape(8, 128, SEQ).transpose(1, 0, 2))

    sh = {}
    sh["wa_qkvT"] = np.ascontiguousarray(
        np.asarray(inputs["L0_attn_Wqkv"], f32).T.reshape(8, 128, 3072).transpose(1, 0, 2))
    sh["wa_outT"] = np.ascontiguousarray(
        np.asarray(inputs["L0_attn_Wout"], f32).T.reshape(8, 128, D).transpose(1, 0, 2))
    bqkv = np.asarray(inputs["L0_attn_bqkv"], f32)
    sh["ba_qkv"] = np.ascontiguousarray(bqkv[:2048].reshape(16, 128).T)
    sh["ba_bv"] = np.ascontiguousarray(bqkv[2048:].reshape(8, 128).T)
    sh["ba_bout"] = np.asarray(inputs["L0_attn_bout"], f32)[None, :]
    sh["wf_qkv"] = np.ascontiguousarray(
        np.asarray(inputs["L0_ffn_wqkv"], f32).reshape(2, 8, 128, 1536).transpose(0, 2, 1, 3))
    sh["wf_wo"] = np.ascontiguousarray(
        np.asarray(inputs["L0_ffn_wo"], f32).reshape(2, 4, 128, D).transpose(0, 2, 1, 3))
    sh["wf_w1"] = np.ascontiguousarray(
        np.asarray(inputs["L0_ffn_w1"], f32).reshape(2, 8, 128, HID2).transpose(0, 2, 1, 3))
    w2p = np.zeros((2, 22 * 128, D), dtype=f32)
    w2p[:, :HID, :] = np.asarray(inputs["L0_ffn_w2"], f32)
    sh["wf_w2"] = np.ascontiguousarray(w2p.reshape(2, 22, 128, D).transpose(0, 2, 1, 3))
    sh["wf_qs"] = np.ascontiguousarray(np.asarray(inputs["L0_ffn_q_scale"], f32)[None])
    sh["wf_ks"] = np.ascontiguousarray(np.asarray(inputs["L0_ffn_k_scale"], f32)[None])
    sh["wf_lng"] = np.ascontiguousarray(
        np.asarray(inputs["L0_ffn_ln_g"], f32).reshape(2, 8, 128).transpose(2, 0, 1))
    sh["wf_lnb"] = np.ascontiguousarray(
        np.asarray(inputs["L0_ffn_ln_b"], f32).reshape(2, 8, 128).transpose(2, 0, 1))
    sh["wf_b1"] = np.ascontiguousarray(np.asarray(inputs["L0_ffn_b1"], f32)[None])
    sh["wf_b2"] = np.ascontiguousarray(np.asarray(inputs["L0_ffn_b2"], f32)[None])
    sh["w1_qT"] = np.ascontiguousarray(
        np.asarray(inputs["L1_attn_Wq"], f32).T.reshape(8, 128, 256).transpose(1, 0, 2))
    sh["w1_kT"] = np.ascontiguousarray(
        np.asarray(inputs["L1_attn_Wk"], f32).T.reshape(8, 128, 256).transpose(1, 0, 2))
    sh["w1_vT"] = np.ascontiguousarray(
        np.asarray(inputs["L1_attn_Wv"], f32).T.reshape(8, 128, D).transpose(1, 0, 2))
    sh["w1_oT"] = np.ascontiguousarray(
        np.asarray(inputs["L1_attn_Wo"], f32).T.reshape(8, 128, D).transpose(1, 0, 2))
    sh["w1_g"] = np.asarray(inputs["L1_attn_gamma"], f32)[None]
    sh["w1_b"] = np.asarray(inputs["L1_attn_beta"], f32)[None]
    sh["we_wT"] = np.ascontiguousarray(
        np.asarray(inputs["L1_ffn_weight"], f32).T.reshape(8, 128, D).transpose(1, 0, 2))
    sh["we_g"] = np.asarray(inputs["L1_ffn_gamma"], f32)[None]
    sh["we_b"] = np.asarray(inputs["L1_ffn_beta"], f32)[None]

    sh["c_ident_bf"] = np.eye(128, dtype=BF)
    sh["c_ident_f"] = np.eye(128, dtype=f32)
    sh["c_ones_bf"] = np.ones((1, 128), dtype=BF)
    sh["c_ones_f"] = np.ones((1, 128), dtype=f32)
    p = np.arange(128)[:, None]
    c256 = np.arange(256)[None, :]
    sh["c_maskA"] = np.where((c256 > p + 64) & (c256 <= p + 128), 0.0, -1e30).astype(BF)
    c128 = np.arange(128)[None, :]
    sh["c_maskA0"] = np.where((c128 <= p) & (c128 > p - 64), 0.0, -1e30).astype(BF)
    c640 = np.arange(640)[None, :]
    sh["c_maskF"] = np.where((c640 >= p) & (c640 <= p + 512), 0.0, -1e30).astype(BF)
    kk = np.arange(128)[:, None]
    qq = np.arange(128)[None, :]
    sh["c_triu"] = (kk <= qq).astype(BF)
    psub = np.zeros((128, 128), dtype=f32)
    psub[np.arange(127), np.arange(1, 128)] = 1.0
    sh["c_psub"] = psub
    e0 = np.zeros((1, 128), dtype=f32)
    e0[0, 0] = 1.0
    sh["c_e0"] = e0
    p127 = np.zeros((128, 128), dtype=f32)
    p127[127, 0] = 1.0
    sh["c_p127"] = p127
    DH, WIN = 64, 512
    half = np.arange(0, DH, 2, dtype=np.float64)
    inv_freq = 1.0 / 10000.0 ** (half / DH)
    tt = np.arange(SEQ, dtype=np.float64)
    freqs = tt[:, None] * inv_freq[None, :]
    freqs = np.concatenate([freqs, freqs], axis=-1)
    base = (half + 0.4 * DH) / (1.4 * DH)
    power = (tt - SEQ // 2) / float(WIN // 2)
    scale = base[None, :] ** power[:, None]
    scale = np.concatenate([scale, scale], axis=-1)
    rot = np.stack([np.cos(freqs) * scale, np.sin(freqs) * scale,
                    np.cos(freqs) / scale, np.sin(freqs) / scale], axis=0).astype(f32)
    sh["c_rot"] = np.ascontiguousarray(rot.reshape(4, NT, 128, 64).transpose(2, 0, 1, 3))

    in_maps = []
    for core in range(NCORES):
        m = dict(sh)
        xb = x[2 * core:2 * core + 2]
        m["x_tok"] = np.stack([tokmaj(xb[0]), tokmaj(xb[1])])
        m["x_feaT"] = np.stack([feamaj(xb[0]), feamaj(xb[1])])
        in_maps.append(m)
    return in_maps


def _run(inputs, stages=ALL_STAGES):
    in_maps = _host_prep(inputs)
    p = P(bass.Bass("TRN2", target_bir_lowering=False, detect_race_conditions=False), stages)
    nc = p.build()
    res = run_bass_kernel_spmd(nc, in_maps, core_ids=list(range(NCORES)))
    outs = []
    for core in range(NCORES):
        o = res.results[core]["out"]
        for b_ in range(NB):
            outs.append(np.ascontiguousarray(o[b_].transpose(1, 0, 2).reshape(SEQ, D)))
    return np.stack(outs).astype(np.float32)


def kernel(**inputs):
    return _run(inputs, ALL_STAGES)


# version 15
# speedup vs baseline: 1.2086x; 1.2086x over previous
import numpy as np
import ml_dtypes
import concourse.bass as bass
import concourse.mybir as mybir
from concourse.bass_utils import run_bass_kernel_spmd
from contextlib import ExitStack

F32 = mybir.dt.float32
BF16 = mybir.dt.bfloat16
AF = mybir.ActivationFunctionType
ALU = mybir.AluOpType
AX = mybir.AxisListType
BF = ml_dtypes.bfloat16

NCORES = 8
NB = 2
NT = 16
D = 1024
SEQ = 2048
HID2 = 5460
HID = 2730
ALL_STAGES = ("A", "B0", "B1", "C", "D", "E")
FFN_ON = True
RELAX_INCW = False
PRECAST_ON = True


class Ctr:
    def __init__(self, h):
        self.h = h
        self.n = 0

    def inc(self, instr, k=1):
        self.n += k
        instr.then_inc(self.h, k)
        return instr

    def wait(self, eng, n=None):
        n = self.n if n is None else n
        if n > 0:
            eng.wait_ge(self.h, n)

    def incx(self, eng, instr, k=1):
        return self.inc(instr, k)

    def incw(self, eng, instr, k=1):
        self.inc(instr, k)
        if not RELAX_INCW:
            eng.wait_ge(self.h, self.n)
        return instr


class Slot:

    def __init__(self, handle):
        self.b = handle
        self.free = None

    def acquire(self, eng):
        if self.free is not None:
            c, n = self.free
            if n > 0:
                eng.wait_ge(c.h, n)

    def release(self, ctr):
        self.free = (ctr, ctr.n)


def bcast_last(ap3, nf):
    return bass.AP(ap3.tensor, ap3.offset, [ap3.ap[0], ap3.ap[1], [0, nf]])


def bcast_mid(ap2, nh):
    return bass.AP(ap2.tensor, ap2.offset, [ap2.ap[0], [0, nh], ap2.ap[1]])


class P:
    def __init__(self, nc, stages=ALL_STAGES):
        self.nc = nc
        self.stages = stages
        self.t, self.v, self.s, self.g, self.y = nc.tensor, nc.vector, nc.scalar, nc.gpsimd, nc.sync
        self.ctx = ExitStack()
        self.mm_i = 0
        self.cc_world = NCORES

    def nt(self, name):
        self.uid = getattr(self, "uid", 0) + 1
        return f"{name}_{self.uid}"

    def sb(self, name, shape, dt):
        return self.ctx.enter_context(self.nc.sbuf_tensor(self.nt(name), shape, dt))

    def next_mm(self):
        m = self.mm[self.mm_i % 4]
        self.mm_i += 1
        return m

    def powx(self, out, in_ap, bias_ap, pw):
        s = self.s
        self.sv.wait(s)
        self.ss.incw(s, s.activation(out=out, in_=in_ap, func=AF.Ln, bias=bias_ap, scale=1.0))
        self.ss.incw(s, s.activation(out=out, in_=out, func=AF.Exp, bias=self.zero1[:], scale=float(pw)))

    def build(self):
        nc = self.nc
        dt = nc.dram_tensor
        self.x_tok = dt("x_tok", [NB, 128, NT, D], F32, kind="ExternalInput").ap()
        self.x_feaT = dt("x_feaT", [NB, 128, 8, SEQ], F32, kind="ExternalInput").ap()
        self.out = dt("out", [NB, 128, NT, D], F32, kind="ExternalOutput").ap()
        self.wa_qkvT = dt("wa_qkvT", [128, 8, 3072], F32, kind="ExternalInput").ap()
        self.wa_outT = dt("wa_outT", [128, 8, D], F32, kind="ExternalInput").ap()
        self.ba_qkv = dt("ba_qkv", [128, 16], F32, kind="ExternalInput").ap()
        self.ba_bv = dt("ba_bv", [128, 8], F32, kind="ExternalInput").ap()
        self.ba_bout = dt("ba_bout", [1, D], F32, kind="ExternalInput").ap()
        self.wf_qkv = dt("wf_qkv", [2, 128, 8, 1536], F32, kind="ExternalInput").ap()
        self.wf_wo = dt("wf_wo", [2, 128, 4, D], F32, kind="ExternalInput").ap()
        self.wf_w1 = dt("wf_w1", [2, 128, 8, HID2], F32, kind="ExternalInput").ap()
        self.wf_w2 = dt("wf_w2", [2, 128, 22, D], F32, kind="ExternalInput").ap()
        self.wf_qs = dt("wf_qs", [1, 2, 64], F32, kind="ExternalInput").ap()
        self.wf_ks = dt("wf_ks", [1, 2, 64], F32, kind="ExternalInput").ap()
        self.wf_lng = dt("wf_lng", [128, 2, 8], F32, kind="ExternalInput").ap()
        self.wf_lnb = dt("wf_lnb", [128, 2, 8], F32, kind="ExternalInput").ap()
        self.wf_b1 = dt("wf_b1", [1, 2, HID2], F32, kind="ExternalInput").ap()
        self.wf_b2 = dt("wf_b2", [1, 2, D], F32, kind="ExternalInput").ap()
        self.w1_qT = dt("w1_qT", [128, 8, 256], F32, kind="ExternalInput").ap()
        self.w1_kT = dt("w1_kT", [128, 8, 256], F32, kind="ExternalInput").ap()
        self.w1_vT = dt("w1_vT", [128, 8, D], F32, kind="ExternalInput").ap()
        self.w1_oT = dt("w1_oT", [128, 8, D], F32, kind="ExternalInput").ap()
        self.w1_g = dt("w1_g", [1, 16], F32, kind="ExternalInput").ap()
        self.w1_b = dt("w1_b", [1, 16], F32, kind="ExternalInput").ap()
        self.we_wT = dt("we_wT", [128, 8, D], F32, kind="ExternalInput").ap()
        self.we_g = dt("we_g", [1, D], F32, kind="ExternalInput").ap()
        self.we_b = dt("we_b", [1, D], F32, kind="ExternalInput").ap()
        self.c_ident_bf = dt("c_ident_bf", [128, 128], BF16, kind="ExternalInput").ap()
        self.c_ident_f = dt("c_ident_f", [128, 128], F32, kind="ExternalInput").ap()
        self.c_ones_bf = dt("c_ones_bf", [1, 128], BF16, kind="ExternalInput").ap()
        self.c_ones_f = dt("c_ones_f", [1, 128], F32, kind="ExternalInput").ap()
        self.c_maskA = dt("c_maskA", [128, 256], BF16, kind="ExternalInput").ap()
        self.c_maskA0 = dt("c_maskA0", [128, 128], BF16, kind="ExternalInput").ap()
        self.c_maskF = dt("c_maskF", [128, 640], BF16, kind="ExternalInput").ap()
        self.c_triu = dt("c_triu", [128, 128], BF16, kind="ExternalInput").ap()
        self.c_psub = dt("c_psub", [128, 128], F32, kind="ExternalInput").ap()
        self.c_e0 = dt("c_e0", [1, 128], F32, kind="ExternalInput").ap()
        self.c_p127 = dt("c_p127", [128, 128], F32, kind="ExternalInput").ap()
        self.c_rot = dt("c_rot", [128, 4, NT, 64], F32, kind="ExternalInput").ap()
        self.x1_spill = dt("x1_spill", [128, NT, D], F32)
        self.w1_bf = dt("w1_bf", [2, 128, 8, 12, 512], BF16)
        self.x3_spill = dt("x3_spill", [128, NT, D], F32)
        self.ar_in = dt("ar_in", [1, 128], F32)
        self.ar_out = dt("ar_out", [1, 128], F32)

        E = self.ctx.enter_context
        self.sd = Ctr(E(nc.semaphore("sd")))
        self.sy_ = Ctr(E(nc.semaphore("sy")))
        self.sv = Ctr(E(nc.semaphore("sv")))
        self.ss = Ctr(E(nc.semaphore("ss")))
        self.st = Ctr(E(nc.semaphore("st")))
        self.scc = Ctr(E(nc.semaphore("scc")))

        self.X = self.sb("X", [128, NT, D], F32)
        self.XT = self.sb("XT", [128, 8, SEQ + 1], BF16)
        self.ident_bf = self.sb("ident_bf", [128, 128], BF16)
        self.ident_f = self.sb("ident_f", [128, 128], F32)
        self.ones_bf = self.sb("ones_bf", [1, 128], BF16)
        self.ones_f = self.sb("ones_f", [1, 128], F32)
        self.zero1 = self.sb("zero1", [128, 1], F32)
        self.eps1 = self.sb("eps1", [128, 1], F32)
        self.magic = self.sb("magic", [128, 1], F32)
        self.nmagic = self.sb("nmagic", [128, 1], F32)
        self.clipmax = self.sb("clipmax", [128, 1], F32)
        self.stats_e = self.sb("stats_e", [128, NB, NT, 2], F32)
        self.gmax = self.sb("gmax", [128, 1], F32)
        self.scale_b = self.sb("scale_b", [128, 1], F32)
        self.iscale_b = self.sb("iscale_b", [128, 1], F32)
        self.wf_lng_sb = self.sb("wf_lng_sb", [128, 2, 8], F32)
        self.wf_lnb_sb = self.sb("wf_lnb_sb", [128, 2, 8], F32)

        self.mm = [Slot(E(nc.psum_tensor(f"mm{i}", [128, 512], F32))) for i in range(4)]
        self.tsp = Slot(E(nc.psum_tensor("tsp", [128, 5, 128], BF16)))
        self.tsp2 = Slot(E(nc.psum_tensor("tsp2", [128, 5, 128], BF16)))
        self.po = Slot(E(nc.psum_tensor("po", [128, 512], F32)))
        self.po2 = Slot(E(nc.psum_tensor("po2", [128, 512], F32)))

        v, s, t, g, y = self.v, self.s, self.t, self.g, self.y
        self.sd.inc(g.dma_start(out=self.ident_bf[:], in_=self.c_ident_bf), 16)
        self.sd.inc(g.dma_start(out=self.ident_f[:], in_=self.c_ident_f), 16)
        self.sd.inc(g.dma_start(out=self.ones_bf[:], in_=self.c_ones_bf), 16)
        self.sd.inc(g.dma_start(out=self.ones_f[:], in_=self.c_ones_f), 16)
        self.sd.inc(g.dma_start(out=self.wf_lng_sb[:], in_=self.wf_lng), 16)
        self.sd.inc(g.dma_start(out=self.wf_lnb_sb[:], in_=self.wf_lnb), 16)
        self.sv.incw(v, v.memset(self.zero1[:], 0.0))
        self.sv.incw(v, v.memset(self.eps1[:], 1e-5))
        self.sv.incw(v, v.memset(self.magic[:], 12582912.0))
        self.sv.incw(v, v.memset(self.nmagic[:], -12582912.0))
        self.sv.incw(v, v.memset(self.clipmax[:], 1e12))
        self.sv.incw(v, v.memset(self.gmax[:], 0.0))
        for b in range(NB):
            self.load_x(b)
            if b == 0 and PRECAST_ON and ("B0" in self.stages or "B1" in self.stages):
                self.precast_w1()
            if "A" in self.stages:
                self.stage_A(b)
            for li, stg in ((0, "B0"), (1, "B1")):
                if stg in self.stages:
                    self.stage_B(b, li)
            if "C" in self.stages:
                self.stage_C(b)
            if "D" in self.stages:
                self.stage_D(b)
                if "E" in self.stages:
                    self.stage_D_eprep(b)
            if "E" in self.stages and b != NB - 1:
                self.ss.wait(y)
                self.sv.wait(y)
                for tc in range(4):
                    self.sy_.inc(y.dma_start(out=self.x3_spill.ap()[:, 4 * tc:4 * tc + 4, :],
                                             in_=self.X[:, 4 * tc:4 * tc + 4, :]), 16)
            if "E" not in self.stages:
                self.ss.wait(y)
                self.sv.wait(y)
                for tc in range(4):
                    self.sy_.inc(y.dma_start(out=self.out[b, :, 4 * tc:4 * tc + 4, :],
                                             in_=self.X[:, 4 * tc:4 * tc + 4, :]), 16)
        if "E" in self.stages:
            self.stage_E_all()
        y.wait_ge(self.sy_.h, self.sy_.n)
        self.ctx.close()
        return nc

    def load_x(self, b):
        y, v, s, t, g = self.y, self.v, self.s, self.t, self.g
        self.sv.wait(y)
        self.ss.wait(y)
        self.st.wait(y)
        self.sv.wait(g)
        self.ss.wait(g)
        self.st.wait(g)
        for tc in range(4):
            self.sy_.inc(y.dma_start(out=self.X[:, 4 * tc:4 * tc + 4, :],
                                     in_=self.x_tok[b, :, 4 * tc:4 * tc + 4, :]), 16)
        self.sy_.wait(v)
        self.sv.incw(v, v.memset(self.XT[:, :, 0:1], 0.0))
        self.sd.inc(g.dma_start(out=self.XT[:, :, 1:SEQ + 1], in_=self.x_feaT[b]), 16)

    def precast_w1(self):
        nc, g, y = self.nc, self.g, self.y
        with ExitStack() as c:
            E = c.enter_context
            scr = [E(nc.sbuf_tensor(self.nt("pc_scr"), [128, 8, 512], BF16)) for _ in range(2)]
            slots = [Slot(s_) for s_ in scr]
            for li in range(2):
                for cc in range(12):
                    p_, half = cc // 2, cc % 2
                    a0 = 512 * p_
                    a1 = min(a0 + 512, HID)
                    w = a1 - a0
                    c0 = a0 if half == 0 else HID + a0
                    i = cc % 2
                    slots[i].acquire(g)
                    self.sd.inc(g.dma_start(out=scr[i][:, :, 0:w], in_=self.wf_w1[li, :, :, c0:c0 + w]), 16)
                    self.sd.wait(y)
                    self.sy_.inc(y.dma_start(out=self.w1_bf.ap()[li, :, :, cc, :], in_=scr[i][:]), 16)
                    slots[i].release(self.sy_)
            g.wait_ge(self.sy_.h, self.sy_.n)
            y.wait_ge(self.sy_.h, self.sy_.n)

    def emit_w2_part(self, it, p_, ti, wch, dacc, w2, hTb, hT_s):
        t, v, s = self.t, self.v, self.s
        nhb = (wch + 127) // 128
        hT_s[it].acquire(t)
        self.sv.wait(t)
        banks = (self.tsp, self.tsp2)

        def evac_mm(hl, n_t):
            hw = min(128, wch - 128 * hl)
            sl = banks[hl % 2]
            if hl % 2 == 0:
                v.wait_ge(self.st.h, n_t)
                self.sv.incx(v, v.tensor_copy(out=hTb[0:hw, it, hl, :], in_=sl.b[0:hw, 2 + hl // 2, :]))
                sl.release(self.sv)
                t.wait_ge(self.sv.h, self.sv.n)
            else:
                s.wait_ge(self.st.h, n_t)
                self.ss.incx(s, s.copy(out=hTb[0:hw, it, hl, :], in_=sl.b[0:hw, 2 + hl // 2, :]))
                sl.release(self.ss)
                t.wait_ge(self.ss.h, self.ss.n)
            hb = 4 * p_ + hl
            for oc in range(2):
                dm = dacc[2 * ti + oc]
                self.st.inc(t.matmul(dm.b[:], hTb[0:hw, it, hl, :], w2[0:hw, hb, 512 * oc:512 * oc + 512],
                                     start=(p_ == 0 and hl == 0), stop=False, skip_group_check=True))

        pend = None
        for hl in range(nhb):
            hw = min(128, wch - 128 * hl)
            sl = banks[hl % 2]
            sl.acquire(t)
            self.st.inc(t.transpose(sl.b[0:hw, 2 + hl // 2, :],
                                    self.hidc_cur[it][:, 128 * hl:128 * hl + hw], self.ident_bf[:]))
            n_t = self.st.n
            if pend is not None:
                evac_mm(*pend)
            pend = (hl, n_t)
        self.hid_s_cur[it].release(self.st)
        if pend is not None:
            evac_mm(*pend)
        hT_s[it].release(self.st)

    def lazy_xt(self, T):
        t, v, s = self.t, self.v, self.s
        xbf = self.xbf_s
        xbf.acquire(s)
        self.sv.wait(s)
        self.ss.incx(s, s.copy(out=xbf.b[:], in_=self.X[:, T, :]))
        for db in range(8):
            self.tsp.acquire(t)
            self.ss.wait(t)
            self.st.inc(t.transpose(self.tsp.b[:, db % 5, :], xbf.b[:, 128 * db:128 * db + 128],
                                    self.ident_bf[:]))
            self.tsp.release(self.st)
            self.st.wait(v)
            self.sv.incw(v, v.tensor_copy(out=self.XT[:, db, 1 + 128 * T:1 + 128 * T + 128],
                                      in_=self.tsp.b[:, db % 5, :]))
            self.tsp.release(self.sv)
        xbf.release(self.st)

    def build_gbc(self, gbc):
        t, v, g = self.t, self.v, self.g
        with self.nc.sbuf_tensor(self.nt("grow"), [1, 2, D], F32) as grow:
            self.sv.wait(g)
            self.sd.inc(g.dma_start(out=grow[0:1, 0, :], in_=self.we_g), 16)
            self.sd.inc(g.dma_start(out=grow[0:1, 1, :], in_=self.we_b), 16)
            for which in range(2):
                for oc in range(2):
                    m = self.next_mm()
                    m.acquire(t)
                    t.wait_ge(self.sd.h, self.sd.n)
                    self.st.inc(t.matmul(m.b[:], self.ones_f[:], grow[0:1, which, 512 * oc:512 * oc + 512],
                                         start=True, stop=True, skip_group_check=True))
                    self.st.wait(v)
                    self.sv.incw(v, v.tensor_copy(out=gbc[:, which, 512 * oc:512 * oc + 512], in_=m.b[:]))
                    m.release(self.sv)

    def ln_combine(self, bn):
        v = self.v
        self.sv.incw(v, v.tensor_tensor(out=bn[:, 16:18], in0=bn[:, 12:14], in1=bn[:, 14:16], op=ALU.add))
        self.sv.incw(v, v.tensor_scalar(out=bn[:, 16:18], in0=bn[:, 16:18], scalar1=0.5, scalar2=None,
                                    op0=ALU.mult, op1=ALU.bypass))
        self.sv.incw(v, v.tensor_tensor(out=bn[:, 18:19], in0=bn[:, 12:13], in1=bn[:, 14:15], op=ALU.subtract))
        self.sv.incw(v, v.tensor_tensor(out=bn[:, 18:19], in0=bn[:, 18:19], in1=bn[:, 18:19], op=ALU.mult))
        self.sv.incw(v, v.scalar_tensor_tensor(out=bn[:, 17:18], in0=bn[:, 18:19], scalar=0.25,
                                           in1=bn[:, 17:18], op0=ALU.mult, op1=ALU.add))

    def ln_apply_tile(self, T):
        v, s = self.v, self.s
        bn = self.bn
        self.ss.wait(v)
        self.sv.incw(v, v.bn_stats(bn[:, 0:6], self.X[:, T, 0:512]))
        self.sv.incw(v, v.bn_stats(bn[:, 6:12], self.X[:, T, 512:1024]))
        self.sv.incw(v, v.bn_aggr(bn[:, 12:14], bn[:, 0:6]))
        self.sv.incw(v, v.bn_aggr(bn[:, 14:16], bn[:, 6:12]))
        self.ln_combine(bn)
        rstd, nmr = bn[:, 19:20], bn[:, 20:21]
        self.powx(rstd, bn[:, 17:18], self.eps1[:], -0.5)
        self.ss.wait(v)
        self.sv.incw(v, v.tensor_tensor(out=nmr, in0=bn[:, 16:17], in1=rstd, op=ALU.mult))
        self.sv.incw(v, v.tensor_scalar(out=nmr, in0=nmr, scalar1=-1.0, scalar2=None,
                                    op0=ALU.mult, op1=ALU.bypass))
        self.sv.wait(s)
        self.ss.incw(s, s.activation(out=self.X[:, T, :], in_=self.X[:, T, :], func=AF.Identity,
                                 bias=nmr, scale=rstd))

    def softmax_block(self, psum_aps, mask_aps, scf_slot, probs_slot, sm, ncols,
                      scf_ap=None, probs_ap=None):
        v, s = self.v, self.s
        scf_b = scf_ap if scf_ap is not None else scf_slot.b[:]
        probs_b = probs_ap if probs_ap is not None else probs_slot.b[:]
        c0 = 0
        for pap, map_ in zip(psum_aps, mask_aps):
            w = pap.shape[-1]
            self.st.wait(v)
            self.sv.incw(v, v.tensor_tensor(out=scf_b[:, c0:c0 + w], in0=pap, in1=map_, op=ALU.add))
            c0 += w
        assert c0 == ncols
        self.sv.incx(v, v.reduce_max(out=sm[:, 0:1], in_=scf_b[:, 0:ncols], axis=AX.X, negate=True))
        self.sv.wait(s)
        self.ss.incx(s, s.activation(out=probs_b[:, 0:ncols], in_=scf_b[:, 0:ncols], func=AF.Exp,
                                 bias=sm[:, 0:1], accum_out=sm[:, 1:2]))
        scf_slot.release(self.ss)
        self.ss.wait(v)
        self.sv.incw(v, v.reciprocal(out=sm[:, 2:3], in_=sm[:, 1:2]))
        self.sv.incx(v, v.tensor_scalar(out=probs_b[:, 0:ncols], in0=probs_b[:, 0:ncols],
                                    scalar1=sm[:, 2:3], scalar2=None, op0=ALU.mult, op1=ALU.bypass))

    def softmax_psum(self, psum_ap, probs_slot, sm, ncols, probs_ap=None):
        v, s = self.v, self.s
        probs_b = probs_ap if probs_ap is not None else probs_slot.b[:]
        self.st.wait(v)
        self.sv.incx(v, v.reduce_max(out=sm[:, 0:1], in_=psum_ap, axis=AX.X, negate=True))
        self.sv.wait(s)
        self.ss.incx(s, s.activation(out=probs_b[:, 0:ncols], in_=psum_ap, func=AF.Exp,
                                     bias=sm[:, 0:1], accum_out=sm[:, 1:2]))
        self.ss.wait(v)
        self.sv.incw(v, v.reciprocal(out=sm[:, 2:3], in_=sm[:, 1:2]))
        self.sv.incx(v, v.tensor_scalar(out=probs_b[:, 0:ncols], in0=probs_b[:, 0:ncols],
                                        scalar1=sm[:, 2:3], scalar2=None, op0=ALU.mult, op1=ALU.bypass))

    def softmax_psum2(self, papA, papB, probs_slot, sm, ncols, probs_ap):
        v, s = self.v, self.s
        if papA is None:
            return self.softmax_psum(papB, probs_slot, sm, ncols, probs_ap)
        wA = papA.shape[-1]
        self.st.wait(v)
        self.sv.incx(v, v.reduce_max(out=sm[:, 0:1], in_=papA, axis=AX.X, negate=True))
        self.sv.incw(v, v.reduce_max(out=sm[:, 3:4], in_=papB, axis=AX.X, negate=True))
        self.sv.incx(v, v.tensor_tensor(out=sm[:, 0:1], in0=sm[:, 0:1], in1=sm[:, 3:4], op=ALU.min))
        self.sv.wait(s)
        self.ss.incx(s, s.activation(out=probs_ap[:, 0:wA], in_=papA, func=AF.Exp,
                                     bias=sm[:, 0:1], accum_out=sm[:, 1:2]))
        self.ss.incx(s, s.activation(out=probs_ap[:, wA:ncols], in_=papB, func=AF.Exp,
                                     bias=sm[:, 0:1], accum_out=sm[:, 3:4]))
        self.ss.wait(v)
        self.sv.incw(v, v.tensor_tensor(out=sm[:, 1:2], in0=sm[:, 1:2], in1=sm[:, 3:4], op=ALU.add))
        self.sv.incw(v, v.reciprocal(out=sm[:, 2:3], in_=sm[:, 1:2]))
        self.sv.incx(v, v.tensor_scalar(out=probs_ap[:, 0:ncols], in0=probs_ap[:, 0:ncols],
                                        scalar1=sm[:, 2:3], scalar2=None, op0=ALU.mult, op1=ALU.bypass))

    def transpose_to(self, src_ap_list, dst_ap_list, ident):
        t, v, s = self.t, self.v, self.s
        banks = (self.tsp, self.tsp2)
        pend = None
        items = list(zip(src_ap_list, dst_ap_list))
        for i, (sap, dap) in enumerate(items):
            b = banks[i % 2]
            sl = (i // 2) % 5
            b.acquire(t)
            self.st.inc(t.transpose(b.b[0:sap.shape[-1], sl, :], sap, ident))
            n_t = self.st.n
            if pend is not None:
                self._tp_evac(*pend)
            pend = (b, (i // 2) % 5, items[i][0].shape[-1], dap, n_t, i)
        if pend is not None:
            self._tp_evac(*pend)

    def _tp_evac(self, b, sl, w, dap, n_t, i):
        v, s = self.v, self.s
        if i % 2 == 0:
            v.wait_ge(self.st.h, n_t)
            self.sv.incx(v, v.tensor_copy(out=dap, in_=b.b[0:w, sl, :]))
            b.release(self.sv)
        else:
            s.wait_ge(self.st.h, n_t)
            self.ss.incx(s, s.copy(out=dap, in_=b.b[0:w, sl, :]))
            b.release(self.ss)

    def stage_A(self, b):
        nc, t, v, s, g, y = self.nc, self.t, self.v, self.s, self.g, self.y
        with ExitStack() as c:
            E = c.enter_context
            wqkv = E(nc.sbuf_tensor(self.nt("a_wqkv"), [128, 8, 3072], BF16))
            wout = E(nc.sbuf_tensor(self.nt("a_wout"), [128, 8, D], BF16))
            bqk = E(nc.sbuf_tensor(self.nt("a_bqk"), [128, 16], F32))
            bv = E(nc.sbuf_tensor(self.nt("a_bv"), [128, 8], BF16))
            brow = E(nc.sbuf_tensor(self.nt("a_brow"), [1, D], BF16))
            browf = E(nc.sbuf_tensor(self.nt("a_browf"), [1, D], F32))
            qk_c = E(nc.sbuf_tensor(self.nt("a_qk"), [128, 16, 512], BF16))
            kcar = E(nc.sbuf_tensor(self.nt("a_kcar"), [128, 8, 128], BF16))
            v_c = E(nc.sbuf_tensor(self.nt("a_v"), [128, 4, D], BF16))
            vcar = E(nc.sbuf_tensor(self.nt("a_vcar"), [128, D], BF16))
            probs = E(nc.sbuf_tensor(self.nt("a_probs"), [128, 3, 256], BF16))
            pT = E(nc.sbuf_tensor(self.nt("a_pT"), [128, 3, 2, 128], BF16))
            oT = E(nc.sbuf_tensor(self.nt("a_oT"), [128, 8, 128], BF16))
            sm = E(nc.sbuf_tensor(self.nt("a_sm"), [128, 3, 4], F32))
            self.bn = E(nc.sbuf_tensor(self.nt("a_bn"), [128, 24], F32))
            maskA = E(nc.sbuf_tensor(self.nt("a_maskA"), [128, 256], BF16))
            maskA0 = E(nc.sbuf_tensor(self.nt("a_maskA0"), [128, 128], BF16))

            qk_s, v_s = Slot(qk_c), Slot(v_c)
            oT_s = Slot(oT)
            probs_s = [Slot(None), Slot(None), Slot(None)]
            pT_s = [Slot(None), Slot(None), Slot(None)]
            kcar_s, vcar_s = Slot(kcar), Slot(vcar)

            self.st.wait(g)
            self.sd.inc(g.dma_start(out=maskA[:], in_=self.c_maskA), 16)
            self.sd.inc(g.dma_start(out=maskA0[:], in_=self.c_maskA0), 16)
            self.sd.inc(g.dma_start(out=wqkv[:], in_=self.wa_qkvT), 16)
            self.sd.inc(g.dma_start(out=wout[:], in_=self.wa_outT), 16)
            self.sd.inc(g.dma_start(out=bqk[:], in_=self.ba_qkv), 16)
            self.sd.inc(g.dma_start(out=bv[:], in_=self.ba_bv), 16)
            self.sd.inc(g.dma_start(out=browf[:], in_=self.ba_bout), 16)

            self.sd.wait(s)
            self.ss.incw(s, s.mul(out=bqk[:, 0:8], in_=bqk[:, 0:8], mul=0.125))

            t.wait_ge(self.sd.h, self.sd.n)
            for oc in range(2):
                m = self.next_mm()
                m.acquire(t)
                for db in range(8):
                    self.st.inc(t.matmul(m.b[0:1, :], bv[:, db:db + 1], wout[:, db, 512 * oc:512 * oc + 512],
                                         start=(db == 0), stop=(db == 7), skip_group_check=True))
                self.st.wait(v)
                self.sv.incw(v, v.tensor_tensor(out=browf[0:1, 512 * oc:512 * oc + 512],
                                            in0=m.b[0:1, :], in1=browf[0:1, 512 * oc:512 * oc + 512], op=ALU.add))
                m.release(self.sv)
            self.sv.incw(v, v.tensor_copy(out=brow[:], in_=browf[:]))

            self.sd.wait(t)
            self.sv.wait(t)
            self.sy_.wait(t)
            self.sy_.wait(v)

            for tc in range(4):
                qk_s.acquire(s)
                qk_s.acquire(t)
                for ob in range(16):
                    m = self.next_mm()
                    m.acquire(t)
                    for db in range(8):
                        self.st.inc(t.matmul(m.b[:], wqkv[:, db, 128 * ob:128 * ob + 128],
                                             self.XT[:, db, 1 + 512 * tc:1 + 512 * tc + 512],
                                             start=(db == 0), stop=(db == 7), skip_group_check=True))
                    self.st.wait(s)
                    self.ss.incw(s, s.activation(out=qk_c[:, ob, :], in_=m.b[:], func=AF.Identity,
                                             bias=bqk[:, ob:ob + 1], scale=(0.125 if ob < 8 else 1.0)))
                    m.release(self.ss)
                v_s.acquire(v)
                for tt in range(4):
                    T = 4 * tc + tt
                    for vc in range(2):
                        m = self.next_mm()
                        m.acquire(t)
                        for db in range(8):
                            self.st.inc(t.matmul(m.b[:], self.XT[:, db, 1 + 128 * T:1 + 128 * T + 128],
                                                 wqkv[:, db, 2048 + 512 * vc:2048 + 512 * vc + 512],
                                                 start=(db == 0), stop=(db == 7), skip_group_check=True))
                        self.st.wait(v)
                        self.sv.incw(v, v.tensor_copy(out=v_c[:, tt, 512 * vc:512 * vc + 512], in_=m.b[:]))
                        m.release(self.sv)

                for tt in range(4):
                    T = 4 * tc + tt
                    ncols = 128 if T == 0 else 256
                    nkt = ncols // 128

                    def a_tail(h, par, sv_n):
                        hp, hh = h // 2, h % 2
                        opo = self.po if hp % 2 == 0 else self.po2
                        pT_s[par].acquire(v)
                        pT_s[par].acquire(s)
                        t.wait_ge(self.sv.h, sv_n)
                        self.transpose_to([probs[:, par, 128 * kt:128 * kt + 128] for kt in range(nkt)],
                                          [pT[:, par, kt, :] for kt in range(nkt)], self.ident_bf[:])
                        probs_s[par].release(self.st)
                        t.wait_ge(self.sv.h, self.sv.n)
                        t.wait_ge(self.ss.h, self.ss.n)
                        if hh == 0:
                            opo.acquire(t)
                        tp = (0, 64 * hh)
                        od = opo.b[64 * hh:64 * hh + 64, 0:128]
                        if T == 0:
                            self.st.inc(t.matmul(od, v_c[:, tt, 64 * h:64 * h + 64], pT[:, par, 0, :],
                                                 start=True, stop=True, tile_position=tp,
                                                 skip_group_check=True))
                        else:
                            vprev = vcar[:, 64 * h:64 * h + 64] if tt == 0 else v_c[:, tt - 1, 64 * h:64 * h + 64]
                            self.st.inc(t.matmul(od, vprev, pT[:, par, 0, :],
                                                 start=True, stop=False, tile_position=tp,
                                                 skip_group_check=True))
                            self.st.inc(t.matmul(od, v_c[:, tt, 64 * h:64 * h + 64], pT[:, par, 1, :],
                                                 start=False, stop=True, tile_position=tp,
                                                 skip_group_check=True))
                        pT_s[par].release(self.st)
                        if hh == 1:
                            oT_s.acquire(v)
                            v.wait_ge(self.st.h, self.st.n)
                            self.sv.incw(v, v.tensor_copy(out=oT[:, hp, :], in_=opo.b[:, 0:128]))
                            opo.release(self.sv)

                    from collections import deque
                    pendq = deque()
                    self.ss.wait(t)
                    self.sv.wait(t)
                    for h in range(16):
                        bp, pbq, pbk = 64 * (h % 2), h // 2, 8 + h // 2
                        par = h % 3
                        msc = self.next_mm()
                        msc.acquire(t)
                        if T == 0:
                            self.st.inc(t.matmul(msc.b[:, 0:128], qk_c[bp:bp + 64, pbq, 0:128],
                                                 qk_c[bp:bp + 64, pbk, 0:128], start=True, stop=False,
                                                 skip_group_check=True))
                        elif tt == 0:
                            self.st.inc(t.matmul(msc.b[:, 0:128], qk_c[bp:bp + 64, pbq, 0:128],
                                                 kcar[bp:bp + 64, h // 2, :], start=True, stop=False,
                                                 skip_group_check=True))
                            self.st.inc(t.matmul(msc.b[:, 128:256], qk_c[bp:bp + 64, pbq, 0:128],
                                                 qk_c[bp:bp + 64, pbk, 0:128], start=False, stop=False,
                                                 skip_group_check=True))
                        else:
                            self.st.inc(t.matmul(msc.b[:, 0:256], qk_c[bp:bp + 64, pbq, 128 * tt:128 * tt + 128],
                                                 qk_c[bp:bp + 64, pbk, 128 * (tt - 1):128 * (tt - 1) + 256],
                                                 start=True, stop=False, skip_group_check=True))
                        mask = maskA0[:, 0:128] if T == 0 else maskA[:, 0:256]
                        self.st.inc(t.matmul(msc.b[:, 0:ncols], self.ident_bf[:], mask,
                                             start=False, stop=True, skip_group_check=True))
                        if len(pendq) >= 2:
                            a_tail(*pendq.popleft())
                        probs_s[par].acquire(s)
                        self.softmax_psum(msc.b[:, 0:ncols], probs_s[par],
                                          sm[:, par, :], ncols, probs[:, par, :])
                        msc.release(self.ss)
                        pendq.append((h, par, self.sv.n))
                    while pendq:
                        a_tail(*pendq.popleft())

                    for oc in range(2):
                        m = self.next_mm()
                        m.acquire(t)
                        self.sv.wait(t)
                        for fb in range(8):
                            self.st.inc(t.matmul(m.b[:], oT[:, fb, :], wout[:, fb, 512 * oc:512 * oc + 512],
                                                 start=(fb == 0), stop=False, skip_group_check=True))
                        self.st.inc(t.matmul(m.b[:], self.ones_bf[:], brow[0:1, 512 * oc:512 * oc + 512],
                                             start=False, stop=True, skip_group_check=True))
                        self.st.wait(v)
                        self.sv.incw(v, v.tensor_tensor(out=self.X[:, T, 512 * oc:512 * oc + 512],
                                                    in0=m.b[:], in1=self.X[:, T, 512 * oc:512 * oc + 512],
                                                    op=ALU.add))
                        m.release(self.sv)
                    oT_s.release(self.st)
                    self.ln_apply_tile(T)
                    if "B0" in self.stages or "C" in self.stages:
                        self.ss.wait(y)
                        self.sy_.inc(y.dma_start(out=self.x1_spill.ap()[:, T, :], in_=self.X[:, T, :]), 16)
                if tc < 3:
                    kcar_s.acquire(v)
                    vcar_s.acquire(v)
                    self.st.wait(v)
                    self.sv.incw(v, v.tensor_copy(out=kcar[:], in_=qk_c[:, 8:16, 384:512]))
                    self.sv.incw(v, v.tensor_copy(out=vcar[:], in_=v_c[:, 3, :]))
                    kcar_s.release(self.sv)
                    vcar_s.release(self.sv)
                qk_s.release(self.st)
                v_s.release(self.st)

    def stage_B(self, b, li):
        nc, t, v, s, g, y = self.nc, self.t, self.v, self.s, self.g, self.y
        with ExitStack() as c:
            E = c.enter_context
            self.bn = E(nc.sbuf_tensor(self.nt("b_bn"), [128, 24], F32))
            with ExitStack() as p1:
                E1 = p1.enter_context
                self.xbf_s = Slot(E1(nc.sbuf_tensor(self.nt("b_xbf"), [128, D], BF16)))
                wqkv = E1(nc.sbuf_tensor(self.nt("b_wqkv"), [128, 8, 1536], BF16))
                wo = E1(nc.sbuf_tensor(self.nt("b_wo"), [128, 4, D], BF16))
                qsb = E1(nc.sbuf_tensor(self.nt("b_qsb"), [128, 64], F32))
                ksb = E1(nc.sbuf_tensor(self.nt("b_ksb"), [128, 64], F32))
                qsrow = E1(nc.sbuf_tensor(self.nt("b_qsrow"), [1, 128], F32))
                kT = E1(nc.sbuf_tensor(self.nt("b_kT"), [128, 4, SEQ], BF16))
                vfull = E1(nc.sbuf_tensor(self.nt("b_v"), [128, NT, 512], BF16))
                qf = E1(nc.sbuf_tensor(self.nt("b_qf"), [128, 512], F32))
                kf = E1(nc.sbuf_tensor(self.nt("b_kf"), [128, 512], F32))
                rtmp = E1(nc.sbuf_tensor(self.nt("b_rtmp"), [128, 512], F32))
                qe = E1(nc.sbuf_tensor(self.nt("b_qe"), [128, 512], BF16))
                ke = E1(nc.sbuf_tensor(self.nt("b_ke"), [128, 512], BF16))
                qT_t = E1(nc.sbuf_tensor(self.nt("b_qT"), [128, 4, 128], BF16))
                scf = E1(nc.sbuf_tensor(self.nt("b_scf"), [128, 1, 640], F32))
                probs = E1(nc.sbuf_tensor(self.nt("b_probs"), [128, 3, 640], BF16))
                pT = E1(nc.sbuf_tensor(self.nt("b_pT"), [128, 3, 5, 128], BF16))
                oT = E1(nc.sbuf_tensor(self.nt("b_oT"), [128, 4, 128], BF16))
                sm = E1(nc.sbuf_tensor(self.nt("b_sm"), [128, 3, 4], F32))
                nrm = E1(nc.sbuf_tensor(self.nt("b_nrm"), [128, 2, 8], F32))
                rot = E1(nc.sbuf_tensor(self.nt("b_rot"), [128, 4, NT, 64], F32))
                rotscr = E1(nc.sbuf_tensor(self.nt("b_rotscr"), [128, 8, 64], F32))
                maskF = E1(nc.sbuf_tensor(self.nt("b_maskF"), [128, 640], BF16))

                self.st.wait(g)
                self.ss.wait(g)
                self.sv.wait(g)
                self.sd.inc(g.dma_start(out=rot[:], in_=self.c_rot), 16)
                self.sd.inc(g.dma_start(out=maskF[:], in_=self.c_maskF), 16)
                self.sd.inc(g.dma_start(out=wqkv[:], in_=self.wf_qkv[li]), 16)
                self.sd.inc(g.dma_start(out=wo[:], in_=self.wf_wo[li]), 16)
                self.sd.inc(g.dma_start(out=qsrow[0:1, 0:64], in_=self.wf_qs[0:1, li, :]), 16)
                self.sd.inc(g.dma_start(out=qsrow[0:1, 64:128], in_=self.wf_ks[0:1, li, :]), 16)
                m = self.next_mm()
                m.acquire(t)
                t.wait_ge(self.sd.h, self.sd.n)
                self.st.inc(t.matmul(m.b[:, 0:128], self.ones_f[:], qsrow[:], start=True, stop=True,
                                     skip_group_check=True))
                self.st.wait(v)
                self.sv.incw(v, v.tensor_scalar(out=qsb[:], in0=m.b[:, 0:64], scalar1=8.0, scalar2=None,
                                            op0=ALU.mult, op1=ALU.bypass))
                self.sv.incw(v, v.tensor_copy(out=ksb[:], in_=m.b[:, 64:128]))
                m.release(self.sv)

                qf_s, kf_s, rtmp_s = Slot(qf), Slot(kf), Slot(rtmp)
                qe_s, ke_s, qT_s = Slot(qe), Slot(ke), Slot(qT_t)
                oT_s = Slot(oT)
                scf_s = [Slot(None), Slot(None)]
                probs_s = [Slot(None), Slot(None), Slot(None)]
                pT_s = [Slot(None), Slot(None), Slot(None)]
                vf_s, kTf_s = Slot(vfull), Slot(kT)

                for T in range(NT):
                    self.lazy_xt(T)

                    def hsl(db, T=T):
                        off = 1 if db < 4 else 0
                        return self.XT[:, db, off + 128 * T: off + 128 * T + 128]

                    for part in range(3):
                        m = self.next_mm()
                        m.acquire(t)
                        self.sv.wait(t)
                        for db in range(8):
                            self.st.inc(t.matmul(m.b[:], hsl(db), wqkv[:, db, 512 * part:512 * part + 512],
                                                 start=(db == 0), stop=(db == 7), skip_group_check=True))
                        if part == 0:
                            qf_s.acquire(s)
                            self.st.wait(s)
                            self.ss.incw(s, s.copy(out=qf[:], in_=m.b[:]))
                            m.release(self.ss)
                        elif part == 1:
                            kf_s.acquire(v)
                            self.st.wait(v)
                            self.sv.incw(v, v.tensor_copy(out=kf[:], in_=m.b[:]))
                            m.release(self.sv)
                        else:
                            vf_s.acquire(v)
                            self.st.wait(v)
                            self.sv.incw(v, v.tensor_copy(out=vfull[:, T, :], in_=m.b[:]))
                            m.release(self.sv)

                    for which in range(2):
                        src = qf if which == 0 else kf
                        src_s = qf_s if which == 0 else kf_s
                        dst = qe if which == 0 else ke
                        dst_s = qe_s if which == 0 else ke_s
                        scl = qsb if which == 0 else ksb
                        ci, si = (0, 1) if which == 0 else (2, 3)
                        src3 = src[:].rearrange("p (h f) -> p h f", h=8)
                        scf_s[0].acquire(v)
                        self.ss.wait(v)
                        sqv = scf[:, 0, 0:512].rearrange("p (h f) -> p h f", h=8)
                        self.sv.incw(v, v.tensor_tensor(out=sqv, in0=src3, in1=src3, op=ALU.mult))
                        self.sv.incw(v, v.reduce_sum(out=nrm[:, 0, :], in_=sqv, axis=AX.X))
                        self.powx(nrm[:, 1, :], nrm[:, 0, :], self.zero1[:], -0.5)
                        self.ss.wait(v)
                        self.sv.incw(v, v.tensor_scalar(out=nrm[:, 1, :], in0=nrm[:, 1, :],
                                                    scalar1=self.clipmax[:, 0:1], scalar2=None,
                                                    op0=ALU.min, op1=ALU.bypass))
                        qh = sqv
                        rb = bcast_last(nrm[:, 1, :].rearrange("p (h a) -> p h a", a=1), 64)
                        self.sv.incw(v, v.tensor_tensor(out=qh, in0=src3, in1=rb, op=ALU.mult))
                        src_s.release(self.sv)
                        self.sv.incw(v, v.tensor_tensor(out=qh, in0=qh, in1=bcast_mid(scl[:], 8), op=ALU.mult))
                        cosb = bcast_mid(rot[:, ci, T, :], 8)
                        sin_lo = bcast_mid(rot[:, si, T, 0:32], 8)
                        sin_hi = bcast_mid(rot[:, si, T, 32:64], 8)
                        rtmp_s.acquire(v)
                        tmp3 = rtmp[:].rearrange("p (h f) -> p h f", h=8)
                        self.sv.incw(v, v.tensor_tensor(out=tmp3, in0=qh, in1=cosb, op=ALU.mult))
                        self.sv.incw(v, v.tensor_tensor(out=rotscr[:, :, 0:32], in0=qh[:, :, 0:32],
                                                    in1=sin_hi, op=ALU.mult))
                        self.sv.incw(v, v.tensor_tensor(out=rotscr[:, :, 32:64], in0=qh[:, :, 32:64],
                                                    in1=sin_lo, op=ALU.mult))
                        scf_s[0].release(self.sv)
                        self.sv.incw(v, v.tensor_tensor(out=tmp3[:, :, 32:64], in0=tmp3[:, :, 32:64],
                                                    in1=rotscr[:, :, 0:32], op=ALU.add))
                        self.sv.incw(v, v.tensor_tensor(out=tmp3[:, :, 0:32], in0=tmp3[:, :, 0:32],
                                                    in1=rotscr[:, :, 32:64], op=ALU.subtract))
                        dst_s.acquire(v)
                        self.sv.incw(v, v.tensor_copy(out=dst[:], in_=rtmp[:]))
                        rtmp_s.release(self.sv)
                        if which == 0:
                            qT_s.acquire(v)
                            qT_s.acquire(s)
                            self.sv.wait(t)
                            self.transpose_to([qe[:, 128 * cc:128 * cc + 128] for cc in range(4)],
                                              [qT_t[:, cc, :] for cc in range(4)], self.ident_bf[:])
                        else:
                            kTf_s.acquire(v)
                            kTf_s.acquire(s)
                            self.sv.wait(t)
                            self.transpose_to([ke[:, 128 * cc:128 * cc + 128] for cc in range(4)],
                                              [kT[:, cc, 128 * T:128 * T + 128] for cc in range(4)],
                                              self.ident_bf[:])
                        dst_s.release(self.st)

                    nk = min(T, 4)
                    ncols = 128 * (nk + 1)

                    def attn_tail(h, par, sv_n):
                        hp, hh = h // 2, h % 2
                        opo = self.po if hp % 2 == 0 else self.po2
                        pT_s[par].acquire(v)
                        pT_s[par].acquire(s)
                        t.wait_ge(self.sv.h, sv_n)
                        self.transpose_to([probs[:, par, 128 * kt:128 * kt + 128] for kt in range(nk + 1)],
                                          [pT[:, par, kt, :] for kt in range(nk + 1)], self.ident_bf[:])
                        probs_s[par].release(self.st)
                        t.wait_ge(self.sv.h, self.sv.n)
                        t.wait_ge(self.ss.h, self.ss.n)
                        if hh == 0:
                            opo.acquire(t)
                        tp = (0, 64 * hh)
                        od = opo.b[64 * hh:64 * hh + 64, 0:128]
                        for kt in range(nk + 1):
                            ktabs = T - nk + kt
                            self.st.inc(t.matmul(od, vfull[:, ktabs, 64 * h:64 * h + 64], pT[:, par, kt, :],
                                                 start=(kt == 0), stop=(kt == nk),
                                                 tile_position=tp, skip_group_check=True))
                        pT_s[par].release(self.st)
                        if hh == 1:
                            oT_s.acquire(v)
                            v.wait_ge(self.st.h, self.st.n)
                            self.sv.incw(v, v.tensor_copy(out=oT[:, hp, :], in_=opo.b[:, 0:128]))
                            opo.release(self.sv)

                    from collections import deque
                    pendq = deque()
                    self.sv.wait(t)
                    self.ss.wait(t)
                    for h in range(8):
                        bp, pb = 64 * (h % 2), h // 2
                        par = h % 3
                        mA = self.next_mm()
                        mB = self.next_mm()
                        mA.acquire(t)
                        mB.acquire(t)
                        if nk > 0:
                            self.st.inc(t.matmul(mA.b[:, 0:128 * nk], qT_t[bp:bp + 64, pb, :],
                                                 kT[bp:bp + 64, pb, 128 * (T - nk):128 * T],
                                                 start=True, stop=False, skip_group_check=True))
                            self.st.inc(t.matmul(mA.b[:, 0:128 * nk], self.ident_bf[:],
                                                 maskF[:, 512 - 128 * nk:512],
                                                 start=False, stop=True, skip_group_check=True))
                        self.st.inc(t.matmul(mB.b[:, 0:128], qT_t[bp:bp + 64, pb, :],
                                             kT[bp:bp + 64, pb, 128 * T:128 * T + 128],
                                             start=True, stop=False, skip_group_check=True))
                        self.st.inc(t.matmul(mB.b[:, 0:128], self.ident_bf[:], maskF[:, 512:640],
                                             start=False, stop=True, skip_group_check=True))
                        if len(pendq) >= 2:
                            attn_tail(*pendq.popleft())
                        probs_s[par].acquire(s)
                        self.softmax_psum2(mA.b[:, 0:128 * nk] if nk > 0 else None, mB.b[:, 0:128],
                                           probs_s[par], sm[:, par, :], ncols, probs[:, par, :])
                        mA.release(self.ss)
                        mB.release(self.ss)
                        pendq.append((h, par, self.sv.n))
                    while pendq:
                        attn_tail(*pendq.popleft())
                    for oc in range(2):
                        m = self.next_mm()
                        m.acquire(t)
                        self.sv.wait(t)
                        for fb in range(4):
                            self.st.inc(t.matmul(m.b[:], oT[:, fb, :], wo[:, fb, 512 * oc:512 * oc + 512],
                                                 start=(fb == 0), stop=(fb == 3), skip_group_check=True))
                        self.st.wait(v)
                        self.sv.incw(v, v.tensor_tensor(out=self.X[:, T, 512 * oc:512 * oc + 512],
                                                    in0=m.b[:], in1=self.X[:, T, 512 * oc:512 * oc + 512],
                                                    op=ALU.add))
                        m.release(self.sv)
                    oT_s.release(self.st)
                    vf_s.release(self.st)
                    kTf_s.release(self.st)
                    qT_s.release(self.st)
            if FFN_ON:
                self.stage_B_ffn(b, li)

    def stage_B_ffn(self, b, li):
        nc, t, v, s, g, y = self.nc, self.t, self.v, self.s, self.g, self.y
        with ExitStack() as c:
            E = c.enter_context
            rows_s = E(nc.sbuf_tensor(self.nt("b_rows_s"), [1, SEQ], BF16))
            rows_nm = E(nc.sbuf_tensor(self.nt("b_rows_nm"), [1, SEQ], BF16))
            rstd_all = E(nc.sbuf_tensor(self.nt("b_rstd"), [128, NT], F32))
            lgw1 = E(nc.sbuf_tensor(self.nt("b_lgw1"), [1, HID2], BF16))
            lb1 = E(nc.sbuf_tensor(self.nt("b_lb1"), [1, HID2], BF16))
            b2r = E(nc.sbuf_tensor(self.nt("b_b2r"), [1, D], BF16))
            lglb = E(nc.sbuf_tensor(self.nt("b_lglb"), [128, 8, 2], BF16))
            histat = E(nc.sbuf_tensor(self.nt("b_histat"), [128, 2, 2], F32))
            psub = E(nc.sbuf_tensor(self.nt("b_psub"), [128, 128], F32))
            p127 = E(nc.sbuf_tensor(self.nt("b_p127"), [128, 128], F32))
            stt3 = E(nc.sbuf_tensor(self.nt("b_stt3"), [128, 2], F32))
            bn = self.bn

            self.st.wait(g)
            self.ss.wait(g)
            self.sv.wait(g)
            self.sd.inc(g.dma_start(out=psub[:], in_=self.c_psub), 16)
            self.sd.inc(g.dma_start(out=p127[:], in_=self.c_p127), 16)
            self.sd.inc(g.dma_start(out=lb1[:], in_=self.wf_b1[0:1, li, :]), 16)
            self.sd.inc(g.dma_start(out=b2r[:], in_=self.wf_b2[0:1, li, :]), 16)
            self.sd.wait(s)
            for db in range(8):
                self.ss.incw(s, s.copy(out=lglb[:, db, 0:1], in_=self.wf_lng_sb[:, li, db:db + 1]))
                self.ss.incw(s, s.copy(out=lglb[:, db, 1:2], in_=self.wf_lnb_sb[:, li, db:db + 1]))

            self.sv.incw(v, v.memset(self.XT[:, :, 0:1], 0.0))
            for T in range(NT):
                for db in range(8):
                    m = self.next_mm()
                    m.acquire(t)
                    self.sv.wait(t)
                    self.st.inc(t.transpose(m.b[:, 0:128], self.X[:, T, 128 * db:128 * db + 128],
                                            self.ident_f[:]))
                    self.st.wait(s)
                    self.ss.incx(s, s.activation(out=self.XT[:, db, 1 + 128 * T:1 + 128 * T + 128],
                                             in_=m.b[:, 0:128], func=AF.Identity,
                                             bias=self.zero1[:], scale=self.wf_lng_sb[:, li, db:db + 1]))
                    m.release(self.ss)

            hist_s, stt3_s = Slot(histat), Slot(stt3)
            for T in range(NT):
                self.ss.wait(v)
                self.sv.incw(v, v.bn_stats(bn[:, 0:6], self.X[:, T, 0:512]))
                self.sv.incw(v, v.bn_stats(bn[:, 6:12], self.X[:, T, 512:1024]))
                self.sv.incw(v, v.bn_aggr(bn[:, 12:14], bn[:, 0:6]))
                hist_s.acquire(v)
                self.sv.incw(v, v.bn_aggr(histat[:, T % 2, :], bn[:, 6:12]))
                self.po.acquire(t)
                self.sv.wait(t)
                self.st.inc(t.matmul(self.po.b[:, 0:2], psub[:], histat[:, T % 2, :],
                                     start=True, stop=(T == 0), skip_group_check=True))
                if T > 0:
                    self.st.inc(t.matmul(self.po.b[:, 0:2], p127[:], histat[:, (T - 1) % 2, :],
                                         start=False, stop=True, skip_group_check=True))
                hist_s.release(self.st)
                self.st.wait(v)
                self.sv.incw(v, v.tensor_copy(out=bn[:, 14:16], in_=self.po.b[:, 0:2]))
                self.po.release(self.sv)
                self.ln_combine(bn)
                stt3_s.acquire(s)
                self.powx(stt3[:, 0:1], bn[:, 17:18], self.eps1[:], +0.5)
                self.ss.wait(v)
                self.sv.incw(v, v.reciprocal(out=rstd_all[:, T:T + 1], in_=stt3[:, 0:1]))
                self.sv.incw(v, v.tensor_scalar(out=stt3[:, 1:2], in0=bn[:, 16:17], scalar1=-1.0, scalar2=None,
                                            op0=ALU.mult, op1=ALU.bypass))
                for which in range(2):
                    m = self.next_mm()
                    m.acquire(t)
                    self.sv.wait(t)
                    self.st.inc(t.transpose(m.b[0:1, 0:128], stt3[:, which:which + 1], self.ident_f[:]))
                    self.st.wait(s)
                    dst = rows_s if which == 0 else rows_nm
                    self.ss.incw(s, s.copy(out=dst[0:1, 128 * T:128 * T + 128], in_=m.b[0:1, 0:128]))
                    m.release(self.ss)
                stt3_s.release(self.st)

            w2 = E(nc.sbuf_tensor(self.nt("b_w2"), [128, 22, D], BF16))
            w1c = [E(nc.sbuf_tensor(self.nt("b_w1c"), [128, 8, 512], BF16)) for _ in range(3)]
            hidc = [E(nc.sbuf_tensor(self.nt("b_hidc"), [128, 512], BF16)) for _ in range(3)]
            hTb = E(nc.sbuf_tensor(self.nt("b_hTb"), [128, 3, 4, 128], BF16))
            gel = E(nc.sbuf_tensor(self.nt("b_gel"), [128, 3, 512], BF16))
            w1_s = [Slot(w) for w in w1c]
            hid_s = [Slot(h) for h in hidc]
            self.hidc_cur = hidc
            self.hid_s_cur = hid_s
            hT_s = [Slot(None), Slot(None), Slot(None)]
            gel_s = [Slot(None), Slot(None), Slot(None)]
            self.sy_.wait(g)
            self.st.wait(g)
            self.sd.inc(g.dma_start(out=w2[:], in_=self.wf_w2[li]), 16)
            w2_ready = self.sd.n
            rows_done = [False] * 6
            CH = [(512 * p_, min(512 * p_ + 512, HID)) for p_ in range(6)]
            for sub in range(8):
                tiles = (2 * sub, 2 * sub + 1)
                dacc = [self.next_mm() for _ in range(4)]
                for dm in dacc:
                    dm.acquire(t)
                from collections import deque
                pendq = deque()
                for p_, (a0, a1) in enumerate(CH):
                    wch = a1 - a0
                    ia, ig = (2 * p_) % 3, (2 * p_ + 1) % 3
                    w1_s[ia].acquire(g)
                    self.sd.inc(g.dma_start(out=w1c[ia][:], in_=self.w1_bf.ap()[li, :, :, 2 * p_, :]), 16)
                    w1_s[ig].acquire(g)
                    self.sd.inc(g.dma_start(out=w1c[ig][:], in_=self.w1_bf.ap()[li, :, :, 2 * p_ + 1, :]), 16)
                    ld_n = self.sd.n
                    if sub == 0 and not rows_done[p_]:
                        t.wait_ge(self.sd.h, ld_n)
                        self.ss.wait(t)
                        for which in range(2):
                            m2 = self.po if which == 0 else self.po2
                            m2.acquire(t)
                            for db in range(8):
                                self.st.inc(t.matmul(m2.b[0:1, 0:wch], lglb[:, db, 0:1],
                                                     w1c[ia if which == 0 else ig][:, db, 0:wch],
                                                     start=(db == 0), stop=(db == 7), skip_group_check=True))
                        self.st.wait(v)
                        self.sv.incw(v, v.tensor_copy(out=lgw1[0:1, a0:a0 + wch], in_=self.po.b[0:1, 0:wch]))
                        self.sv.incw(v, v.tensor_copy(out=lgw1[0:1, HID + a0:HID + a0 + wch],
                                                      in_=self.po2.b[0:1, 0:wch]))
                        self.po.release(self.sv)
                        self.po2.release(self.sv)
                        for which in range(2):
                            m2 = self.po if which == 0 else self.po2
                            m2.acquire(t)
                            for db in range(8):
                                self.st.inc(t.matmul(m2.b[0:1, 0:wch], lglb[:, db, 1:2],
                                                     w1c[ia if which == 0 else ig][:, db, 0:wch],
                                                     start=(db == 0), stop=(db == 7), skip_group_check=True))
                        self.st.wait(v)
                        self.sv.incw(v, v.tensor_tensor(out=lb1[0:1, a0:a0 + wch], in0=self.po.b[0:1, 0:wch],
                                                        in1=lb1[0:1, a0:a0 + wch], op=ALU.add))
                        self.sv.incw(v, v.tensor_tensor(out=lb1[0:1, HID + a0:HID + a0 + wch],
                                                        in0=self.po2.b[0:1, 0:wch],
                                                        in1=lb1[0:1, HID + a0:HID + a0 + wch], op=ALU.add))
                        self.po.release(self.sv)
                        self.po2.release(self.sv)
                        rows_done[p_] = True
                    for ti, T in enumerate(tiles):
                        it = (2 * p_ + ti) % 3

                        def hsl2(db, T=T):
                            off = 1 if db < 4 else 0
                            return self.XT[:, db, off + 128 * T: off + 128 * T + 128]
                        self.po.acquire(t)
                        self.po2.acquire(t)
                        t.wait_ge(self.sd.h, ld_n)
                        self.sv.wait(t)
                        self.ss.wait(t)
                        for which, mmx in ((0, self.po), (1, self.po2)):
                            c0 = a0 if which == 0 else HID + a0
                            for db in range(8):
                                self.st.inc(t.matmul(mmx.b[:, 0:wch], hsl2(db),
                                                     w1c[ia if which == 0 else ig][:, db, 0:wch],
                                                     start=(db == 0), stop=False, skip_group_check=True))
                            self.st.inc(t.matmul(mmx.b[:, 0:wch], rows_nm[0:1, 128 * T:128 * T + 128],
                                                 lgw1[0:1, c0:c0 + wch], start=False, stop=False,
                                                 skip_group_check=True))
                            self.st.inc(t.matmul(mmx.b[:, 0:wch], rows_s[0:1, 128 * T:128 * T + 128],
                                                 lb1[0:1, c0:c0 + wch], start=False, stop=True,
                                                 skip_group_check=True))
                        st_w1 = self.st.n
                        if len(pendq) > 1:
                            self.emit_w2_part(*pendq.popleft(), dacc, w2, hTb, hT_s)
                        gel_s[it].acquire(s)
                        s.wait_ge(self.st.h, st_w1)
                        self.ss.incx(s, s.activation(out=gel[:, it, 0:wch], in_=self.po2.b[:, 0:wch],
                                                     func=AF.Gelu, bias=self.zero1[:],
                                                     scale=rstd_all[:, T:T + 1]))
                        self.po2.release(self.ss)
                        hid_s[it].acquire(v)
                        self.ss.wait(v)
                        self.sv.incx(v, v.scalar_tensor_tensor(out=hidc[it][:, 0:wch], in0=self.po.b[:, 0:wch],
                                                               scalar=rstd_all[:, T:T + 1], in1=gel[:, it, 0:wch],
                                                               op0=ALU.mult, op1=ALU.mult))
                        self.po.release(self.sv)
                        gel_s[it].release(self.sv)
                        pendq.append((it, p_, ti, wch))
                    w1_s[ia].release(self.st)
                    w1_s[ig].release(self.st)
                while pendq:
                    self.emit_w2_part(*pendq.popleft(), dacc, w2, hTb, hT_s)
                for ti, T in enumerate(tiles):
                    for oc in range(2):
                        dm = dacc[2 * ti + oc]
                        self.st.inc(t.matmul(dm.b[:], self.ones_bf[:], b2r[0:1, 512 * oc:512 * oc + 512],
                                             start=False, stop=True, skip_group_check=True))
                        self.st.wait(v)
                        self.sv.incw(v, v.tensor_tensor(out=self.X[:, T, 512 * oc:512 * oc + 512],
                                                        in0=dm.b[:], in1=self.X[:, T, 512 * oc:512 * oc + 512],
                                                        op=ALU.add))
                        dm.release(self.sv)

    def stage_C(self, b):
        nc, t, v, s, g, y = self.nc, self.t, self.v, self.s, self.g, self.y
        with ExitStack() as c:
            E = c.enter_context
            x1t = E(nc.sbuf_tensor(self.nt("c_x1"), [128, 4, D], F32))
            self.bn = E(nc.sbuf_tensor(self.nt("c_bn"), [128, 24], F32))
            x1_s = Slot(x1t)
            self.sv.wait(y)
            self.ss.wait(y)
            self.st.wait(y)
            for tc in range(4):
                x1_s.acquire(y)
                self.sy_.inc(y.dma_start(out=x1t[:], in_=self.x1_spill.ap()[:, 4 * tc:4 * tc + 4, :]), 16)
                self.sy_.wait(v)
                for tt in range(4):
                    T = 4 * tc + tt
                    self.ss.wait(v)
                    self.sv.incw(v, v.tensor_tensor(out=self.X[:, T, :], in0=self.X[:, T, :],
                                                in1=x1t[:, tt, :], op=ALU.add))
                    self.ln_apply_tile(T)
                x1_s.release(self.sv)

    def stage_D(self, b):
        nc, t, v, s, g, y = self.nc, self.t, self.v, self.s, self.g, self.y
        with ExitStack() as c:
            E = c.enter_context
            self.bn = E(nc.sbuf_tensor(self.nt("d_bn"), [128, 24], F32))
            self.xbf_s = Slot(E(nc.sbuf_tensor(self.nt("d_xbf"), [128, D], BF16)))
            qT_pad = E(nc.sbuf_tensor(self.nt("d_qT"), [128, 4, SEQ], BF16))
            kT_pad = E(nc.sbuf_tensor(self.nt("d_kT"), [128, 4, SEQ], BF16))
            v65 = E(nc.sbuf_tensor(self.nt("d_v65"), [128, NT, 16, 65], BF16))
            zz = E(nc.sbuf_tensor(self.nt("d_zz"), [128, 2], F32))
            eu = E(nc.sbuf_tensor(self.nt("d_eu"), [128, D], F32))
            eu_s = Slot(eu)
            qTp_s, kTp_s, v65_s = Slot(qT_pad), Slot(kT_pad), Slot(v65)

            self.st.wait(g)
            self.ss.wait(g)
            self.sv.wait(g)
            with ExitStack() as dproj:
                E1 = dproj.enter_context
                gb = E1(nc.sbuf_tensor(self.nt("d_gb"), [128, 4, 16], F32))
                gbrow = E1(nc.sbuf_tensor(self.nt("d_gbrow"), [1, 128], F32))
                qf = E1(nc.sbuf_tensor(self.nt("d_qf"), [128, 256], F32))
                q2 = E1(nc.sbuf_tensor(self.nt("d_q2"), [128, 256], F32))
                fmst = E1(nc.sbuf_tensor(self.nt("d_fmst"), [128, 4, 16], F32))
                qpad = E1(nc.sbuf_tensor(self.nt("d_qpad"), [128, 16, 32], BF16))
                wq_sb = E1(nc.sbuf_tensor(self.nt("d_wq"), [128, 8, 256], BF16))
                wk_sb = E1(nc.sbuf_tensor(self.nt("d_wk"), [128, 8, 256], BF16))
                wv_sb = E1(nc.sbuf_tensor(self.nt("d_wv"), [128, 8, D], BF16))
                self.sd.inc(g.dma_start(out=wq_sb[:], in_=self.w1_qT), 16)
                self.sd.inc(g.dma_start(out=wk_sb[:], in_=self.w1_kT), 16)
                self.sd.inc(g.dma_start(out=wv_sb[:], in_=self.w1_vT), 16)
                self.sd.inc(g.dma_start(out=gbrow[0:1, 0:16], in_=self.w1_g), 16)
                self.sd.inc(g.dma_start(out=gbrow[0:1, 16:32], in_=self.w1_b), 16)
                m = self.next_mm()
                m.acquire(t)
                t.wait_ge(self.sd.h, self.sd.n)
                self.st.inc(t.matmul(m.b[:, 0:32], self.ones_f[:], gbrow[0:1, 0:32], start=True, stop=True,
                                     skip_group_check=True))
                self.st.wait(v)
                self.sv.incw(v, v.tensor_scalar(out=gb[:, 0, :], in0=m.b[:, 0:16], scalar1=0.25, scalar2=None,
                                            op0=ALU.mult, op1=ALU.bypass))
                self.sv.incw(v, v.tensor_scalar(out=gb[:, 1, :], in0=m.b[:, 16:32], scalar1=0.25, scalar2=None,
                                            op0=ALU.mult, op1=ALU.bypass))
                self.sv.incw(v, v.tensor_copy(out=gb[:, 2, :], in_=m.b[:, 0:16]))
                self.sv.incw(v, v.tensor_copy(out=gb[:, 3, :], in_=m.b[:, 16:32]))
                m.release(self.sv)
                self.sv.incw(v, v.memset(qpad[:], 0.0))
                self.sv.incw(v, v.memset(v65[:], 1.0))

                qf_s, q2_s, qp_s = Slot(qf), Slot(q2), Slot(qpad)

                for T in range(NT):
                    self.lazy_xt(T)
                    for which in range(2):
                        wsb = wq_sb if which == 0 else wk_sb
                        m = self.next_mm()
                        m.acquire(t)
                        self.sv.wait(t)
                        for db in range(8):
                            self.st.inc(t.matmul(m.b[:, 0:256], self.XT[:, db, 1 + 128 * T:1 + 128 * T + 128],
                                                 wsb[:, db, :], start=(db == 0), stop=(db == 7),
                                                 skip_group_check=True))
                        qf_s.acquire(v)
                        self.st.wait(v)
                        self.sv.incw(v, v.tensor_copy(out=qf[:], in_=m.b[:, 0:256]))
                        m.release(self.sv)
                        q3 = qf[:].rearrange("p (h f) -> p h f", h=16)
                        q2_s.acquire(s)
                        self.sv.wait(s)
                        self.ss.incw(s, s.square(out=q2[:], in_=qf[:]))
                        q23 = q2[:].rearrange("p (h f) -> p h f", h=16)
                        self.ss.wait(v)
                        self.sv.incw(v, v.reduce_sum(out=fmst[:, 0, :], in_=q3, axis=AX.X))
                        self.sv.incw(v, v.reduce_sum(out=fmst[:, 1, :], in_=q23, axis=AX.X))
                        q2_s.release(self.sv)
                        self.sv.incw(v, v.tensor_scalar(out=fmst[:, 0, :], in0=fmst[:, 0, :], scalar1=1.0 / 16,
                                                    scalar2=None, op0=ALU.mult, op1=ALU.bypass))
                        self.sv.incw(v, v.tensor_scalar(out=fmst[:, 1, :], in0=fmst[:, 1, :], scalar1=1.0 / 16,
                                                    scalar2=None, op0=ALU.mult, op1=ALU.bypass))
                        self.sv.incw(v, v.tensor_tensor(out=fmst[:, 2, :], in0=fmst[:, 0, :], in1=fmst[:, 0, :],
                                                    op=ALU.mult))
                        self.sv.incw(v, v.tensor_tensor(out=fmst[:, 2, :], in0=fmst[:, 1, :], in1=fmst[:, 2, :],
                                                    op=ALU.subtract))
                        self.powx(fmst[:, 3, :], fmst[:, 2, :], self.eps1[:], -0.5)
                        mu_b = bcast_last(fmst[:, 0, :].rearrange("p (h a) -> p h a", a=1), 16)
                        rs_b = bcast_last(fmst[:, 3, :].rearrange("p (h a) -> p h a", a=1), 16)
                        g_b = bcast_mid(gb[:, 0 if which == 0 else 2, :], 16)
                        b_b = bcast_mid(gb[:, 1 if which == 0 else 3, :], 16)
                        self.ss.wait(v)
                        self.sv.incw(v, v.tensor_tensor(out=q3, in0=q3, in1=mu_b, op=ALU.subtract))
                        self.sv.incw(v, v.tensor_tensor(out=q3, in0=q3, in1=rs_b, op=ALU.mult))
                        self.sv.incw(v, v.tensor_tensor(out=q3, in0=q3, in1=g_b, op=ALU.mult))
                        qp_s.acquire(v)
                        self.sv.incw(v, v.tensor_tensor(out=qpad[:, :, 0:16], in0=q3, in1=b_b, op=ALU.add))
                        qf_s.release(self.sv)
                        qpf = qpad[:].rearrange("p h f -> p (h f)")
                        dstT = qT_pad if which == 0 else kT_pad
                        (qTp_s if which == 0 else kTp_s).acquire(v)
                        (qTp_s if which == 0 else kTp_s).acquire(s)
                        self.sv.wait(t)
                        self.transpose_to([qpf[:, 128 * cc:128 * cc + 128] for cc in range(4)],
                                          [dstT[:, cc, 128 * T:128 * T + 128] for cc in range(4)],
                                          self.ident_bf[:])
                        qp_s.release(self.st)
                    v65_s.acquire(v)
                    for vc in range(2):
                        m = self.next_mm()
                        m.acquire(t)
                        self.sv.wait(t)
                        for db in range(8):
                            self.st.inc(t.matmul(m.b[:], self.XT[:, db, 1 + 128 * T:1 + 128 * T + 128],
                                                 wv_sb[:, db, 512 * vc:512 * vc + 512],
                                                 start=(db == 0), stop=(db == 7), skip_group_check=True))
                        self.st.wait(v)
                        dst = v65[:, T, 8 * vc:8 * vc + 8, 0:64]
                        self.sv.incw(v, v.tensor_copy(out=dst, in_=m.b[:].rearrange("p (h f) -> p h f", h=8)))
                        m.release(self.sv)

            s2T = E(nc.sbuf_tensor(self.nt("d_s2T"), [128, 3, 512], BF16))
            triu = E(nc.sbuf_tensor(self.nt("d_triu"), [128, 128], BF16))
            onorm = E(nc.sbuf_tensor(self.nt("d_on"), [128, 4, 16, 64], BF16))
            oev = E(nc.sbuf_tensor(self.nt("d_oev"), [128, 4, 65], F32))
            rz4 = E(nc.sbuf_tensor(self.nt("d_rz4"), [128, 4, 1], F32))
            oTd = E(nc.sbuf_tensor(self.nt("d_oT"), [128, 8, 128], BF16))
            wo_sb = E(nc.sbuf_tensor(self.nt("d_wo"), [128, 8, D], BF16))
            s2_s = [Slot(None), Slot(None), Slot(None)]
            on_s, oT_s, oev_s = Slot(onorm), Slot(oTd), Slot(oev)
            self.st.wait(g)
            self.sd.inc(g.dma_start(out=triu[:], in_=self.c_triu), 16)
            self.sd.inc(g.dma_start(out=wo_sb[:], in_=self.w1_oT), 16)
            self.sd.wait(t)
            for qc in range(4):
                on_s.acquire(v)
                self.sv.wait(t)
                self.ss.wait(t)
                for h in range(16):
                    bp, pb = 32 * (h % 4), h // 4
                    nkt = 4 * qc + 4
                    opo = [self.mm[j] for j in range(4)]
                    for j in range(4):
                        opo[j].acquire(t)
                    def omm_d(kt, was_diag, used_v=False):
                        sb_i = kt % 3
                        if was_diag or used_v:
                            self.sv.wait(t)
                        else:
                            self.ss.wait(t)
                        for j in range(4):
                            qtile = 4 * qc + j
                            if kt > qtile:
                                continue
                            self.st.inc(t.matmul(opo[j].b[:, 0:65], s2T[:, sb_i, 128 * j:128 * j + 128],
                                                 v65[:, kt, h, :], start=(kt == 0), stop=(kt == qtile),
                                                 skip_group_check=True))
                        s2_s[sb_i].release(self.st)

                    from collections import deque
                    pq = deque()
                    for kt in range(nkt):
                        sps = self.po if kt % 2 == 0 else self.po2
                        sps.acquire(t)
                        self.st.inc(t.matmul(sps.b[:], kT_pad[bp:bp + 32, pb, 128 * kt:128 * kt + 128],
                                             qT_pad[bp:bp + 32, pb, 512 * qc:512 * qc + 512],
                                             start=True, stop=True, tile_position=(bp, 0),
                                             skip_group_check=True))
                        sb_i = kt % 3
                        jd = kt - 4 * qc
                        is_diag = 0 <= jd < 4
                        c0 = 128 * jd if is_diag else 0
                        s2_s[sb_i].acquire(s)
                        self.st.wait(s)
                        self.ss.incx(s, s.square(out=s2T[:, sb_i, c0:512], in_=sps.b[:, c0:512]))
                        sps.release(self.ss)
                        if is_diag:
                            self.ss.wait(v)
                            self.sv.incx(v, v.tensor_tensor(out=s2T[:, sb_i, 128 * jd:128 * jd + 128],
                                                            in0=s2T[:, sb_i, 128 * jd:128 * jd + 128],
                                                            in1=triu[:], op=ALU.mult))
                        pq.append((kt, is_diag, False))
                        if len(pq) > 2:
                            omm_d(*pq.popleft())
                    while pq:
                        omm_d(*pq.popleft())
                    oev_s.acquire(v)
                    self.st.wait(v)
                    for j in range(4):
                        if j < 3:
                            self.sv.incx(v, v.tensor_copy(out=oev[:, j, :], in_=opo[j].b[:, 0:65]))
                        else:
                            self.sv.incw(v, v.tensor_copy(out=oev[:, j, :], in_=opo[j].b[:, 0:65]))
                        opo[j].release(self.sv)
                    self.sv.incw(v, v.tensor_scalar(out=rz4[:], in0=oev[:, :, 64:65], scalar1=1e-5,
                                                    scalar2=None, op0=ALU.add, op1=ALU.bypass))
                    self.sv.incw(v, v.reciprocal(out=rz4[:], in_=rz4[:]))
                    self.sv.incw(v, v.tensor_tensor(out=onorm[:, :, h, :], in0=oev[:, :, 0:64],
                                                    in1=bcast_last(rz4[:], 64), op=ALU.mult))
                    oev_s.release(self.sv)
                for j in range(4):
                    T = 4 * qc + j
                    onf = onorm[:, j].rearrange("p h f -> p (h f)")
                    oT_s.acquire(v)
                    oT_s.acquire(s)
                    self.sv.wait(t)
                    self.transpose_to([onf[:, 128 * cc:128 * cc + 128] for cc in range(8)],
                                      [oTd[:, cc, :] for cc in range(8)], self.ident_bf[:])
                    for oc in range(2):
                        m = self.next_mm()
                        m.acquire(t)
                        self.sv.wait(t)
                        self.ss.wait(t)
                        for fb in range(8):
                            self.st.inc(t.matmul(m.b[:], oTd[:, fb, :], wo_sb[:, fb, 512 * oc:512 * oc + 512],
                                                 start=(fb == 0), stop=(fb == 7), skip_group_check=True))
                        self.st.wait(v)
                        self.sv.incw(v, v.tensor_tensor(out=self.X[:, T, 512 * oc:512 * oc + 512],
                                                        in0=m.b[:], in1=self.X[:, T, 512 * oc:512 * oc + 512],
                                                        op=ALU.add))
                        m.release(self.sv)
                    oT_s.release(self.st)
                    self.ln_apply_tile(T)
                on_s.release(self.st)

    def stage_D_eprep(self, b):
        nc, t, v, s, g, y = self.nc, self.t, self.v, self.s, self.g, self.y
        with ExitStack() as c:
            E = c.enter_context
            self.bn = E(nc.sbuf_tensor(self.nt("de_bn"), [128, 24], F32))
            gbc = E(nc.sbuf_tensor(self.nt("de_gbc"), [128, 2, D], F32))
            eu = E(nc.sbuf_tensor(self.nt("de_eu"), [128, D], F32))
            zz = E(nc.sbuf_tensor(self.nt("de_zz"), [128, 2], F32))
            self.sv.wait(g)
            self.ss.wait(g)
            self.st.wait(g)
            self.build_gbc(gbc)
            eu_s = Slot(eu)
            bn = self.bn
            for T in range(NT):
                self.ss.wait(v)
                self.sv.incw(v, v.bn_stats(bn[:, 0:6], self.X[:, T, 0:512]))
                self.sv.incw(v, v.bn_stats(bn[:, 6:12], self.X[:, T, 512:1024]))
                self.sv.incw(v, v.bn_aggr(bn[:, 12:14], bn[:, 0:6]))
                self.sv.incw(v, v.bn_aggr(bn[:, 14:16], bn[:, 6:12]))
                self.ln_combine(bn)
                rstd3 = self.stats_e[:, b, T, 0:1]
                nmr3 = self.stats_e[:, b, T, 1:2]
                self.powx(rstd3, bn[:, 17:18], self.eps1[:], -0.5)
                self.ss.wait(v)
                self.sv.incw(v, v.tensor_tensor(out=nmr3, in0=bn[:, 16:17], in1=rstd3, op=ALU.mult))
                self.sv.incw(v, v.tensor_scalar(out=nmr3, in0=nmr3, scalar1=-1.0, scalar2=None,
                                            op0=ALU.mult, op1=ALU.bypass))
                eu_s.acquire(s)
                self.sv.wait(s)
                self.ss.incw(s, s.activation(out=eu[:], in_=self.X[:, T, :], func=AF.Identity,
                                         bias=nmr3, scale=rstd3))
                self.ss.wait(v)
                self.sv.incw(v, v.tensor_tensor(out=eu[:], in0=eu[:], in1=gbc[:, 0, :], op=ALU.mult))
                self.sv.incw(v, v.tensor_tensor(out=eu[:], in0=eu[:], in1=gbc[:, 1, :], op=ALU.add))
                self.sv.incw(v, v.reduce_max(out=zz[:, 0:1], in_=eu[:], axis=AX.X, apply_absolute_value=True))
                eu_s.release(self.sv)
                self.sv.incw(v, v.tensor_tensor(out=self.gmax[:], in0=self.gmax[:], in1=zz[:, 0:1], op=ALU.max))

    def stage_E_all(self):
        nc, t, v, s, g, y = self.nc, self.t, self.v, self.s, self.g, self.y
        with ExitStack() as c:
            E = c.enter_context
            self.bn = E(nc.sbuf_tensor(self.nt("e_bn"), [128, 24], F32))
            wtf = E(nc.sbuf_tensor(self.nt("e_wtf"), [128, 8, D], F32))
            absc = E(nc.sbuf_tensor(self.nt("e_absc"), [128, D], F32))
            maskc = E(nc.sbuf_tensor(self.nt("e_maskc"), [128, D], F32))
            sgn = E(nc.sbuf_tensor(self.nt("e_sgn"), [128, D], F32))
            sums = E(nc.sbuf_tensor(self.nt("e_sums"), [128, 8], F32))
            sm2 = E(nc.sbuf_tensor(self.nt("e_sm2"), [128, 2], F32))
            smt = E(nc.sbuf_tensor(self.nt("e_smt"), [2, 1], F32))
            al = E(nc.sbuf_tensor(self.nt("e_al"), [1, 4], F32))
            alb = E(nc.sbuf_tensor(self.nt("e_alb"), [128, 2], F32))
            u = E(nc.sbuf_tensor(self.nt("e_u"), [128, D], F32))
            xq = E(nc.sbuf_tensor(self.nt("e_xq"), [128, D], BF16))
            xqT = E(nc.sbuf_tensor(self.nt("e_xqT"), [128, 8, 128], BF16))
            arr = E(nc.sbuf_tensor(self.nt("e_arr"), [1, 128], F32))
            tmpc = E(nc.sbuf_tensor(self.nt("e_tmpc"), [128, 1], F32))
            self.wq_ns = E(nc.sbuf_tensor(self.nt("wq_ns"), [128, 8, D], BF16))

            self.st.wait(g)
            self.ss.wait(g)
            self.sv.wait(g)
            self.gbc = E(nc.sbuf_tensor(self.nt("e_gbc"), [128, 2, D], F32))
            self.build_gbc(self.gbc)
            self.sd.inc(g.dma_start(out=wtf[:], in_=self.we_wT), 16)
            self.sd.wait(s)
            self.sd.wait(v)

            for db in range(8):
                self.sv.wait(s)
                self.ss.incw(s, s.activation(out=absc[:], in_=wtf[:, db, :], func=AF.Abs,
                                         bias=self.zero1[:], scale=1.0))
                self.ss.wait(v)
                self.sv.incw(v, v.reduce_sum(out=sums[:, db:db + 1], in_=absc[:], axis=AX.X))
            self.sv.incw(v, v.reduce_sum(out=sm2[:, 0:1], in_=sums[:], axis=AX.X))
            m = self.next_mm()
            m.acquire(t)
            self.sv.wait(t)
            self.st.inc(t.transpose(m.b[0:1, 0:128], sm2[:, 0:1], self.ident_f[:]))
            self.st.wait(v)
            self.sv.incw(v, v.reduce_sum(out=al[0:1, 0:1], in_=m.b[0:1, 0:128], axis=AX.X))
            m.release(self.sv)
            self.sv.incw(v, v.tensor_scalar(out=al[0:1, 0:1], in0=al[0:1, 0:1],
                                        scalar1=0.7 / (1024.0 * 1024.0), scalar2=None,
                                        op0=ALU.mult, op1=ALU.bypass))
            m = self.next_mm()
            m.acquire(t)
            self.sv.wait(t)
            self.st.inc(t.matmul(m.b[:, 0:1], self.ones_f[:], al[0:1, 0:1], start=True, stop=True,
                                 skip_group_check=True))
            self.st.wait(v)
            self.sv.incw(v, v.tensor_copy(out=alb[:, 0:1], in_=m.b[:, 0:1]))
            m.release(self.sv)
            self.sv.incw(v, v.memset(sm2[:], 0.0))
            for db in range(8):
                self.sv.wait(s)
                self.ss.incw(s, s.activation(out=absc[:], in_=wtf[:, db, :], func=AF.Abs,
                                         bias=self.zero1[:], scale=1.0))
                self.ss.wait(v)
                self.sv.incw(v, v.tensor_scalar(out=maskc[:], in0=absc[:], scalar1=alb[:, 0:1], scalar2=None,
                                            op0=ALU.is_gt, op1=ALU.bypass))
                self.sv.incw(v, v.scalar_tensor_tensor(out=absc[:], in0=absc[:], scalar=1.0, in1=maskc[:],
                                                   op0=ALU.mult, op1=ALU.mult, accum_out=tmpc[:]))
                self.sv.incw(v, v.tensor_tensor(out=sm2[:, 0:1], in0=sm2[:, 0:1], in1=tmpc[:], op=ALU.add))
                self.sv.incw(v, v.tensor_scalar(out=maskc[:], in0=maskc[:], scalar1=1.0, scalar2=0.0,
                                            op0=ALU.mult, op1=ALU.add, accum_out=tmpc[:]))
                self.sv.incw(v, v.tensor_tensor(out=sm2[:, 1:2], in0=sm2[:, 1:2], in1=tmpc[:], op=ALU.add))
            m = self.next_mm()
            m.acquire(t)
            self.sv.wait(t)
            self.st.inc(t.transpose(m.b[0:2, 0:128], sm2[:], self.ident_f[:]))
            self.st.wait(v)
            self.sv.incw(v, v.reduce_sum(out=smt[0:2, 0:1], in_=m.b[0:2, 0:128], axis=AX.X))
            m.release(self.sv)
            m = self.next_mm()
            m.acquire(t)
            self.sv.wait(t)
            self.st.inc(t.transpose(m.b[0:1, 0:2], smt[0:2, 0:1], self.ident_f[0:2, 0:2]))
            self.st.wait(v)
            self.sv.incw(v, v.tensor_scalar(out=al[0:1, 1:2], in0=m.b[0:1, 1:2], scalar1=1.0,
                                        scalar2=None, op0=ALU.max, op1=ALU.bypass))
            self.sv.incw(v, v.reciprocal(out=al[0:1, 1:2], in_=al[0:1, 1:2]))
            self.sv.incw(v, v.tensor_tensor(out=al[0:1, 2:3], in0=m.b[0:1, 0:1], in1=al[0:1, 1:2],
                                        op=ALU.mult))
            m.release(self.sv)
            m = self.next_mm()
            m.acquire(t)
            self.sv.wait(t)
            self.st.inc(t.matmul(m.b[:, 0:1], self.ones_f[:], al[0:1, 2:3], start=True, stop=True,
                                 skip_group_check=True))
            self.st.wait(v)
            self.sv.incw(v, v.tensor_copy(out=alb[:, 1:2], in_=m.b[:, 0:1]))
            m.release(self.sv)
            for db in range(8):
                self.sv.wait(s)
                self.ss.incw(s, s.activation(out=absc[:], in_=wtf[:, db, :], func=AF.Abs,
                                         bias=self.zero1[:], scale=1.0))
                self.ss.incw(s, s.sign(out=sgn[:], in_=wtf[:, db, :]))
                self.ss.wait(v)
                self.sv.incw(v, v.tensor_scalar(out=maskc[:], in0=absc[:], scalar1=alb[:, 0:1], scalar2=None,
                                            op0=ALU.is_gt, op1=ALU.bypass))
                self.sv.incw(v, v.tensor_tensor(out=sgn[:], in0=sgn[:], in1=maskc[:], op=ALU.mult))
                self.sv.incw(v, v.tensor_scalar(out=self.wq_ns[:, db, :], in0=sgn[:], scalar1=alb[:, 1:2],
                                            scalar2=None, op0=ALU.mult, op1=ALU.bypass))

            m = self.next_mm()
            m.acquire(t)
            self.sv.wait(t)
            self.st.inc(t.transpose(m.b[0:1, 0:128], self.gmax[:], self.ident_f[:]))
            self.st.wait(v)
            self.sv.incw(v, v.memset(arr[:], 0.0))
            self.sv.incw(v, v.reduce_max(out=arr[0:1, 0:1], in_=m.b[0:1, 0:128], axis=AX.X))
            m.release(self.sv)
            self.sv.wait(g)
            self.sd.inc(g.dma_start(out=self.ar_in.ap(), in_=arr[:]), 16)
            self.sd.wait(g)
            self.scc.inc(g.collective_compute("AllReduce", ALU.max, replica_groups=[list(range(self.cc_world))],
                                              ins=[self.ar_in.ap().opt()], outs=[self.ar_out.ap().opt()]))
            self.scc.wait(g)
            self.sd.inc(g.dma_start(out=arr[0:1, 0:1], in_=self.ar_out.ap()[0:1, 0:1]), 16)
            self.sd.wait(v)
            self.sv.incw(v, v.tensor_scalar(out=al[0:1, 3:4], in0=arr[0:1, 0:1], scalar1=1e-8,
                                        scalar2=1.0 / 127.0, op0=ALU.max, op1=ALU.mult))
            m = self.next_mm()
            m.acquire(t)
            self.sv.wait(t)
            self.st.inc(t.matmul(m.b[:, 0:1], self.ones_f[:], al[0:1, 3:4], start=True, stop=True,
                                 skip_group_check=True))
            self.st.wait(v)
            self.sv.incw(v, v.tensor_copy(out=self.scale_b[:], in_=m.b[:, 0:1]))
            m.release(self.sv)
            self.sv.incw(v, v.reciprocal(out=self.iscale_b[:], in_=self.scale_b[:]))

            u_s, xq_s, xqT_s = Slot(u), Slot(xq), Slot(xqT)
            for b in range(NB - 1, -1, -1):
                if b != NB - 1:
                    self.sv.wait(y)
                    self.ss.wait(y)
                    self.st.wait(y)
                    for tc in range(4):
                        self.sy_.inc(y.dma_start(out=self.X[:, 4 * tc:4 * tc + 4, :],
                                                 in_=self.x3_spill.ap()[:, 4 * tc:4 * tc + 4, :]), 16)
                    self.sy_.wait(v)
                    self.sy_.wait(s)
                for T in range(NT):
                    st_ap = self.stats_e[:, b, T, :]
                    u_s.acquire(s)
                    self.sv.wait(s)
                    self.ss.incw(s, s.activation(out=u[:], in_=self.X[:, T, :], func=AF.Identity,
                                             bias=st_ap[:, 1:2], scale=st_ap[:, 0:1]))
                    self.ss.wait(v)
                    self.sv.incw(v, v.tensor_tensor(out=u[:], in0=u[:], in1=self.gbc[:, 0, :], op=ALU.mult))
                    self.sv.incw(v, v.tensor_tensor(out=u[:], in0=u[:], in1=self.gbc[:, 1, :], op=ALU.add))
                    self.sv.incw(v, v.tensor_scalar(out=u[:], in0=u[:], scalar1=self.iscale_b[:, 0:1],
                                                scalar2=None, op0=ALU.mult, op1=ALU.bypass))
                    self.sv.incw(v, v.tensor_scalar(out=u[:], in0=u[:], scalar1=127.0, scalar2=-127.0,
                                                op0=ALU.min, op1=ALU.max))
                    xq_s.acquire(s)
                    self.sv.wait(s)
                    self.ss.incw(s, s.activation(out=u[:], in_=u[:], func=AF.Identity,
                                             bias=self.magic[:], scale=1.0))
                    self.ss.incw(s, s.activation(out=xq[:], in_=u[:], func=AF.Identity,
                                             bias=self.nmagic[:], scale=1.0))
                    u_s.release(self.ss)
                    xqT_s.acquire(v)
                    xqT_s.acquire(s)
                    self.ss.wait(t)
                    self.transpose_to([xq[:, 128 * db:128 * db + 128] for db in range(8)],
                                      [xqT[:, db, :] for db in range(8)], self.ident_bf[:])
                    xq_s.release(self.st)
                    for oc in range(2):
                        m = self.next_mm()
                        m.acquire(t)
                        self.sv.wait(t)
                        self.ss.wait(t)
                        for db in range(8):
                            self.st.inc(t.matmul(m.b[:], xqT[:, db, :],
                                                 self.wq_ns[:, db, 512 * oc:512 * oc + 512],
                                                 start=(db == 0), stop=(db == 7), skip_group_check=True))
                        self.st.wait(v)
                        self.sv.incw(v, v.scalar_tensor_tensor(out=self.X[:, T, 512 * oc:512 * oc + 512],
                                                           in0=m.b[:], scalar=self.scale_b[:, 0:1],
                                                           in1=self.X[:, T, 512 * oc:512 * oc + 512],
                                                           op0=ALU.mult, op1=ALU.add))
                        m.release(self.sv)
                    xqT_s.release(self.st)
                    self.ln_apply_tile(T)
                    self.ss.wait(y)
                    self.sy_.inc(y.dma_start(out=self.out[b, :, T, :], in_=self.X[:, T, :]), 16)


def _host_prep(inputs):
    f32 = np.float32
    x = np.ascontiguousarray(inputs["L0_attn_x"], dtype=f32)

    def tokmaj(xb):
        return np.ascontiguousarray(xb.reshape(NT, 128, D).transpose(1, 0, 2))

    def feamaj(xb):
        return np.ascontiguousarray(xb.T.reshape(8, 128, SEQ).transpose(1, 0, 2))

    sh = {}
    sh["wa_qkvT"] = np.ascontiguousarray(
        np.asarray(inputs["L0_attn_Wqkv"], f32).T.reshape(8, 128, 3072).transpose(1, 0, 2))
    sh["wa_outT"] = np.ascontiguousarray(
        np.asarray(inputs["L0_attn_Wout"], f32).T.reshape(8, 128, D).transpose(1, 0, 2))
    bqkv = np.asarray(inputs["L0_attn_bqkv"], f32)
    sh["ba_qkv"] = np.ascontiguousarray(bqkv[:2048].reshape(16, 128).T)
    sh["ba_bv"] = np.ascontiguousarray(bqkv[2048:].reshape(8, 128).T)
    sh["ba_bout"] = np.asarray(inputs["L0_attn_bout"], f32)[None, :]
    sh["wf_qkv"] = np.ascontiguousarray(
        np.asarray(inputs["L0_ffn_wqkv"], f32).reshape(2, 8, 128, 1536).transpose(0, 2, 1, 3))
    sh["wf_wo"] = np.ascontiguousarray(
        np.asarray(inputs["L0_ffn_wo"], f32).reshape(2, 4, 128, D).transpose(0, 2, 1, 3))
    sh["wf_w1"] = np.ascontiguousarray(
        np.asarray(inputs["L0_ffn_w1"], f32).reshape(2, 8, 128, HID2).transpose(0, 2, 1, 3))
    w2p = np.zeros((2, 22 * 128, D), dtype=f32)
    w2p[:, :HID, :] = np.asarray(inputs["L0_ffn_w2"], f32)
    sh["wf_w2"] = np.ascontiguousarray(w2p.reshape(2, 22, 128, D).transpose(0, 2, 1, 3))
    sh["wf_qs"] = np.ascontiguousarray(np.asarray(inputs["L0_ffn_q_scale"], f32)[None])
    sh["wf_ks"] = np.ascontiguousarray(np.asarray(inputs["L0_ffn_k_scale"], f32)[None])
    sh["wf_lng"] = np.ascontiguousarray(
        np.asarray(inputs["L0_ffn_ln_g"], f32).reshape(2, 8, 128).transpose(2, 0, 1))
    sh["wf_lnb"] = np.ascontiguousarray(
        np.asarray(inputs["L0_ffn_ln_b"], f32).reshape(2, 8, 128).transpose(2, 0, 1))
    sh["wf_b1"] = np.ascontiguousarray(np.asarray(inputs["L0_ffn_b1"], f32)[None])
    sh["wf_b2"] = np.ascontiguousarray(np.asarray(inputs["L0_ffn_b2"], f32)[None])
    sh["w1_qT"] = np.ascontiguousarray(
        np.asarray(inputs["L1_attn_Wq"], f32).T.reshape(8, 128, 256).transpose(1, 0, 2))
    sh["w1_kT"] = np.ascontiguousarray(
        np.asarray(inputs["L1_attn_Wk"], f32).T.reshape(8, 128, 256).transpose(1, 0, 2))
    sh["w1_vT"] = np.ascontiguousarray(
        np.asarray(inputs["L1_attn_Wv"], f32).T.reshape(8, 128, D).transpose(1, 0, 2))
    sh["w1_oT"] = np.ascontiguousarray(
        np.asarray(inputs["L1_attn_Wo"], f32).T.reshape(8, 128, D).transpose(1, 0, 2))
    sh["w1_g"] = np.asarray(inputs["L1_attn_gamma"], f32)[None]
    sh["w1_b"] = np.asarray(inputs["L1_attn_beta"], f32)[None]
    sh["we_wT"] = np.ascontiguousarray(
        np.asarray(inputs["L1_ffn_weight"], f32).T.reshape(8, 128, D).transpose(1, 0, 2))
    sh["we_g"] = np.asarray(inputs["L1_ffn_gamma"], f32)[None]
    sh["we_b"] = np.asarray(inputs["L1_ffn_beta"], f32)[None]

    sh["c_ident_bf"] = np.eye(128, dtype=BF)
    sh["c_ident_f"] = np.eye(128, dtype=f32)
    sh["c_ones_bf"] = np.ones((1, 128), dtype=BF)
    sh["c_ones_f"] = np.ones((1, 128), dtype=f32)
    p = np.arange(128)[:, None]
    c256 = np.arange(256)[None, :]
    sh["c_maskA"] = np.where((c256 > p + 64) & (c256 <= p + 128), 0.0, -1e30).astype(BF)
    c128 = np.arange(128)[None, :]
    sh["c_maskA0"] = np.where((c128 <= p) & (c128 > p - 64), 0.0, -1e30).astype(BF)
    c640 = np.arange(640)[None, :]
    sh["c_maskF"] = np.where((c640 >= p) & (c640 <= p + 512), 0.0, -1e30).astype(BF)
    kk = np.arange(128)[:, None]
    qq = np.arange(128)[None, :]
    sh["c_triu"] = (kk <= qq).astype(BF)
    psub = np.zeros((128, 128), dtype=f32)
    psub[np.arange(127), np.arange(1, 128)] = 1.0
    sh["c_psub"] = psub
    e0 = np.zeros((1, 128), dtype=f32)
    e0[0, 0] = 1.0
    sh["c_e0"] = e0
    p127 = np.zeros((128, 128), dtype=f32)
    p127[127, 0] = 1.0
    sh["c_p127"] = p127
    DH, WIN = 64, 512
    half = np.arange(0, DH, 2, dtype=np.float64)
    inv_freq = 1.0 / 10000.0 ** (half / DH)
    tt = np.arange(SEQ, dtype=np.float64)
    freqs = tt[:, None] * inv_freq[None, :]
    freqs = np.concatenate([freqs, freqs], axis=-1)
    base = (half + 0.4 * DH) / (1.4 * DH)
    power = (tt - SEQ // 2) / float(WIN // 2)
    scale = base[None, :] ** power[:, None]
    scale = np.concatenate([scale, scale], axis=-1)
    rot = np.stack([np.cos(freqs) * scale, np.sin(freqs) * scale,
                    np.cos(freqs) / scale, np.sin(freqs) / scale], axis=0).astype(f32)
    sh["c_rot"] = np.ascontiguousarray(rot.reshape(4, NT, 128, 64).transpose(2, 0, 1, 3))

    in_maps = []
    for core in range(NCORES):
        m = dict(sh)
        xb = x[2 * core:2 * core + 2]
        m["x_tok"] = np.stack([tokmaj(xb[0]), tokmaj(xb[1])])
        m["x_feaT"] = np.stack([feamaj(xb[0]), feamaj(xb[1])])
        in_maps.append(m)
    return in_maps


def _run(inputs, stages=ALL_STAGES):
    in_maps = _host_prep(inputs)
    p = P(bass.Bass("TRN2", target_bir_lowering=False, detect_race_conditions=False), stages)
    nc = p.build()
    res = run_bass_kernel_spmd(nc, in_maps, core_ids=list(range(NCORES)))
    outs = []
    for core in range(NCORES):
        o = res.results[core]["out"]
        for b_ in range(NB):
            outs.append(np.ascontiguousarray(o[b_].transpose(1, 0, 2).reshape(SEQ, D)))
    return np.stack(outs).astype(np.float32)


def kernel(**inputs):
    return _run(inputs, ALL_STAGES)


# version 16
# speedup vs baseline: 1.2191x; 1.0087x over previous
import numpy as np
import ml_dtypes
import concourse.bass as bass
import concourse.mybir as mybir
from concourse.bass_utils import run_bass_kernel_spmd
from contextlib import ExitStack

F32 = mybir.dt.float32
BF16 = mybir.dt.bfloat16
AF = mybir.ActivationFunctionType
ALU = mybir.AluOpType
AX = mybir.AxisListType
BF = ml_dtypes.bfloat16

NCORES = 8
NB = 2
NT = 16
D = 1024
SEQ = 2048
HID2 = 5460
HID = 2730
ALL_STAGES = ("A", "B0", "B1", "C", "D", "E")
FFN_ON = True
RELAX_INCW = False
PRECAST_ON = True


class Ctr:
    def __init__(self, h):
        self.h = h
        self.n = 0

    def inc(self, instr, k=1):
        self.n += k
        instr.then_inc(self.h, k)
        return instr

    def wait(self, eng, n=None):
        n = self.n if n is None else n
        if n > 0:
            eng.wait_ge(self.h, n)

    def incx(self, eng, instr, k=1):
        return self.inc(instr, k)

    def incw(self, eng, instr, k=1):
        self.inc(instr, k)
        if not RELAX_INCW:
            eng.wait_ge(self.h, self.n)
        return instr


class Slot:

    def __init__(self, handle):
        self.b = handle
        self.free = None

    def acquire(self, eng):
        if self.free is not None:
            c, n = self.free
            if n > 0:
                eng.wait_ge(c.h, n)

    def release(self, ctr):
        self.free = (ctr, ctr.n)


def bcast_last(ap3, nf):
    return bass.AP(ap3.tensor, ap3.offset, [ap3.ap[0], ap3.ap[1], [0, nf]])


def bcast_mid(ap2, nh):
    return bass.AP(ap2.tensor, ap2.offset, [ap2.ap[0], [0, nh], ap2.ap[1]])


class P:
    def __init__(self, nc, stages=ALL_STAGES):
        self.nc = nc
        self.stages = stages
        self.t, self.v, self.s, self.g, self.y = nc.tensor, nc.vector, nc.scalar, nc.gpsimd, nc.sync
        self.ctx = ExitStack()
        self.mm_i = 0
        self.cc_world = NCORES

    def nt(self, name):
        self.uid = getattr(self, "uid", 0) + 1
        return f"{name}_{self.uid}"

    def sb(self, name, shape, dt):
        return self.ctx.enter_context(self.nc.sbuf_tensor(self.nt(name), shape, dt))

    def next_mm(self):
        m = self.mm[self.mm_i % 4]
        self.mm_i += 1
        return m

    def powx(self, out, in_ap, bias_ap, pw):
        s = self.s
        self.sv.wait(s)
        self.ss.incw(s, s.activation(out=out, in_=in_ap, func=AF.Ln, bias=bias_ap, scale=1.0))
        self.ss.incw(s, s.activation(out=out, in_=out, func=AF.Exp, bias=self.zero1[:], scale=float(pw)))

    def build(self):
        nc = self.nc
        dt = nc.dram_tensor
        self.x_tok = dt("x_tok", [NB, 128, NT, D], F32, kind="ExternalInput").ap()
        self.x_feaT = dt("x_feaT", [NB, 128, 8, SEQ], F32, kind="ExternalInput").ap()
        self.out = dt("out", [NB, 128, NT, D], F32, kind="ExternalOutput").ap()
        self.wa_qkvT = dt("wa_qkvT", [128, 8, 3072], F32, kind="ExternalInput").ap()
        self.wa_outT = dt("wa_outT", [128, 8, D], F32, kind="ExternalInput").ap()
        self.ba_qkv = dt("ba_qkv", [128, 16], F32, kind="ExternalInput").ap()
        self.ba_bv = dt("ba_bv", [128, 8], F32, kind="ExternalInput").ap()
        self.ba_bout = dt("ba_bout", [1, D], F32, kind="ExternalInput").ap()
        self.wf_qkv = dt("wf_qkv", [2, 128, 8, 1536], F32, kind="ExternalInput").ap()
        self.wf_wo = dt("wf_wo", [2, 128, 4, D], F32, kind="ExternalInput").ap()
        self.wf_w1 = dt("wf_w1", [2, 128, 8, HID2], F32, kind="ExternalInput").ap()
        self.wf_w2 = dt("wf_w2", [2, 128, 22, D], F32, kind="ExternalInput").ap()
        self.wf_qs = dt("wf_qs", [1, 2, 64], F32, kind="ExternalInput").ap()
        self.wf_ks = dt("wf_ks", [1, 2, 64], F32, kind="ExternalInput").ap()
        self.wf_lng = dt("wf_lng", [128, 2, 8], F32, kind="ExternalInput").ap()
        self.wf_lnb = dt("wf_lnb", [128, 2, 8], F32, kind="ExternalInput").ap()
        self.wf_b1 = dt("wf_b1", [1, 2, HID2], F32, kind="ExternalInput").ap()
        self.wf_b2 = dt("wf_b2", [1, 2, D], F32, kind="ExternalInput").ap()
        self.w1_qT = dt("w1_qT", [128, 8, 256], F32, kind="ExternalInput").ap()
        self.w1_kT = dt("w1_kT", [128, 8, 256], F32, kind="ExternalInput").ap()
        self.w1_vT = dt("w1_vT", [128, 8, D], F32, kind="ExternalInput").ap()
        self.w1_oT = dt("w1_oT", [128, 8, D], F32, kind="ExternalInput").ap()
        self.w1_g = dt("w1_g", [1, 16], F32, kind="ExternalInput").ap()
        self.w1_b = dt("w1_b", [1, 16], F32, kind="ExternalInput").ap()
        self.we_wT = dt("we_wT", [128, 8, D], F32, kind="ExternalInput").ap()
        self.we_g = dt("we_g", [1, D], F32, kind="ExternalInput").ap()
        self.we_b = dt("we_b", [1, D], F32, kind="ExternalInput").ap()
        self.c_ident_bf = dt("c_ident_bf", [128, 128], BF16, kind="ExternalInput").ap()
        self.c_ident_f = dt("c_ident_f", [128, 128], F32, kind="ExternalInput").ap()
        self.c_ones_bf = dt("c_ones_bf", [1, 128], BF16, kind="ExternalInput").ap()
        self.c_ones_f = dt("c_ones_f", [1, 128], F32, kind="ExternalInput").ap()
        self.c_maskA = dt("c_maskA", [128, 256], BF16, kind="ExternalInput").ap()
        self.c_maskA0 = dt("c_maskA0", [128, 128], BF16, kind="ExternalInput").ap()
        self.c_maskF = dt("c_maskF", [128, 640], BF16, kind="ExternalInput").ap()
        self.c_triu = dt("c_triu", [128, 128], BF16, kind="ExternalInput").ap()
        self.c_psub = dt("c_psub", [128, 128], F32, kind="ExternalInput").ap()
        self.c_e0 = dt("c_e0", [1, 128], F32, kind="ExternalInput").ap()
        self.c_p127 = dt("c_p127", [128, 128], F32, kind="ExternalInput").ap()
        self.c_rot = dt("c_rot", [128, 4, NT, 64], F32, kind="ExternalInput").ap()
        self.x1_spill = dt("x1_spill", [128, NT, D], F32)
        self.w1_bf = dt("w1_bf", [2, 128, 8, 12, 512], BF16)
        self.x3_spill = dt("x3_spill", [128, NT, D], F32)
        self.ar_in = dt("ar_in", [1, 128], F32)
        self.ar_out = dt("ar_out", [1, 128], F32)

        E = self.ctx.enter_context
        self.sd = Ctr(E(nc.semaphore("sd")))
        self.sy_ = Ctr(E(nc.semaphore("sy")))
        self.sv = Ctr(E(nc.semaphore("sv")))
        self.ss = Ctr(E(nc.semaphore("ss")))
        self.st = Ctr(E(nc.semaphore("st")))
        self.scc = Ctr(E(nc.semaphore("scc")))

        self.X = self.sb("X", [128, NT, D], F32)
        self.XT = self.sb("XT", [128, 8, SEQ + 1], BF16)
        self.ident_bf = self.sb("ident_bf", [128, 128], BF16)
        self.ident_f = self.sb("ident_f", [128, 128], F32)
        self.ones_bf = self.sb("ones_bf", [1, 128], BF16)
        self.ones_f = self.sb("ones_f", [1, 128], F32)
        self.zero1 = self.sb("zero1", [128, 1], F32)
        self.eps1 = self.sb("eps1", [128, 1], F32)
        self.magic = self.sb("magic", [128, 1], F32)
        self.nmagic = self.sb("nmagic", [128, 1], F32)
        self.clipmax = self.sb("clipmax", [128, 1], F32)
        self.stats_e = self.sb("stats_e", [128, NB, NT, 2], F32)
        self.gmax = self.sb("gmax", [128, 1], F32)
        self.scale_b = self.sb("scale_b", [128, 1], F32)
        self.iscale_b = self.sb("iscale_b", [128, 1], F32)
        self.wf_lng_sb = self.sb("wf_lng_sb", [128, 2, 8], F32)
        self.wf_lnb_sb = self.sb("wf_lnb_sb", [128, 2, 8], F32)

        self.mm = [Slot(E(nc.psum_tensor(f"mm{i}", [128, 512], F32))) for i in range(4)]
        self.tsp = Slot(E(nc.psum_tensor("tsp", [128, 5, 128], BF16)))
        self.tsp2 = Slot(E(nc.psum_tensor("tsp2", [128, 5, 128], BF16)))
        self.po = Slot(E(nc.psum_tensor("po", [128, 512], F32)))
        self.po2 = Slot(E(nc.psum_tensor("po2", [128, 512], F32)))

        v, s, t, g, y = self.v, self.s, self.t, self.g, self.y
        self.sd.inc(g.dma_start(out=self.ident_bf[:], in_=self.c_ident_bf), 16)
        self.sd.inc(g.dma_start(out=self.ident_f[:], in_=self.c_ident_f), 16)
        self.sd.inc(g.dma_start(out=self.ones_bf[:], in_=self.c_ones_bf), 16)
        self.sd.inc(g.dma_start(out=self.ones_f[:], in_=self.c_ones_f), 16)
        self.sd.inc(g.dma_start(out=self.wf_lng_sb[:], in_=self.wf_lng), 16)
        self.sd.inc(g.dma_start(out=self.wf_lnb_sb[:], in_=self.wf_lnb), 16)
        self.sv.incw(v, v.memset(self.zero1[:], 0.0))
        self.sv.incw(v, v.memset(self.eps1[:], 1e-5))
        self.sv.incw(v, v.memset(self.magic[:], 12582912.0))
        self.sv.incw(v, v.memset(self.nmagic[:], -12582912.0))
        self.sv.incw(v, v.memset(self.clipmax[:], 1e12))
        self.sv.incw(v, v.memset(self.gmax[:], 0.0))
        for b in range(NB):
            self.load_x(b)
            if b == 0 and PRECAST_ON and ("B0" in self.stages or "B1" in self.stages):
                self.precast_w1()
            if "A" in self.stages:
                self.stage_A(b)
            for li, stg in ((0, "B0"), (1, "B1")):
                if stg in self.stages:
                    self.stage_B(b, li)
            if "C" in self.stages:
                self.stage_C(b)
            if "D" in self.stages:
                self.stage_D(b)
                if "E" in self.stages:
                    self.stage_D_eprep(b)
            if "E" in self.stages and b != NB - 1:
                self.ss.wait(y)
                self.sv.wait(y)
                for tc in range(4):
                    self.sy_.inc(y.dma_start(out=self.x3_spill.ap()[:, 4 * tc:4 * tc + 4, :],
                                             in_=self.X[:, 4 * tc:4 * tc + 4, :]), 16)
            if "E" not in self.stages:
                self.ss.wait(y)
                self.sv.wait(y)
                for tc in range(4):
                    self.sy_.inc(y.dma_start(out=self.out[b, :, 4 * tc:4 * tc + 4, :],
                                             in_=self.X[:, 4 * tc:4 * tc + 4, :]), 16)
        if "E" in self.stages:
            self.stage_E_all()
        y.wait_ge(self.sy_.h, self.sy_.n)
        self.ctx.close()
        return nc

    def load_x(self, b):
        y, v, s, t, g = self.y, self.v, self.s, self.t, self.g
        self.sv.wait(y)
        self.ss.wait(y)
        self.st.wait(y)
        self.sv.wait(g)
        self.ss.wait(g)
        self.st.wait(g)
        for tc in range(4):
            self.sy_.inc(y.dma_start(out=self.X[:, 4 * tc:4 * tc + 4, :],
                                     in_=self.x_tok[b, :, 4 * tc:4 * tc + 4, :]), 16)
        self.sy_.wait(v)
        self.sv.incw(v, v.memset(self.XT[:, :, 0:1], 0.0))
        self.sd.inc(g.dma_start(out=self.XT[:, :, 1:SEQ + 1], in_=self.x_feaT[b]), 16)

    def precast_w1(self):
        nc, g, y = self.nc, self.g, self.y
        with ExitStack() as c:
            E = c.enter_context
            scr = [E(nc.sbuf_tensor(self.nt("pc_scr"), [128, 8, 512], BF16)) for _ in range(2)]
            slots = [Slot(s_) for s_ in scr]
            for li in range(2):
                for cc in range(12):
                    p_, half = cc // 2, cc % 2
                    a0 = 512 * p_
                    a1 = min(a0 + 512, HID)
                    w = a1 - a0
                    c0 = a0 if half == 0 else HID + a0
                    i = cc % 2
                    slots[i].acquire(g)
                    self.sd.inc(g.dma_start(out=scr[i][:, :, 0:w], in_=self.wf_w1[li, :, :, c0:c0 + w]), 16)
                    self.sd.wait(y)
                    self.sy_.inc(y.dma_start(out=self.w1_bf.ap()[li, :, :, cc, :], in_=scr[i][:]), 16)
                    slots[i].release(self.sy_)
            g.wait_ge(self.sy_.h, self.sy_.n)
            y.wait_ge(self.sy_.h, self.sy_.n)

    def emit_w2_part(self, it, p_, ti, wch, dacc, w2, hTb, hT_s):
        t, v, s = self.t, self.v, self.s
        nhb = (wch + 127) // 128
        hT_s[it].acquire(t)
        self.sv.wait(t)
        banks = (self.tsp, self.tsp2)

        def evac_mm(hl, n_t):
            hw = min(128, wch - 128 * hl)
            sl = banks[hl % 2]
            if hl % 2 == 0:
                v.wait_ge(self.st.h, n_t)
                self.sv.incx(v, v.tensor_copy(out=hTb[0:hw, it, hl, :], in_=sl.b[0:hw, 2 + hl // 2, :]))
                sl.release(self.sv)
                t.wait_ge(self.sv.h, self.sv.n)
            else:
                s.wait_ge(self.st.h, n_t)
                self.ss.incx(s, s.copy(out=hTb[0:hw, it, hl, :], in_=sl.b[0:hw, 2 + hl // 2, :]))
                sl.release(self.ss)
                t.wait_ge(self.ss.h, self.ss.n)
            hb = 4 * p_ + hl
            for oc in range(2):
                dm = dacc[2 * ti + oc]
                self.st.inc(t.matmul(dm.b[:], hTb[0:hw, it, hl, :], w2[0:hw, hb, 512 * oc:512 * oc + 512],
                                     start=(p_ == 0 and hl == 0), stop=False, skip_group_check=True))

        pend = None
        for hl in range(nhb):
            hw = min(128, wch - 128 * hl)
            sl = banks[hl % 2]
            sl.acquire(t)
            self.st.inc(t.transpose(sl.b[0:hw, 2 + hl // 2, :],
                                    self.hidc_cur[it][:, 128 * hl:128 * hl + hw], self.ident_bf[:]))
            n_t = self.st.n
            if pend is not None:
                evac_mm(*pend)
            pend = (hl, n_t)
        self.hid_s_cur[it].release(self.st)
        if pend is not None:
            evac_mm(*pend)
        hT_s[it].release(self.st)

    def lazy_xt(self, T):
        t, v, s = self.t, self.v, self.s
        xbf = self.xbf_s
        xbf.acquire(s)
        self.sv.wait(s)
        self.ss.incx(s, s.copy(out=xbf.b[:], in_=self.X[:, T, :]))
        for db in range(8):
            self.tsp.acquire(t)
            self.ss.wait(t)
            self.st.inc(t.transpose(self.tsp.b[:, db % 5, :], xbf.b[:, 128 * db:128 * db + 128],
                                    self.ident_bf[:]))
            self.tsp.release(self.st)
            self.st.wait(v)
            self.sv.incx(v, v.tensor_copy(out=self.XT[:, db, 1 + 128 * T:1 + 128 * T + 128],
                                      in_=self.tsp.b[:, db % 5, :]))
            self.tsp.release(self.sv)
        xbf.release(self.st)

    def build_gbc(self, gbc):
        t, v, g = self.t, self.v, self.g
        with self.nc.sbuf_tensor(self.nt("grow"), [1, 2, D], F32) as grow:
            self.sv.wait(g)
            self.sd.inc(g.dma_start(out=grow[0:1, 0, :], in_=self.we_g), 16)
            self.sd.inc(g.dma_start(out=grow[0:1, 1, :], in_=self.we_b), 16)
            for which in range(2):
                for oc in range(2):
                    m = self.next_mm()
                    m.acquire(t)
                    t.wait_ge(self.sd.h, self.sd.n)
                    self.st.inc(t.matmul(m.b[:], self.ones_f[:], grow[0:1, which, 512 * oc:512 * oc + 512],
                                         start=True, stop=True, skip_group_check=True))
                    self.st.wait(v)
                    self.sv.incw(v, v.tensor_copy(out=gbc[:, which, 512 * oc:512 * oc + 512], in_=m.b[:]))
                    m.release(self.sv)

    def ln_combine(self, bn):
        v = self.v
        self.sv.incw(v, v.tensor_tensor(out=bn[:, 16:18], in0=bn[:, 12:14], in1=bn[:, 14:16], op=ALU.add))
        self.sv.incw(v, v.tensor_scalar(out=bn[:, 16:18], in0=bn[:, 16:18], scalar1=0.5, scalar2=None,
                                    op0=ALU.mult, op1=ALU.bypass))
        self.sv.incw(v, v.tensor_tensor(out=bn[:, 18:19], in0=bn[:, 12:13], in1=bn[:, 14:15], op=ALU.subtract))
        self.sv.incw(v, v.tensor_tensor(out=bn[:, 18:19], in0=bn[:, 18:19], in1=bn[:, 18:19], op=ALU.mult))
        self.sv.incw(v, v.scalar_tensor_tensor(out=bn[:, 17:18], in0=bn[:, 18:19], scalar=0.25,
                                           in1=bn[:, 17:18], op0=ALU.mult, op1=ALU.add))

    def ln_apply_tile(self, T):
        v, s = self.v, self.s
        bn = self.bn
        self.ss.wait(v)
        self.sv.incw(v, v.bn_stats(bn[:, 0:6], self.X[:, T, 0:512]))
        self.sv.incw(v, v.bn_stats(bn[:, 6:12], self.X[:, T, 512:1024]))
        self.sv.incw(v, v.bn_aggr(bn[:, 12:14], bn[:, 0:6]))
        self.sv.incw(v, v.bn_aggr(bn[:, 14:16], bn[:, 6:12]))
        self.ln_combine(bn)
        rstd, nmr = bn[:, 19:20], bn[:, 20:21]
        self.powx(rstd, bn[:, 17:18], self.eps1[:], -0.5)
        self.ss.wait(v)
        self.sv.incw(v, v.tensor_tensor(out=nmr, in0=bn[:, 16:17], in1=rstd, op=ALU.mult))
        self.sv.incw(v, v.tensor_scalar(out=nmr, in0=nmr, scalar1=-1.0, scalar2=None,
                                    op0=ALU.mult, op1=ALU.bypass))
        self.sv.wait(s)
        self.ss.incw(s, s.activation(out=self.X[:, T, :], in_=self.X[:, T, :], func=AF.Identity,
                                 bias=nmr, scale=rstd))

    def softmax_block(self, psum_aps, mask_aps, scf_slot, probs_slot, sm, ncols,
                      scf_ap=None, probs_ap=None):
        v, s = self.v, self.s
        scf_b = scf_ap if scf_ap is not None else scf_slot.b[:]
        probs_b = probs_ap if probs_ap is not None else probs_slot.b[:]
        c0 = 0
        for pap, map_ in zip(psum_aps, mask_aps):
            w = pap.shape[-1]
            self.st.wait(v)
            self.sv.incw(v, v.tensor_tensor(out=scf_b[:, c0:c0 + w], in0=pap, in1=map_, op=ALU.add))
            c0 += w
        assert c0 == ncols
        self.sv.incx(v, v.reduce_max(out=sm[:, 0:1], in_=scf_b[:, 0:ncols], axis=AX.X, negate=True))
        self.sv.wait(s)
        self.ss.incx(s, s.activation(out=probs_b[:, 0:ncols], in_=scf_b[:, 0:ncols], func=AF.Exp,
                                 bias=sm[:, 0:1], accum_out=sm[:, 1:2]))
        scf_slot.release(self.ss)
        self.ss.wait(v)
        self.sv.incw(v, v.reciprocal(out=sm[:, 2:3], in_=sm[:, 1:2]))
        self.sv.incx(v, v.tensor_scalar(out=probs_b[:, 0:ncols], in0=probs_b[:, 0:ncols],
                                    scalar1=sm[:, 2:3], scalar2=None, op0=ALU.mult, op1=ALU.bypass))

    def softmax_psum(self, psum_ap, probs_slot, sm, ncols, probs_ap=None):
        v, s = self.v, self.s
        probs_b = probs_ap if probs_ap is not None else probs_slot.b[:]
        self.st.wait(v)
        self.sv.incx(v, v.reduce_max(out=sm[:, 0:1], in_=psum_ap, axis=AX.X, negate=True))
        self.sv.wait(s)
        self.ss.incx(s, s.activation(out=probs_b[:, 0:ncols], in_=psum_ap, func=AF.Exp,
                                     bias=sm[:, 0:1], accum_out=sm[:, 1:2]))
        self.ss.wait(v)
        self.sv.incw(v, v.reciprocal(out=sm[:, 2:3], in_=sm[:, 1:2]))
        self.sv.incx(v, v.tensor_scalar(out=probs_b[:, 0:ncols], in0=probs_b[:, 0:ncols],
                                        scalar1=sm[:, 2:3], scalar2=None, op0=ALU.mult, op1=ALU.bypass))

    def softmax_psum2(self, papA, papB, probs_slot, sm, ncols, probs_ap):
        v, s = self.v, self.s
        if papA is None:
            return self.softmax_psum(papB, probs_slot, sm, ncols, probs_ap)
        wA = papA.shape[-1]
        self.st.wait(v)
        self.sv.incx(v, v.reduce_max(out=sm[:, 0:1], in_=papA, axis=AX.X, negate=True))
        self.sv.incw(v, v.reduce_max(out=sm[:, 3:4], in_=papB, axis=AX.X, negate=True))
        self.sv.incx(v, v.tensor_tensor(out=sm[:, 0:1], in0=sm[:, 0:1], in1=sm[:, 3:4], op=ALU.min))
        self.sv.wait(s)
        self.ss.incx(s, s.activation(out=probs_ap[:, 0:wA], in_=papA, func=AF.Exp,
                                     bias=sm[:, 0:1], accum_out=sm[:, 1:2]))
        self.ss.incx(s, s.activation(out=probs_ap[:, wA:ncols], in_=papB, func=AF.Exp,
                                     bias=sm[:, 0:1], accum_out=sm[:, 3:4]))
        self.ss.wait(v)
        self.sv.incw(v, v.tensor_tensor(out=sm[:, 1:2], in0=sm[:, 1:2], in1=sm[:, 3:4], op=ALU.add))
        self.sv.incw(v, v.reciprocal(out=sm[:, 2:3], in_=sm[:, 1:2]))
        self.sv.incx(v, v.tensor_scalar(out=probs_ap[:, 0:ncols], in0=probs_ap[:, 0:ncols],
                                        scalar1=sm[:, 2:3], scalar2=None, op0=ALU.mult, op1=ALU.bypass))

    def transpose_to(self, src_ap_list, dst_ap_list, ident):
        t, v, s = self.t, self.v, self.s
        banks = (self.tsp, self.tsp2)
        pend = None
        items = list(zip(src_ap_list, dst_ap_list))
        for i, (sap, dap) in enumerate(items):
            b = banks[i % 2]
            sl = (i // 2) % 5
            b.acquire(t)
            self.st.inc(t.transpose(b.b[0:sap.shape[-1], sl, :], sap, ident))
            n_t = self.st.n
            if pend is not None:
                self._tp_evac(*pend)
            pend = (b, (i // 2) % 5, items[i][0].shape[-1], dap, n_t, i)
        if pend is not None:
            self._tp_evac(*pend)

    def _tp_evac(self, b, sl, w, dap, n_t, i):
        v, s = self.v, self.s
        if i % 2 == 0:
            v.wait_ge(self.st.h, n_t)
            self.sv.incx(v, v.tensor_copy(out=dap, in_=b.b[0:w, sl, :]))
            b.release(self.sv)
        else:
            s.wait_ge(self.st.h, n_t)
            self.ss.incx(s, s.copy(out=dap, in_=b.b[0:w, sl, :]))
            b.release(self.ss)

    def stage_A(self, b):
        nc, t, v, s, g, y = self.nc, self.t, self.v, self.s, self.g, self.y
        with ExitStack() as c:
            E = c.enter_context
            wqkv = E(nc.sbuf_tensor(self.nt("a_wqkv"), [128, 8, 3072], BF16))
            wout = E(nc.sbuf_tensor(self.nt("a_wout"), [128, 8, D], BF16))
            bqk = E(nc.sbuf_tensor(self.nt("a_bqk"), [128, 16], F32))
            bv = E(nc.sbuf_tensor(self.nt("a_bv"), [128, 8], BF16))
            brow = E(nc.sbuf_tensor(self.nt("a_brow"), [1, D], BF16))
            browf = E(nc.sbuf_tensor(self.nt("a_browf"), [1, D], F32))
            qk_c = E(nc.sbuf_tensor(self.nt("a_qk"), [128, 16, 512], BF16))
            kcar = E(nc.sbuf_tensor(self.nt("a_kcar"), [128, 8, 128], BF16))
            v_c = E(nc.sbuf_tensor(self.nt("a_v"), [128, 4, D], BF16))
            vcar = E(nc.sbuf_tensor(self.nt("a_vcar"), [128, D], BF16))
            probs = E(nc.sbuf_tensor(self.nt("a_probs"), [128, 3, 256], BF16))
            pT = E(nc.sbuf_tensor(self.nt("a_pT"), [128, 3, 2, 128], BF16))
            oT = E(nc.sbuf_tensor(self.nt("a_oT"), [128, 8, 128], BF16))
            sm = E(nc.sbuf_tensor(self.nt("a_sm"), [128, 3, 4], F32))
            self.bn = E(nc.sbuf_tensor(self.nt("a_bn"), [128, 24], F32))
            maskA = E(nc.sbuf_tensor(self.nt("a_maskA"), [128, 256], BF16))
            maskA0 = E(nc.sbuf_tensor(self.nt("a_maskA0"), [128, 128], BF16))

            qk_s, v_s = Slot(qk_c), Slot(v_c)
            oT_s = Slot(oT)
            probs_s = [Slot(None), Slot(None), Slot(None)]
            pT_s = [Slot(None), Slot(None), Slot(None)]
            kcar_s, vcar_s = Slot(kcar), Slot(vcar)

            self.st.wait(g)
            self.sd.inc(g.dma_start(out=maskA[:], in_=self.c_maskA), 16)
            self.sd.inc(g.dma_start(out=maskA0[:], in_=self.c_maskA0), 16)
            self.sd.inc(g.dma_start(out=wqkv[:], in_=self.wa_qkvT), 16)
            self.sd.inc(g.dma_start(out=wout[:], in_=self.wa_outT), 16)
            self.sd.inc(g.dma_start(out=bqk[:], in_=self.ba_qkv), 16)
            self.sd.inc(g.dma_start(out=bv[:], in_=self.ba_bv), 16)
            self.sd.inc(g.dma_start(out=browf[:], in_=self.ba_bout), 16)

            self.sd.wait(s)
            self.ss.incw(s, s.mul(out=bqk[:, 0:8], in_=bqk[:, 0:8], mul=0.125))

            t.wait_ge(self.sd.h, self.sd.n)
            for oc in range(2):
                m = self.next_mm()
                m.acquire(t)
                for db in range(8):
                    self.st.inc(t.matmul(m.b[0:1, :], bv[:, db:db + 1], wout[:, db, 512 * oc:512 * oc + 512],
                                         start=(db == 0), stop=(db == 7), skip_group_check=True))
                self.st.wait(v)
                self.sv.incw(v, v.tensor_tensor(out=browf[0:1, 512 * oc:512 * oc + 512],
                                            in0=m.b[0:1, :], in1=browf[0:1, 512 * oc:512 * oc + 512], op=ALU.add))
                m.release(self.sv)
            self.sv.incw(v, v.tensor_copy(out=brow[:], in_=browf[:]))

            self.sd.wait(t)
            self.sv.wait(t)
            self.sy_.wait(t)
            self.sy_.wait(v)

            for tc in range(4):
                qk_s.acquire(s)
                qk_s.acquire(t)
                for ob in range(16):
                    m = self.next_mm()
                    m.acquire(t)
                    for db in range(8):
                        self.st.inc(t.matmul(m.b[:], wqkv[:, db, 128 * ob:128 * ob + 128],
                                             self.XT[:, db, 1 + 512 * tc:1 + 512 * tc + 512],
                                             start=(db == 0), stop=(db == 7), skip_group_check=True))
                    self.st.wait(s)
                    self.ss.incx(s, s.activation(out=qk_c[:, ob, :], in_=m.b[:], func=AF.Identity,
                                             bias=bqk[:, ob:ob + 1], scale=(0.125 if ob < 8 else 1.0)))
                    m.release(self.ss)
                v_s.acquire(v)
                for tt in range(4):
                    T = 4 * tc + tt
                    for vc in range(2):
                        m = self.next_mm()
                        m.acquire(t)
                        for db in range(8):
                            self.st.inc(t.matmul(m.b[:], self.XT[:, db, 1 + 128 * T:1 + 128 * T + 128],
                                                 wqkv[:, db, 2048 + 512 * vc:2048 + 512 * vc + 512],
                                                 start=(db == 0), stop=(db == 7), skip_group_check=True))
                        self.st.wait(v)
                        self.sv.incx(v, v.tensor_copy(out=v_c[:, tt, 512 * vc:512 * vc + 512], in_=m.b[:]))
                        m.release(self.sv)

                for tt in range(4):
                    T = 4 * tc + tt
                    ncols = 128 if T == 0 else 256
                    nkt = ncols // 128

                    def a_tail(h, par, sv_n):
                        hp, hh = h // 2, h % 2
                        opo = self.po if hp % 2 == 0 else self.po2
                        pT_s[par].acquire(v)
                        pT_s[par].acquire(s)
                        t.wait_ge(self.sv.h, sv_n)
                        self.transpose_to([probs[:, par, 128 * kt:128 * kt + 128] for kt in range(nkt)],
                                          [pT[:, par, kt, :] for kt in range(nkt)], self.ident_bf[:])
                        probs_s[par].release(self.st)
                        t.wait_ge(self.sv.h, self.sv.n)
                        t.wait_ge(self.ss.h, self.ss.n)
                        if hh == 0:
                            opo.acquire(t)
                        tp = (0, 64 * hh)
                        od = opo.b[64 * hh:64 * hh + 64, 0:128]
                        if T == 0:
                            self.st.inc(t.matmul(od, v_c[:, tt, 64 * h:64 * h + 64], pT[:, par, 0, :],
                                                 start=True, stop=True, tile_position=tp,
                                                 skip_group_check=True))
                        else:
                            vprev = vcar[:, 64 * h:64 * h + 64] if tt == 0 else v_c[:, tt - 1, 64 * h:64 * h + 64]
                            self.st.inc(t.matmul(od, vprev, pT[:, par, 0, :],
                                                 start=True, stop=False, tile_position=tp,
                                                 skip_group_check=True))
                            self.st.inc(t.matmul(od, v_c[:, tt, 64 * h:64 * h + 64], pT[:, par, 1, :],
                                                 start=False, stop=True, tile_position=tp,
                                                 skip_group_check=True))
                        pT_s[par].release(self.st)
                        if hh == 1:
                            oT_s.acquire(v)
                            v.wait_ge(self.st.h, self.st.n)
                            self.sv.incx(v, v.tensor_copy(out=oT[:, hp, :], in_=opo.b[:, 0:128]))
                            opo.release(self.sv)

                    from collections import deque
                    pendq = deque()
                    self.ss.wait(t)
                    self.sv.wait(t)
                    for h in range(16):
                        bp, pbq, pbk = 64 * (h % 2), h // 2, 8 + h // 2
                        par = h % 3
                        msc = self.next_mm()
                        msc.acquire(t)
                        if T == 0:
                            self.st.inc(t.matmul(msc.b[:, 0:128], qk_c[bp:bp + 64, pbq, 0:128],
                                                 qk_c[bp:bp + 64, pbk, 0:128], start=True, stop=False,
                                                 skip_group_check=True))
                        elif tt == 0:
                            self.st.inc(t.matmul(msc.b[:, 0:128], qk_c[bp:bp + 64, pbq, 0:128],
                                                 kcar[bp:bp + 64, h // 2, :], start=True, stop=False,
                                                 skip_group_check=True))
                            self.st.inc(t.matmul(msc.b[:, 128:256], qk_c[bp:bp + 64, pbq, 0:128],
                                                 qk_c[bp:bp + 64, pbk, 0:128], start=False, stop=False,
                                                 skip_group_check=True))
                        else:
                            self.st.inc(t.matmul(msc.b[:, 0:256], qk_c[bp:bp + 64, pbq, 128 * tt:128 * tt + 128],
                                                 qk_c[bp:bp + 64, pbk, 128 * (tt - 1):128 * (tt - 1) + 256],
                                                 start=True, stop=False, skip_group_check=True))
                        mask = maskA0[:, 0:128] if T == 0 else maskA[:, 0:256]
                        self.st.inc(t.matmul(msc.b[:, 0:ncols], self.ident_bf[:], mask,
                                             start=False, stop=True, skip_group_check=True))
                        if len(pendq) >= 2:
                            a_tail(*pendq.popleft())
                        probs_s[par].acquire(s)
                        self.softmax_psum(msc.b[:, 0:ncols], probs_s[par],
                                          sm[:, par, :], ncols, probs[:, par, :])
                        msc.release(self.ss)
                        pendq.append((h, par, self.sv.n))
                    while pendq:
                        a_tail(*pendq.popleft())

                    for oc in range(2):
                        m = self.next_mm()
                        m.acquire(t)
                        self.sv.wait(t)
                        for fb in range(8):
                            self.st.inc(t.matmul(m.b[:], oT[:, fb, :], wout[:, fb, 512 * oc:512 * oc + 512],
                                                 start=(fb == 0), stop=False, skip_group_check=True))
                        self.st.inc(t.matmul(m.b[:], self.ones_bf[:], brow[0:1, 512 * oc:512 * oc + 512],
                                             start=False, stop=True, skip_group_check=True))
                        self.st.wait(v)
                        self.sv.incw(v, v.tensor_tensor(out=self.X[:, T, 512 * oc:512 * oc + 512],
                                                    in0=m.b[:], in1=self.X[:, T, 512 * oc:512 * oc + 512],
                                                    op=ALU.add))
                        m.release(self.sv)
                    oT_s.release(self.st)
                    self.ln_apply_tile(T)
                    if "B0" in self.stages or "C" in self.stages:
                        self.ss.wait(y)
                        self.sy_.inc(y.dma_start(out=self.x1_spill.ap()[:, T, :], in_=self.X[:, T, :]), 16)
                if tc < 3:
                    kcar_s.acquire(v)
                    vcar_s.acquire(v)
                    self.st.wait(v)
                    self.sv.incx(v, v.tensor_copy(out=kcar[:], in_=qk_c[:, 8:16, 384:512]))
                    self.sv.incw(v, v.tensor_copy(out=vcar[:], in_=v_c[:, 3, :]))
                    kcar_s.release(self.sv)
                    vcar_s.release(self.sv)
                qk_s.release(self.st)
                v_s.release(self.st)

    def stage_B(self, b, li):
        nc, t, v, s, g, y = self.nc, self.t, self.v, self.s, self.g, self.y
        with ExitStack() as c:
            E = c.enter_context
            self.bn = E(nc.sbuf_tensor(self.nt("b_bn"), [128, 24], F32))
            with ExitStack() as p1:
                E1 = p1.enter_context
                self.xbf_s = Slot(E1(nc.sbuf_tensor(self.nt("b_xbf"), [128, D], BF16)))
                wqkv = E1(nc.sbuf_tensor(self.nt("b_wqkv"), [128, 8, 1536], BF16))
                wo = E1(nc.sbuf_tensor(self.nt("b_wo"), [128, 4, D], BF16))
                qsb = E1(nc.sbuf_tensor(self.nt("b_qsb"), [128, 64], F32))
                ksb = E1(nc.sbuf_tensor(self.nt("b_ksb"), [128, 64], F32))
                qsrow = E1(nc.sbuf_tensor(self.nt("b_qsrow"), [1, 128], F32))
                kT = E1(nc.sbuf_tensor(self.nt("b_kT"), [128, 4, SEQ], BF16))
                vfull = E1(nc.sbuf_tensor(self.nt("b_v"), [128, NT, 512], BF16))
                qf = E1(nc.sbuf_tensor(self.nt("b_qf"), [128, 512], F32))
                kf = E1(nc.sbuf_tensor(self.nt("b_kf"), [128, 512], F32))
                rtmp = E1(nc.sbuf_tensor(self.nt("b_rtmp"), [128, 512], F32))
                qe = E1(nc.sbuf_tensor(self.nt("b_qe"), [128, 512], BF16))
                ke = E1(nc.sbuf_tensor(self.nt("b_ke"), [128, 512], BF16))
                qT_t = E1(nc.sbuf_tensor(self.nt("b_qT"), [128, 4, 128], BF16))
                scf = E1(nc.sbuf_tensor(self.nt("b_scf"), [128, 1, 640], F32))
                probs = E1(nc.sbuf_tensor(self.nt("b_probs"), [128, 3, 640], BF16))
                pT = E1(nc.sbuf_tensor(self.nt("b_pT"), [128, 3, 5, 128], BF16))
                oT = E1(nc.sbuf_tensor(self.nt("b_oT"), [128, 4, 128], BF16))
                sm = E1(nc.sbuf_tensor(self.nt("b_sm"), [128, 3, 4], F32))
                nrm = E1(nc.sbuf_tensor(self.nt("b_nrm"), [128, 2, 8], F32))
                rot = E1(nc.sbuf_tensor(self.nt("b_rot"), [128, 4, NT, 64], F32))
                rotscr = E1(nc.sbuf_tensor(self.nt("b_rotscr"), [128, 8, 64], F32))
                maskF = E1(nc.sbuf_tensor(self.nt("b_maskF"), [128, 640], BF16))

                self.st.wait(g)
                self.ss.wait(g)
                self.sv.wait(g)
                self.sd.inc(g.dma_start(out=rot[:], in_=self.c_rot), 16)
                self.sd.inc(g.dma_start(out=maskF[:], in_=self.c_maskF), 16)
                self.sd.inc(g.dma_start(out=wqkv[:], in_=self.wf_qkv[li]), 16)
                self.sd.inc(g.dma_start(out=wo[:], in_=self.wf_wo[li]), 16)
                self.sd.inc(g.dma_start(out=qsrow[0:1, 0:64], in_=self.wf_qs[0:1, li, :]), 16)
                self.sd.inc(g.dma_start(out=qsrow[0:1, 64:128], in_=self.wf_ks[0:1, li, :]), 16)
                m = self.next_mm()
                m.acquire(t)
                t.wait_ge(self.sd.h, self.sd.n)
                self.st.inc(t.matmul(m.b[:, 0:128], self.ones_f[:], qsrow[:], start=True, stop=True,
                                     skip_group_check=True))
                self.st.wait(v)
                self.sv.incw(v, v.tensor_scalar(out=qsb[:], in0=m.b[:, 0:64], scalar1=8.0, scalar2=None,
                                            op0=ALU.mult, op1=ALU.bypass))
                self.sv.incw(v, v.tensor_copy(out=ksb[:], in_=m.b[:, 64:128]))
                m.release(self.sv)

                qf_s, kf_s, rtmp_s = Slot(qf), Slot(kf), Slot(rtmp)
                qe_s, ke_s, qT_s = Slot(qe), Slot(ke), Slot(qT_t)
                oT_s = Slot(oT)
                scf_s = [Slot(None), Slot(None)]
                probs_s = [Slot(None), Slot(None), Slot(None)]
                pT_s = [Slot(None), Slot(None), Slot(None)]
                vf_s, kTf_s = Slot(vfull), Slot(kT)

                for T in range(NT):
                    self.lazy_xt(T)

                    def hsl(db, T=T):
                        off = 1 if db < 4 else 0
                        return self.XT[:, db, off + 128 * T: off + 128 * T + 128]

                    for part in range(3):
                        m = self.next_mm()
                        m.acquire(t)
                        self.sv.wait(t)
                        for db in range(8):
                            self.st.inc(t.matmul(m.b[:], hsl(db), wqkv[:, db, 512 * part:512 * part + 512],
                                                 start=(db == 0), stop=(db == 7), skip_group_check=True))
                        if part == 0:
                            qf_s.acquire(s)
                            self.st.wait(s)
                            self.ss.incw(s, s.copy(out=qf[:], in_=m.b[:]))
                            m.release(self.ss)
                        elif part == 1:
                            kf_s.acquire(v)
                            self.st.wait(v)
                            self.sv.incw(v, v.tensor_copy(out=kf[:], in_=m.b[:]))
                            m.release(self.sv)
                        else:
                            vf_s.acquire(v)
                            self.st.wait(v)
                            self.sv.incw(v, v.tensor_copy(out=vfull[:, T, :], in_=m.b[:]))
                            m.release(self.sv)

                    for which in range(2):
                        src = qf if which == 0 else kf
                        src_s = qf_s if which == 0 else kf_s
                        dst = qe if which == 0 else ke
                        dst_s = qe_s if which == 0 else ke_s
                        scl = qsb if which == 0 else ksb
                        ci, si = (0, 1) if which == 0 else (2, 3)
                        src3 = src[:].rearrange("p (h f) -> p h f", h=8)
                        scf_s[0].acquire(v)
                        self.ss.wait(v)
                        sqv = scf[:, 0, 0:512].rearrange("p (h f) -> p h f", h=8)
                        self.sv.incw(v, v.tensor_tensor(out=sqv, in0=src3, in1=src3, op=ALU.mult))
                        self.sv.incw(v, v.reduce_sum(out=nrm[:, 0, :], in_=sqv, axis=AX.X))
                        self.powx(nrm[:, 1, :], nrm[:, 0, :], self.zero1[:], -0.5)
                        self.ss.wait(v)
                        self.sv.incw(v, v.tensor_scalar(out=nrm[:, 1, :], in0=nrm[:, 1, :],
                                                    scalar1=self.clipmax[:, 0:1], scalar2=None,
                                                    op0=ALU.min, op1=ALU.bypass))
                        qh = sqv
                        rb = bcast_last(nrm[:, 1, :].rearrange("p (h a) -> p h a", a=1), 64)
                        self.sv.incw(v, v.tensor_tensor(out=qh, in0=src3, in1=rb, op=ALU.mult))
                        src_s.release(self.sv)
                        self.sv.incw(v, v.tensor_tensor(out=qh, in0=qh, in1=bcast_mid(scl[:], 8), op=ALU.mult))
                        cosb = bcast_mid(rot[:, ci, T, :], 8)
                        sin_lo = bcast_mid(rot[:, si, T, 0:32], 8)
                        sin_hi = bcast_mid(rot[:, si, T, 32:64], 8)
                        rtmp_s.acquire(v)
                        tmp3 = rtmp[:].rearrange("p (h f) -> p h f", h=8)
                        self.sv.incx(v, v.tensor_tensor(out=tmp3, in0=qh, in1=cosb, op=ALU.mult))
                        self.sv.incx(v, v.tensor_tensor(out=rotscr[:, :, 0:32], in0=qh[:, :, 0:32],
                                                    in1=sin_hi, op=ALU.mult))
                        self.sv.incw(v, v.tensor_tensor(out=rotscr[:, :, 32:64], in0=qh[:, :, 32:64],
                                                    in1=sin_lo, op=ALU.mult))
                        scf_s[0].release(self.sv)
                        self.sv.incx(v, v.tensor_tensor(out=tmp3[:, :, 32:64], in0=tmp3[:, :, 32:64],
                                                    in1=rotscr[:, :, 0:32], op=ALU.add))
                        self.sv.incw(v, v.tensor_tensor(out=tmp3[:, :, 0:32], in0=tmp3[:, :, 0:32],
                                                    in1=rotscr[:, :, 32:64], op=ALU.subtract))
                        dst_s.acquire(v)
                        self.sv.incx(v, v.tensor_copy(out=dst[:], in_=rtmp[:]))
                        rtmp_s.release(self.sv)
                        if which == 0:
                            qT_s.acquire(v)
                            qT_s.acquire(s)
                            self.sv.wait(t)
                            self.transpose_to([qe[:, 128 * cc:128 * cc + 128] for cc in range(4)],
                                              [qT_t[:, cc, :] for cc in range(4)], self.ident_bf[:])
                        else:
                            kTf_s.acquire(v)
                            kTf_s.acquire(s)
                            self.sv.wait(t)
                            self.transpose_to([ke[:, 128 * cc:128 * cc + 128] for cc in range(4)],
                                              [kT[:, cc, 128 * T:128 * T + 128] for cc in range(4)],
                                              self.ident_bf[:])
                        dst_s.release(self.st)

                    nk = min(T, 4)
                    ncols = 128 * (nk + 1)

                    def attn_tail(h, par, sv_n):
                        hp, hh = h // 2, h % 2
                        opo = self.po if hp % 2 == 0 else self.po2
                        pT_s[par].acquire(v)
                        pT_s[par].acquire(s)
                        t.wait_ge(self.sv.h, sv_n)
                        self.transpose_to([probs[:, par, 128 * kt:128 * kt + 128] for kt in range(nk + 1)],
                                          [pT[:, par, kt, :] for kt in range(nk + 1)], self.ident_bf[:])
                        probs_s[par].release(self.st)
                        t.wait_ge(self.sv.h, self.sv.n)
                        t.wait_ge(self.ss.h, self.ss.n)
                        if hh == 0:
                            opo.acquire(t)
                        tp = (0, 64 * hh)
                        od = opo.b[64 * hh:64 * hh + 64, 0:128]
                        for kt in range(nk + 1):
                            ktabs = T - nk + kt
                            self.st.inc(t.matmul(od, vfull[:, ktabs, 64 * h:64 * h + 64], pT[:, par, kt, :],
                                                 start=(kt == 0), stop=(kt == nk),
                                                 tile_position=tp, skip_group_check=True))
                        pT_s[par].release(self.st)
                        if hh == 1:
                            oT_s.acquire(v)
                            v.wait_ge(self.st.h, self.st.n)
                            self.sv.incx(v, v.tensor_copy(out=oT[:, hp, :], in_=opo.b[:, 0:128]))
                            opo.release(self.sv)

                    from collections import deque
                    pendq = deque()
                    self.sv.wait(t)
                    self.ss.wait(t)
                    for h in range(8):
                        bp, pb = 64 * (h % 2), h // 2
                        par = h % 3
                        mA = self.next_mm()
                        mB = self.next_mm()
                        mA.acquire(t)
                        mB.acquire(t)
                        if nk > 0:
                            self.st.inc(t.matmul(mA.b[:, 0:128 * nk], qT_t[bp:bp + 64, pb, :],
                                                 kT[bp:bp + 64, pb, 128 * (T - nk):128 * T],
                                                 start=True, stop=False, skip_group_check=True))
                            self.st.inc(t.matmul(mA.b[:, 0:128 * nk], self.ident_bf[:],
                                                 maskF[:, 512 - 128 * nk:512],
                                                 start=False, stop=True, skip_group_check=True))
                        self.st.inc(t.matmul(mB.b[:, 0:128], qT_t[bp:bp + 64, pb, :],
                                             kT[bp:bp + 64, pb, 128 * T:128 * T + 128],
                                             start=True, stop=False, skip_group_check=True))
                        self.st.inc(t.matmul(mB.b[:, 0:128], self.ident_bf[:], maskF[:, 512:640],
                                             start=False, stop=True, skip_group_check=True))
                        if len(pendq) >= 2:
                            attn_tail(*pendq.popleft())
                        probs_s[par].acquire(s)
                        self.softmax_psum2(mA.b[:, 0:128 * nk] if nk > 0 else None, mB.b[:, 0:128],
                                           probs_s[par], sm[:, par, :], ncols, probs[:, par, :])
                        mA.release(self.ss)
                        mB.release(self.ss)
                        pendq.append((h, par, self.sv.n))
                    while pendq:
                        attn_tail(*pendq.popleft())
                    for oc in range(2):
                        m = self.next_mm()
                        m.acquire(t)
                        self.sv.wait(t)
                        for fb in range(4):
                            self.st.inc(t.matmul(m.b[:], oT[:, fb, :], wo[:, fb, 512 * oc:512 * oc + 512],
                                                 start=(fb == 0), stop=(fb == 3), skip_group_check=True))
                        self.st.wait(v)
                        self.sv.incw(v, v.tensor_tensor(out=self.X[:, T, 512 * oc:512 * oc + 512],
                                                    in0=m.b[:], in1=self.X[:, T, 512 * oc:512 * oc + 512],
                                                    op=ALU.add))
                        m.release(self.sv)
                    oT_s.release(self.st)
                    vf_s.release(self.st)
                    kTf_s.release(self.st)
                    qT_s.release(self.st)
            if FFN_ON:
                self.stage_B_ffn(b, li)

    def stage_B_ffn(self, b, li):
        nc, t, v, s, g, y = self.nc, self.t, self.v, self.s, self.g, self.y
        with ExitStack() as c:
            E = c.enter_context
            rows_s = E(nc.sbuf_tensor(self.nt("b_rows_s"), [1, SEQ], BF16))
            rows_nm = E(nc.sbuf_tensor(self.nt("b_rows_nm"), [1, SEQ], BF16))
            rstd_all = E(nc.sbuf_tensor(self.nt("b_rstd"), [128, NT], F32))
            lgw1 = E(nc.sbuf_tensor(self.nt("b_lgw1"), [1, HID2], BF16))
            lb1 = E(nc.sbuf_tensor(self.nt("b_lb1"), [1, HID2], BF16))
            b2r = E(nc.sbuf_tensor(self.nt("b_b2r"), [1, D], BF16))
            lglb = E(nc.sbuf_tensor(self.nt("b_lglb"), [128, 8, 2], BF16))
            histat = E(nc.sbuf_tensor(self.nt("b_histat"), [128, 2, 2], F32))
            psub = E(nc.sbuf_tensor(self.nt("b_psub"), [128, 128], F32))
            p127 = E(nc.sbuf_tensor(self.nt("b_p127"), [128, 128], F32))
            stt3 = E(nc.sbuf_tensor(self.nt("b_stt3"), [128, 2], F32))
            bn = self.bn

            self.st.wait(g)
            self.ss.wait(g)
            self.sv.wait(g)
            self.sd.inc(g.dma_start(out=psub[:], in_=self.c_psub), 16)
            self.sd.inc(g.dma_start(out=p127[:], in_=self.c_p127), 16)
            self.sd.inc(g.dma_start(out=lb1[:], in_=self.wf_b1[0:1, li, :]), 16)
            self.sd.inc(g.dma_start(out=b2r[:], in_=self.wf_b2[0:1, li, :]), 16)
            self.sd.wait(s)
            for db in range(8):
                self.ss.incw(s, s.copy(out=lglb[:, db, 0:1], in_=self.wf_lng_sb[:, li, db:db + 1]))
                self.ss.incw(s, s.copy(out=lglb[:, db, 1:2], in_=self.wf_lnb_sb[:, li, db:db + 1]))

            self.sv.incw(v, v.memset(self.XT[:, :, 0:1], 0.0))
            for T in range(NT):
                for db in range(8):
                    m = self.next_mm()
                    m.acquire(t)
                    self.sv.wait(t)
                    self.st.inc(t.transpose(m.b[:, 0:128], self.X[:, T, 128 * db:128 * db + 128],
                                            self.ident_f[:]))
                    self.st.wait(s)
                    self.ss.incx(s, s.activation(out=self.XT[:, db, 1 + 128 * T:1 + 128 * T + 128],
                                             in_=m.b[:, 0:128], func=AF.Identity,
                                             bias=self.zero1[:], scale=self.wf_lng_sb[:, li, db:db + 1]))
                    m.release(self.ss)

            hist_s, stt3_s = Slot(histat), Slot(stt3)
            for T in range(NT):
                self.ss.wait(v)
                self.sv.incw(v, v.bn_stats(bn[:, 0:6], self.X[:, T, 0:512]))
                self.sv.incw(v, v.bn_stats(bn[:, 6:12], self.X[:, T, 512:1024]))
                self.sv.incw(v, v.bn_aggr(bn[:, 12:14], bn[:, 0:6]))
                hist_s.acquire(v)
                self.sv.incw(v, v.bn_aggr(histat[:, T % 2, :], bn[:, 6:12]))
                self.po.acquire(t)
                self.sv.wait(t)
                self.st.inc(t.matmul(self.po.b[:, 0:2], psub[:], histat[:, T % 2, :],
                                     start=True, stop=(T == 0), skip_group_check=True))
                if T > 0:
                    self.st.inc(t.matmul(self.po.b[:, 0:2], p127[:], histat[:, (T - 1) % 2, :],
                                         start=False, stop=True, skip_group_check=True))
                hist_s.release(self.st)
                self.st.wait(v)
                self.sv.incw(v, v.tensor_copy(out=bn[:, 14:16], in_=self.po.b[:, 0:2]))
                self.po.release(self.sv)
                self.ln_combine(bn)
                stt3_s.acquire(s)
                self.powx(stt3[:, 0:1], bn[:, 17:18], self.eps1[:], +0.5)
                self.ss.wait(v)
                self.sv.incw(v, v.reciprocal(out=rstd_all[:, T:T + 1], in_=stt3[:, 0:1]))
                self.sv.incw(v, v.tensor_scalar(out=stt3[:, 1:2], in0=bn[:, 16:17], scalar1=-1.0, scalar2=None,
                                            op0=ALU.mult, op1=ALU.bypass))
                for which in range(2):
                    m = self.next_mm()
                    m.acquire(t)
                    self.sv.wait(t)
                    self.st.inc(t.transpose(m.b[0:1, 0:128], stt3[:, which:which + 1], self.ident_f[:]))
                    self.st.wait(s)
                    dst = rows_s if which == 0 else rows_nm
                    self.ss.incw(s, s.copy(out=dst[0:1, 128 * T:128 * T + 128], in_=m.b[0:1, 0:128]))
                    m.release(self.ss)
                stt3_s.release(self.st)

            w2 = E(nc.sbuf_tensor(self.nt("b_w2"), [128, 22, D], BF16))
            w1c = [E(nc.sbuf_tensor(self.nt("b_w1c"), [128, 8, 512], BF16)) for _ in range(3)]
            hidc = [E(nc.sbuf_tensor(self.nt("b_hidc"), [128, 512], BF16)) for _ in range(3)]
            hTb = E(nc.sbuf_tensor(self.nt("b_hTb"), [128, 3, 4, 128], BF16))
            gel = E(nc.sbuf_tensor(self.nt("b_gel"), [128, 3, 512], BF16))
            w1_s = [Slot(w) for w in w1c]
            hid_s = [Slot(h) for h in hidc]
            self.hidc_cur = hidc
            self.hid_s_cur = hid_s
            hT_s = [Slot(None), Slot(None), Slot(None)]
            gel_s = [Slot(None), Slot(None), Slot(None)]
            self.sy_.wait(g)
            self.st.wait(g)
            self.sd.inc(g.dma_start(out=w2[:], in_=self.wf_w2[li]), 16)
            w2_ready = self.sd.n
            rows_done = [False] * 6
            CH = [(512 * p_, min(512 * p_ + 512, HID)) for p_ in range(6)]
            for sub in range(8):
                tiles = (2 * sub, 2 * sub + 1)
                dacc = [self.next_mm() for _ in range(4)]
                for dm in dacc:
                    dm.acquire(t)
                from collections import deque
                pendq = deque()
                for p_, (a0, a1) in enumerate(CH):
                    wch = a1 - a0
                    ia, ig = (2 * p_) % 3, (2 * p_ + 1) % 3
                    w1_s[ia].acquire(g)
                    self.sd.inc(g.dma_start(out=w1c[ia][:], in_=self.w1_bf.ap()[li, :, :, 2 * p_, :]), 16)
                    w1_s[ig].acquire(g)
                    self.sd.inc(g.dma_start(out=w1c[ig][:], in_=self.w1_bf.ap()[li, :, :, 2 * p_ + 1, :]), 16)
                    ld_n = self.sd.n
                    if sub == 0 and not rows_done[p_]:
                        t.wait_ge(self.sd.h, ld_n)
                        self.ss.wait(t)
                        for which in range(2):
                            m2 = self.po if which == 0 else self.po2
                            m2.acquire(t)
                            for db in range(8):
                                self.st.inc(t.matmul(m2.b[0:1, 0:wch], lglb[:, db, 0:1],
                                                     w1c[ia if which == 0 else ig][:, db, 0:wch],
                                                     start=(db == 0), stop=(db == 7), skip_group_check=True))
                        self.st.wait(v)
                        self.sv.incw(v, v.tensor_copy(out=lgw1[0:1, a0:a0 + wch], in_=self.po.b[0:1, 0:wch]))
                        self.sv.incw(v, v.tensor_copy(out=lgw1[0:1, HID + a0:HID + a0 + wch],
                                                      in_=self.po2.b[0:1, 0:wch]))
                        self.po.release(self.sv)
                        self.po2.release(self.sv)
                        for which in range(2):
                            m2 = self.po if which == 0 else self.po2
                            m2.acquire(t)
                            for db in range(8):
                                self.st.inc(t.matmul(m2.b[0:1, 0:wch], lglb[:, db, 1:2],
                                                     w1c[ia if which == 0 else ig][:, db, 0:wch],
                                                     start=(db == 0), stop=(db == 7), skip_group_check=True))
                        self.st.wait(v)
                        self.sv.incw(v, v.tensor_tensor(out=lb1[0:1, a0:a0 + wch], in0=self.po.b[0:1, 0:wch],
                                                        in1=lb1[0:1, a0:a0 + wch], op=ALU.add))
                        self.sv.incw(v, v.tensor_tensor(out=lb1[0:1, HID + a0:HID + a0 + wch],
                                                        in0=self.po2.b[0:1, 0:wch],
                                                        in1=lb1[0:1, HID + a0:HID + a0 + wch], op=ALU.add))
                        self.po.release(self.sv)
                        self.po2.release(self.sv)
                        rows_done[p_] = True
                    for ti, T in enumerate(tiles):
                        it = (2 * p_ + ti) % 3

                        def hsl2(db, T=T):
                            off = 1 if db < 4 else 0
                            return self.XT[:, db, off + 128 * T: off + 128 * T + 128]
                        self.po.acquire(t)
                        self.po2.acquire(t)
                        t.wait_ge(self.sd.h, ld_n)
                        self.sv.wait(t)
                        self.ss.wait(t)
                        for which, mmx in ((0, self.po), (1, self.po2)):
                            c0 = a0 if which == 0 else HID + a0
                            for db in range(8):
                                self.st.inc(t.matmul(mmx.b[:, 0:wch], hsl2(db),
                                                     w1c[ia if which == 0 else ig][:, db, 0:wch],
                                                     start=(db == 0), stop=False, skip_group_check=True))
                            self.st.inc(t.matmul(mmx.b[:, 0:wch], rows_nm[0:1, 128 * T:128 * T + 128],
                                                 lgw1[0:1, c0:c0 + wch], start=False, stop=False,
                                                 skip_group_check=True))
                            self.st.inc(t.matmul(mmx.b[:, 0:wch], rows_s[0:1, 128 * T:128 * T + 128],
                                                 lb1[0:1, c0:c0 + wch], start=False, stop=True,
                                                 skip_group_check=True))
                        st_w1 = self.st.n
                        if len(pendq) > 1:
                            self.emit_w2_part(*pendq.popleft(), dacc, w2, hTb, hT_s)
                        gel_s[it].acquire(s)
                        s.wait_ge(self.st.h, st_w1)
                        self.ss.incx(s, s.activation(out=gel[:, it, 0:wch], in_=self.po2.b[:, 0:wch],
                                                     func=AF.Gelu, bias=self.zero1[:],
                                                     scale=rstd_all[:, T:T + 1]))
                        self.po2.release(self.ss)
                        hid_s[it].acquire(v)
                        self.ss.wait(v)
                        self.sv.incx(v, v.scalar_tensor_tensor(out=hidc[it][:, 0:wch], in0=self.po.b[:, 0:wch],
                                                               scalar=rstd_all[:, T:T + 1], in1=gel[:, it, 0:wch],
                                                               op0=ALU.mult, op1=ALU.mult))
                        self.po.release(self.sv)
                        gel_s[it].release(self.sv)
                        pendq.append((it, p_, ti, wch))
                    w1_s[ia].release(self.st)
                    w1_s[ig].release(self.st)
                while pendq:
                    self.emit_w2_part(*pendq.popleft(), dacc, w2, hTb, hT_s)
                for ti, T in enumerate(tiles):
                    for oc in range(2):
                        dm = dacc[2 * ti + oc]
                        self.st.inc(t.matmul(dm.b[:], self.ones_bf[:], b2r[0:1, 512 * oc:512 * oc + 512],
                                             start=False, stop=True, skip_group_check=True))
                        self.st.wait(v)
                        self.sv.incw(v, v.tensor_tensor(out=self.X[:, T, 512 * oc:512 * oc + 512],
                                                        in0=dm.b[:], in1=self.X[:, T, 512 * oc:512 * oc + 512],
                                                        op=ALU.add))
                        dm.release(self.sv)

    def stage_C(self, b):
        nc, t, v, s, g, y = self.nc, self.t, self.v, self.s, self.g, self.y
        with ExitStack() as c:
            E = c.enter_context
            x1t = E(nc.sbuf_tensor(self.nt("c_x1"), [128, 4, D], F32))
            self.bn = E(nc.sbuf_tensor(self.nt("c_bn"), [128, 24], F32))
            x1_s = Slot(x1t)
            self.sv.wait(y)
            self.ss.wait(y)
            self.st.wait(y)
            for tc in range(4):
                x1_s.acquire(y)
                self.sy_.inc(y.dma_start(out=x1t[:], in_=self.x1_spill.ap()[:, 4 * tc:4 * tc + 4, :]), 16)
                self.sy_.wait(v)
                for tt in range(4):
                    T = 4 * tc + tt
                    self.ss.wait(v)
                    self.sv.incw(v, v.tensor_tensor(out=self.X[:, T, :], in0=self.X[:, T, :],
                                                in1=x1t[:, tt, :], op=ALU.add))
                    self.ln_apply_tile(T)
                x1_s.release(self.sv)

    def stage_D(self, b):
        nc, t, v, s, g, y = self.nc, self.t, self.v, self.s, self.g, self.y
        with ExitStack() as c:
            E = c.enter_context
            self.bn = E(nc.sbuf_tensor(self.nt("d_bn"), [128, 24], F32))
            self.xbf_s = Slot(E(nc.sbuf_tensor(self.nt("d_xbf"), [128, D], BF16)))
            qT_pad = E(nc.sbuf_tensor(self.nt("d_qT"), [128, 4, SEQ], BF16))
            kT_pad = E(nc.sbuf_tensor(self.nt("d_kT"), [128, 4, SEQ], BF16))
            v65 = E(nc.sbuf_tensor(self.nt("d_v65"), [128, NT, 16, 65], BF16))
            zz = E(nc.sbuf_tensor(self.nt("d_zz"), [128, 2], F32))
            eu = E(nc.sbuf_tensor(self.nt("d_eu"), [128, D], F32))
            eu_s = Slot(eu)
            qTp_s, kTp_s, v65_s = Slot(qT_pad), Slot(kT_pad), Slot(v65)

            self.st.wait(g)
            self.ss.wait(g)
            self.sv.wait(g)
            with ExitStack() as dproj:
                E1 = dproj.enter_context
                gb = E1(nc.sbuf_tensor(self.nt("d_gb"), [128, 4, 16], F32))
                gbrow = E1(nc.sbuf_tensor(self.nt("d_gbrow"), [1, 128], F32))
                qf = E1(nc.sbuf_tensor(self.nt("d_qf"), [128, 256], F32))
                q2 = E1(nc.sbuf_tensor(self.nt("d_q2"), [128, 256], F32))
                fmst = E1(nc.sbuf_tensor(self.nt("d_fmst"), [128, 4, 16], F32))
                qpad = E1(nc.sbuf_tensor(self.nt("d_qpad"), [128, 16, 32], BF16))
                wq_sb = E1(nc.sbuf_tensor(self.nt("d_wq"), [128, 8, 256], BF16))
                wk_sb = E1(nc.sbuf_tensor(self.nt("d_wk"), [128, 8, 256], BF16))
                wv_sb = E1(nc.sbuf_tensor(self.nt("d_wv"), [128, 8, D], BF16))
                self.sd.inc(g.dma_start(out=wq_sb[:], in_=self.w1_qT), 16)
                self.sd.inc(g.dma_start(out=wk_sb[:], in_=self.w1_kT), 16)
                self.sd.inc(g.dma_start(out=wv_sb[:], in_=self.w1_vT), 16)
                self.sd.inc(g.dma_start(out=gbrow[0:1, 0:16], in_=self.w1_g), 16)
                self.sd.inc(g.dma_start(out=gbrow[0:1, 16:32], in_=self.w1_b), 16)
                m = self.next_mm()
                m.acquire(t)
                t.wait_ge(self.sd.h, self.sd.n)
                self.st.inc(t.matmul(m.b[:, 0:32], self.ones_f[:], gbrow[0:1, 0:32], start=True, stop=True,
                                     skip_group_check=True))
                self.st.wait(v)
                self.sv.incw(v, v.tensor_scalar(out=gb[:, 0, :], in0=m.b[:, 0:16], scalar1=0.25, scalar2=None,
                                            op0=ALU.mult, op1=ALU.bypass))
                self.sv.incw(v, v.tensor_scalar(out=gb[:, 1, :], in0=m.b[:, 16:32], scalar1=0.25, scalar2=None,
                                            op0=ALU.mult, op1=ALU.bypass))
                self.sv.incw(v, v.tensor_copy(out=gb[:, 2, :], in_=m.b[:, 0:16]))
                self.sv.incw(v, v.tensor_copy(out=gb[:, 3, :], in_=m.b[:, 16:32]))
                m.release(self.sv)
                self.sv.incw(v, v.memset(qpad[:], 0.0))
                self.sv.incw(v, v.memset(v65[:], 1.0))

                qf_s, q2_s, qp_s = Slot(qf), Slot(q2), Slot(qpad)

                for T in range(NT):
                    self.lazy_xt(T)
                    for which in range(2):
                        wsb = wq_sb if which == 0 else wk_sb
                        m = self.next_mm()
                        m.acquire(t)
                        self.sv.wait(t)
                        for db in range(8):
                            self.st.inc(t.matmul(m.b[:, 0:256], self.XT[:, db, 1 + 128 * T:1 + 128 * T + 128],
                                                 wsb[:, db, :], start=(db == 0), stop=(db == 7),
                                                 skip_group_check=True))
                        qf_s.acquire(v)
                        self.st.wait(v)
                        self.sv.incw(v, v.tensor_copy(out=qf[:], in_=m.b[:, 0:256]))
                        m.release(self.sv)
                        q3 = qf[:].rearrange("p (h f) -> p h f", h=16)
                        q2_s.acquire(s)
                        self.sv.wait(s)
                        self.ss.incw(s, s.square(out=q2[:], in_=qf[:]))
                        q23 = q2[:].rearrange("p (h f) -> p h f", h=16)
                        self.ss.wait(v)
                        self.sv.incw(v, v.reduce_sum(out=fmst[:, 0, :], in_=q3, axis=AX.X))
                        self.sv.incw(v, v.reduce_sum(out=fmst[:, 1, :], in_=q23, axis=AX.X))
                        q2_s.release(self.sv)
                        self.sv.incw(v, v.tensor_scalar(out=fmst[:, 0, :], in0=fmst[:, 0, :], scalar1=1.0 / 16,
                                                    scalar2=None, op0=ALU.mult, op1=ALU.bypass))
                        self.sv.incw(v, v.tensor_scalar(out=fmst[:, 1, :], in0=fmst[:, 1, :], scalar1=1.0 / 16,
                                                    scalar2=None, op0=ALU.mult, op1=ALU.bypass))
                        self.sv.incw(v, v.tensor_tensor(out=fmst[:, 2, :], in0=fmst[:, 0, :], in1=fmst[:, 0, :],
                                                    op=ALU.mult))
                        self.sv.incw(v, v.tensor_tensor(out=fmst[:, 2, :], in0=fmst[:, 1, :], in1=fmst[:, 2, :],
                                                    op=ALU.subtract))
                        self.powx(fmst[:, 3, :], fmst[:, 2, :], self.eps1[:], -0.5)
                        mu_b = bcast_last(fmst[:, 0, :].rearrange("p (h a) -> p h a", a=1), 16)
                        rs_b = bcast_last(fmst[:, 3, :].rearrange("p (h a) -> p h a", a=1), 16)
                        g_b = bcast_mid(gb[:, 0 if which == 0 else 2, :], 16)
                        b_b = bcast_mid(gb[:, 1 if which == 0 else 3, :], 16)
                        self.ss.wait(v)
                        self.sv.incw(v, v.tensor_tensor(out=q3, in0=q3, in1=mu_b, op=ALU.subtract))
                        self.sv.incw(v, v.tensor_tensor(out=q3, in0=q3, in1=rs_b, op=ALU.mult))
                        self.sv.incw(v, v.tensor_tensor(out=q3, in0=q3, in1=g_b, op=ALU.mult))
                        qp_s.acquire(v)
                        self.sv.incw(v, v.tensor_tensor(out=qpad[:, :, 0:16], in0=q3, in1=b_b, op=ALU.add))
                        qf_s.release(self.sv)
                        qpf = qpad[:].rearrange("p h f -> p (h f)")
                        dstT = qT_pad if which == 0 else kT_pad
                        (qTp_s if which == 0 else kTp_s).acquire(v)
                        (qTp_s if which == 0 else kTp_s).acquire(s)
                        self.sv.wait(t)
                        self.transpose_to([qpf[:, 128 * cc:128 * cc + 128] for cc in range(4)],
                                          [dstT[:, cc, 128 * T:128 * T + 128] for cc in range(4)],
                                          self.ident_bf[:])
                        qp_s.release(self.st)
                    v65_s.acquire(v)
                    for vc in range(2):
                        m = self.next_mm()
                        m.acquire(t)
                        self.sv.wait(t)
                        for db in range(8):
                            self.st.inc(t.matmul(m.b[:], self.XT[:, db, 1 + 128 * T:1 + 128 * T + 128],
                                                 wv_sb[:, db, 512 * vc:512 * vc + 512],
                                                 start=(db == 0), stop=(db == 7), skip_group_check=True))
                        self.st.wait(v)
                        dst = v65[:, T, 8 * vc:8 * vc + 8, 0:64]
                        self.sv.incw(v, v.tensor_copy(out=dst, in_=m.b[:].rearrange("p (h f) -> p h f", h=8)))
                        m.release(self.sv)

            s2T = E(nc.sbuf_tensor(self.nt("d_s2T"), [128, 3, 512], BF16))
            triu = E(nc.sbuf_tensor(self.nt("d_triu"), [128, 128], BF16))
            onorm = E(nc.sbuf_tensor(self.nt("d_on"), [128, 4, 16, 64], BF16))
            oev = E(nc.sbuf_tensor(self.nt("d_oev"), [128, 4, 65], F32))
            rz4 = E(nc.sbuf_tensor(self.nt("d_rz4"), [128, 4, 1], F32))
            oTd = E(nc.sbuf_tensor(self.nt("d_oT"), [128, 8, 128], BF16))
            wo_sb = E(nc.sbuf_tensor(self.nt("d_wo"), [128, 8, D], BF16))
            s2_s = [Slot(None), Slot(None), Slot(None)]
            on_s, oT_s, oev_s = Slot(onorm), Slot(oTd), Slot(oev)
            self.st.wait(g)
            self.sd.inc(g.dma_start(out=triu[:], in_=self.c_triu), 16)
            self.sd.inc(g.dma_start(out=wo_sb[:], in_=self.w1_oT), 16)
            self.sd.wait(t)
            for qc in range(4):
                on_s.acquire(v)
                self.sv.wait(t)
                self.ss.wait(t)
                for h in range(16):
                    bp, pb = 32 * (h % 4), h // 4
                    nkt = 4 * qc + 4
                    opo = [self.mm[j] for j in range(4)]
                    for j in range(4):
                        opo[j].acquire(t)
                    def omm_d(kt, was_diag, used_v=False):
                        sb_i = kt % 3
                        if was_diag or used_v:
                            self.sv.wait(t)
                        else:
                            self.ss.wait(t)
                        for j in range(4):
                            qtile = 4 * qc + j
                            if kt > qtile:
                                continue
                            self.st.inc(t.matmul(opo[j].b[:, 0:65], s2T[:, sb_i, 128 * j:128 * j + 128],
                                                 v65[:, kt, h, :], start=(kt == 0), stop=(kt == qtile),
                                                 skip_group_check=True))
                        s2_s[sb_i].release(self.st)

                    from collections import deque
                    pq = deque()
                    for kt in range(nkt):
                        sps = self.po if kt % 2 == 0 else self.po2
                        sps.acquire(t)
                        self.st.inc(t.matmul(sps.b[:], kT_pad[bp:bp + 32, pb, 128 * kt:128 * kt + 128],
                                             qT_pad[bp:bp + 32, pb, 512 * qc:512 * qc + 512],
                                             start=True, stop=True, tile_position=(bp, 0),
                                             skip_group_check=True))
                        sb_i = kt % 3
                        jd = kt - 4 * qc
                        is_diag = 0 <= jd < 4
                        c0 = 128 * jd if is_diag else 0
                        s2_s[sb_i].acquire(s)
                        self.st.wait(s)
                        self.ss.incx(s, s.square(out=s2T[:, sb_i, c0:512], in_=sps.b[:, c0:512]))
                        sps.release(self.ss)
                        if is_diag:
                            self.ss.wait(v)
                            self.sv.incx(v, v.tensor_tensor(out=s2T[:, sb_i, 128 * jd:128 * jd + 128],
                                                            in0=s2T[:, sb_i, 128 * jd:128 * jd + 128],
                                                            in1=triu[:], op=ALU.mult))
                        pq.append((kt, is_diag, False))
                        if len(pq) > 2:
                            omm_d(*pq.popleft())
                    while pq:
                        omm_d(*pq.popleft())
                    oev_s.acquire(v)
                    self.st.wait(v)
                    for j in range(4):
                        if j < 3:
                            self.sv.incx(v, v.tensor_copy(out=oev[:, j, :], in_=opo[j].b[:, 0:65]))
                        else:
                            self.sv.incw(v, v.tensor_copy(out=oev[:, j, :], in_=opo[j].b[:, 0:65]))
                        opo[j].release(self.sv)
                    self.sv.incw(v, v.tensor_scalar(out=rz4[:], in0=oev[:, :, 64:65], scalar1=1e-5,
                                                    scalar2=None, op0=ALU.add, op1=ALU.bypass))
                    self.sv.incw(v, v.reciprocal(out=rz4[:], in_=rz4[:]))
                    self.sv.incw(v, v.tensor_tensor(out=onorm[:, :, h, :], in0=oev[:, :, 0:64],
                                                    in1=bcast_last(rz4[:], 64), op=ALU.mult))
                    oev_s.release(self.sv)
                for j in range(4):
                    T = 4 * qc + j
                    onf = onorm[:, j].rearrange("p h f -> p (h f)")
                    oT_s.acquire(v)
                    oT_s.acquire(s)
                    self.sv.wait(t)
                    self.transpose_to([onf[:, 128 * cc:128 * cc + 128] for cc in range(8)],
                                      [oTd[:, cc, :] for cc in range(8)], self.ident_bf[:])
                    for oc in range(2):
                        m = self.next_mm()
                        m.acquire(t)
                        self.sv.wait(t)
                        self.ss.wait(t)
                        for fb in range(8):
                            self.st.inc(t.matmul(m.b[:], oTd[:, fb, :], wo_sb[:, fb, 512 * oc:512 * oc + 512],
                                                 start=(fb == 0), stop=(fb == 7), skip_group_check=True))
                        self.st.wait(v)
                        self.sv.incw(v, v.tensor_tensor(out=self.X[:, T, 512 * oc:512 * oc + 512],
                                                        in0=m.b[:], in1=self.X[:, T, 512 * oc:512 * oc + 512],
                                                        op=ALU.add))
                        m.release(self.sv)
                    oT_s.release(self.st)
                    self.ln_apply_tile(T)
                on_s.release(self.st)

    def stage_D_eprep(self, b):
        nc, t, v, s, g, y = self.nc, self.t, self.v, self.s, self.g, self.y
        with ExitStack() as c:
            E = c.enter_context
            self.bn = E(nc.sbuf_tensor(self.nt("de_bn"), [128, 24], F32))
            gbc = E(nc.sbuf_tensor(self.nt("de_gbc"), [128, 2, D], F32))
            eu = E(nc.sbuf_tensor(self.nt("de_eu"), [128, D], F32))
            zz = E(nc.sbuf_tensor(self.nt("de_zz"), [128, 2], F32))
            self.sv.wait(g)
            self.ss.wait(g)
            self.st.wait(g)
            self.build_gbc(gbc)
            eu_s = Slot(eu)
            bn = self.bn
            for T in range(NT):
                self.ss.wait(v)
                self.sv.incw(v, v.bn_stats(bn[:, 0:6], self.X[:, T, 0:512]))
                self.sv.incw(v, v.bn_stats(bn[:, 6:12], self.X[:, T, 512:1024]))
                self.sv.incw(v, v.bn_aggr(bn[:, 12:14], bn[:, 0:6]))
                self.sv.incw(v, v.bn_aggr(bn[:, 14:16], bn[:, 6:12]))
                self.ln_combine(bn)
                rstd3 = self.stats_e[:, b, T, 0:1]
                nmr3 = self.stats_e[:, b, T, 1:2]
                self.powx(rstd3, bn[:, 17:18], self.eps1[:], -0.5)
                self.ss.wait(v)
                self.sv.incw(v, v.tensor_tensor(out=nmr3, in0=bn[:, 16:17], in1=rstd3, op=ALU.mult))
                self.sv.incw(v, v.tensor_scalar(out=nmr3, in0=nmr3, scalar1=-1.0, scalar2=None,
                                            op0=ALU.mult, op1=ALU.bypass))
                eu_s.acquire(s)
                self.sv.wait(s)
                self.ss.incw(s, s.activation(out=eu[:], in_=self.X[:, T, :], func=AF.Identity,
                                         bias=nmr3, scale=rstd3))
                self.ss.wait(v)
                self.sv.incw(v, v.tensor_tensor(out=eu[:], in0=eu[:], in1=gbc[:, 0, :], op=ALU.mult))
                self.sv.incw(v, v.tensor_tensor(out=eu[:], in0=eu[:], in1=gbc[:, 1, :], op=ALU.add))
                self.sv.incw(v, v.reduce_max(out=zz[:, 0:1], in_=eu[:], axis=AX.X, apply_absolute_value=True))
                eu_s.release(self.sv)
                self.sv.incw(v, v.tensor_tensor(out=self.gmax[:], in0=self.gmax[:], in1=zz[:, 0:1], op=ALU.max))

    def stage_E_all(self):
        nc, t, v, s, g, y = self.nc, self.t, self.v, self.s, self.g, self.y
        with ExitStack() as c:
            E = c.enter_context
            self.bn = E(nc.sbuf_tensor(self.nt("e_bn"), [128, 24], F32))
            wtf = E(nc.sbuf_tensor(self.nt("e_wtf"), [128, 8, D], F32))
            absc = E(nc.sbuf_tensor(self.nt("e_absc"), [128, D], F32))
            maskc = E(nc.sbuf_tensor(self.nt("e_maskc"), [128, D], F32))
            sgn = E(nc.sbuf_tensor(self.nt("e_sgn"), [128, D], F32))
            sums = E(nc.sbuf_tensor(self.nt("e_sums"), [128, 8], F32))
            sm2 = E(nc.sbuf_tensor(self.nt("e_sm2"), [128, 2], F32))
            smt = E(nc.sbuf_tensor(self.nt("e_smt"), [2, 1], F32))
            al = E(nc.sbuf_tensor(self.nt("e_al"), [1, 4], F32))
            alb = E(nc.sbuf_tensor(self.nt("e_alb"), [128, 2], F32))
            u = E(nc.sbuf_tensor(self.nt("e_u"), [128, D], F32))
            xq = E(nc.sbuf_tensor(self.nt("e_xq"), [128, D], BF16))
            xqT = E(nc.sbuf_tensor(self.nt("e_xqT"), [128, 8, 128], BF16))
            arr = E(nc.sbuf_tensor(self.nt("e_arr"), [1, 128], F32))
            tmpc = E(nc.sbuf_tensor(self.nt("e_tmpc"), [128, 1], F32))
            self.wq_ns = E(nc.sbuf_tensor(self.nt("wq_ns"), [128, 8, D], BF16))

            self.st.wait(g)
            self.ss.wait(g)
            self.sv.wait(g)
            self.gbc = E(nc.sbuf_tensor(self.nt("e_gbc"), [128, 2, D], F32))
            self.build_gbc(self.gbc)
            self.sd.inc(g.dma_start(out=wtf[:], in_=self.we_wT), 16)
            self.sd.wait(s)
            self.sd.wait(v)

            for db in range(8):
                self.sv.wait(s)
                self.ss.incw(s, s.activation(out=absc[:], in_=wtf[:, db, :], func=AF.Abs,
                                         bias=self.zero1[:], scale=1.0))
                self.ss.wait(v)
                self.sv.incw(v, v.reduce_sum(out=sums[:, db:db + 1], in_=absc[:], axis=AX.X))
            self.sv.incw(v, v.reduce_sum(out=sm2[:, 0:1], in_=sums[:], axis=AX.X))
            m = self.next_mm()
            m.acquire(t)
            self.sv.wait(t)
            self.st.inc(t.transpose(m.b[0:1, 0:128], sm2[:, 0:1], self.ident_f[:]))
            self.st.wait(v)
            self.sv.incw(v, v.reduce_sum(out=al[0:1, 0:1], in_=m.b[0:1, 0:128], axis=AX.X))
            m.release(self.sv)
            self.sv.incw(v, v.tensor_scalar(out=al[0:1, 0:1], in0=al[0:1, 0:1],
                                        scalar1=0.7 / (1024.0 * 1024.0), scalar2=None,
                                        op0=ALU.mult, op1=ALU.bypass))
            m = self.next_mm()
            m.acquire(t)
            self.sv.wait(t)
            self.st.inc(t.matmul(m.b[:, 0:1], self.ones_f[:], al[0:1, 0:1], start=True, stop=True,
                                 skip_group_check=True))
            self.st.wait(v)
            self.sv.incw(v, v.tensor_copy(out=alb[:, 0:1], in_=m.b[:, 0:1]))
            m.release(self.sv)
            self.sv.incw(v, v.memset(sm2[:], 0.0))
            for db in range(8):
                self.sv.wait(s)
                self.ss.incw(s, s.activation(out=absc[:], in_=wtf[:, db, :], func=AF.Abs,
                                         bias=self.zero1[:], scale=1.0))
                self.ss.wait(v)
                self.sv.incw(v, v.tensor_scalar(out=maskc[:], in0=absc[:], scalar1=alb[:, 0:1], scalar2=None,
                                            op0=ALU.is_gt, op1=ALU.bypass))
                self.sv.incw(v, v.scalar_tensor_tensor(out=absc[:], in0=absc[:], scalar=1.0, in1=maskc[:],
                                                   op0=ALU.mult, op1=ALU.mult, accum_out=tmpc[:]))
                self.sv.incw(v, v.tensor_tensor(out=sm2[:, 0:1], in0=sm2[:, 0:1], in1=tmpc[:], op=ALU.add))
                self.sv.incw(v, v.tensor_scalar(out=maskc[:], in0=maskc[:], scalar1=1.0, scalar2=0.0,
                                            op0=ALU.mult, op1=ALU.add, accum_out=tmpc[:]))
                self.sv.incw(v, v.tensor_tensor(out=sm2[:, 1:2], in0=sm2[:, 1:2], in1=tmpc[:], op=ALU.add))
            m = self.next_mm()
            m.acquire(t)
            self.sv.wait(t)
            self.st.inc(t.transpose(m.b[0:2, 0:128], sm2[:], self.ident_f[:]))
            self.st.wait(v)
            self.sv.incw(v, v.reduce_sum(out=smt[0:2, 0:1], in_=m.b[0:2, 0:128], axis=AX.X))
            m.release(self.sv)
            m = self.next_mm()
            m.acquire(t)
            self.sv.wait(t)
            self.st.inc(t.transpose(m.b[0:1, 0:2], smt[0:2, 0:1], self.ident_f[0:2, 0:2]))
            self.st.wait(v)
            self.sv.incw(v, v.tensor_scalar(out=al[0:1, 1:2], in0=m.b[0:1, 1:2], scalar1=1.0,
                                        scalar2=None, op0=ALU.max, op1=ALU.bypass))
            self.sv.incw(v, v.reciprocal(out=al[0:1, 1:2], in_=al[0:1, 1:2]))
            self.sv.incw(v, v.tensor_tensor(out=al[0:1, 2:3], in0=m.b[0:1, 0:1], in1=al[0:1, 1:2],
                                        op=ALU.mult))
            m.release(self.sv)
            m = self.next_mm()
            m.acquire(t)
            self.sv.wait(t)
            self.st.inc(t.matmul(m.b[:, 0:1], self.ones_f[:], al[0:1, 2:3], start=True, stop=True,
                                 skip_group_check=True))
            self.st.wait(v)
            self.sv.incw(v, v.tensor_copy(out=alb[:, 1:2], in_=m.b[:, 0:1]))
            m.release(self.sv)
            for db in range(8):
                self.sv.wait(s)
                self.ss.incw(s, s.activation(out=absc[:], in_=wtf[:, db, :], func=AF.Abs,
                                         bias=self.zero1[:], scale=1.0))
                self.ss.incw(s, s.sign(out=sgn[:], in_=wtf[:, db, :]))
                self.ss.wait(v)
                self.sv.incw(v, v.tensor_scalar(out=maskc[:], in0=absc[:], scalar1=alb[:, 0:1], scalar2=None,
                                            op0=ALU.is_gt, op1=ALU.bypass))
                self.sv.incw(v, v.tensor_tensor(out=sgn[:], in0=sgn[:], in1=maskc[:], op=ALU.mult))
                self.sv.incw(v, v.tensor_scalar(out=self.wq_ns[:, db, :], in0=sgn[:], scalar1=alb[:, 1:2],
                                            scalar2=None, op0=ALU.mult, op1=ALU.bypass))

            m = self.next_mm()
            m.acquire(t)
            self.sv.wait(t)
            self.st.inc(t.transpose(m.b[0:1, 0:128], self.gmax[:], self.ident_f[:]))
            self.st.wait(v)
            self.sv.incw(v, v.memset(arr[:], 0.0))
            self.sv.incw(v, v.reduce_max(out=arr[0:1, 0:1], in_=m.b[0:1, 0:128], axis=AX.X))
            m.release(self.sv)
            self.sv.wait(g)
            self.sd.inc(g.dma_start(out=self.ar_in.ap(), in_=arr[:]), 16)
            self.sd.wait(g)
            self.scc.inc(g.collective_compute("AllReduce", ALU.max, replica_groups=[list(range(self.cc_world))],
                                              ins=[self.ar_in.ap().opt()], outs=[self.ar_out.ap().opt()]))
            self.scc.wait(g)
            self.sd.inc(g.dma_start(out=arr[0:1, 0:1], in_=self.ar_out.ap()[0:1, 0:1]), 16)
            self.sd.wait(v)
            self.sv.incw(v, v.tensor_scalar(out=al[0:1, 3:4], in0=arr[0:1, 0:1], scalar1=1e-8,
                                        scalar2=1.0 / 127.0, op0=ALU.max, op1=ALU.mult))
            m = self.next_mm()
            m.acquire(t)
            self.sv.wait(t)
            self.st.inc(t.matmul(m.b[:, 0:1], self.ones_f[:], al[0:1, 3:4], start=True, stop=True,
                                 skip_group_check=True))
            self.st.wait(v)
            self.sv.incw(v, v.tensor_copy(out=self.scale_b[:], in_=m.b[:, 0:1]))
            m.release(self.sv)
            self.sv.incw(v, v.reciprocal(out=self.iscale_b[:], in_=self.scale_b[:]))

            u_s, xq_s, xqT_s = Slot(u), Slot(xq), Slot(xqT)
            for b in range(NB - 1, -1, -1):
                if b != NB - 1:
                    self.sv.wait(y)
                    self.ss.wait(y)
                    self.st.wait(y)
                    for tc in range(4):
                        self.sy_.inc(y.dma_start(out=self.X[:, 4 * tc:4 * tc + 4, :],
                                                 in_=self.x3_spill.ap()[:, 4 * tc:4 * tc + 4, :]), 16)
                    self.sy_.wait(v)
                    self.sy_.wait(s)
                for T in range(NT):
                    st_ap = self.stats_e[:, b, T, :]
                    u_s.acquire(s)
                    self.sv.wait(s)
                    self.ss.incw(s, s.activation(out=u[:], in_=self.X[:, T, :], func=AF.Identity,
                                             bias=st_ap[:, 1:2], scale=st_ap[:, 0:1]))
                    self.ss.wait(v)
                    self.sv.incw(v, v.tensor_tensor(out=u[:], in0=u[:], in1=self.gbc[:, 0, :], op=ALU.mult))
                    self.sv.incw(v, v.tensor_tensor(out=u[:], in0=u[:], in1=self.gbc[:, 1, :], op=ALU.add))
                    self.sv.incw(v, v.tensor_scalar(out=u[:], in0=u[:], scalar1=self.iscale_b[:, 0:1],
                                                scalar2=None, op0=ALU.mult, op1=ALU.bypass))
                    self.sv.incw(v, v.tensor_scalar(out=u[:], in0=u[:], scalar1=127.0, scalar2=-127.0,
                                                op0=ALU.min, op1=ALU.max))
                    xq_s.acquire(s)
                    self.sv.wait(s)
                    self.ss.incw(s, s.activation(out=u[:], in_=u[:], func=AF.Identity,
                                             bias=self.magic[:], scale=1.0))
                    self.ss.incw(s, s.activation(out=xq[:], in_=u[:], func=AF.Identity,
                                             bias=self.nmagic[:], scale=1.0))
                    u_s.release(self.ss)
                    xqT_s.acquire(v)
                    xqT_s.acquire(s)
                    self.ss.wait(t)
                    self.transpose_to([xq[:, 128 * db:128 * db + 128] for db in range(8)],
                                      [xqT[:, db, :] for db in range(8)], self.ident_bf[:])
                    xq_s.release(self.st)
                    for oc in range(2):
                        m = self.next_mm()
                        m.acquire(t)
                        self.sv.wait(t)
                        self.ss.wait(t)
                        for db in range(8):
                            self.st.inc(t.matmul(m.b[:], xqT[:, db, :],
                                                 self.wq_ns[:, db, 512 * oc:512 * oc + 512],
                                                 start=(db == 0), stop=(db == 7), skip_group_check=True))
                        self.st.wait(v)
                        self.sv.incw(v, v.scalar_tensor_tensor(out=self.X[:, T, 512 * oc:512 * oc + 512],
                                                           in0=m.b[:], scalar=self.scale_b[:, 0:1],
                                                           in1=self.X[:, T, 512 * oc:512 * oc + 512],
                                                           op0=ALU.mult, op1=ALU.add))
                        m.release(self.sv)
                    xqT_s.release(self.st)
                    self.ln_apply_tile(T)
                    self.ss.wait(y)
                    self.sy_.inc(y.dma_start(out=self.out[b, :, T, :], in_=self.X[:, T, :]), 16)


def _host_prep(inputs):
    f32 = np.float32
    x = np.ascontiguousarray(inputs["L0_attn_x"], dtype=f32)

    def tokmaj(xb):
        return np.ascontiguousarray(xb.reshape(NT, 128, D).transpose(1, 0, 2))

    def feamaj(xb):
        return np.ascontiguousarray(xb.T.reshape(8, 128, SEQ).transpose(1, 0, 2))

    sh = {}
    sh["wa_qkvT"] = np.ascontiguousarray(
        np.asarray(inputs["L0_attn_Wqkv"], f32).T.reshape(8, 128, 3072).transpose(1, 0, 2))
    sh["wa_outT"] = np.ascontiguousarray(
        np.asarray(inputs["L0_attn_Wout"], f32).T.reshape(8, 128, D).transpose(1, 0, 2))
    bqkv = np.asarray(inputs["L0_attn_bqkv"], f32)
    sh["ba_qkv"] = np.ascontiguousarray(bqkv[:2048].reshape(16, 128).T)
    sh["ba_bv"] = np.ascontiguousarray(bqkv[2048:].reshape(8, 128).T)
    sh["ba_bout"] = np.asarray(inputs["L0_attn_bout"], f32)[None, :]
    sh["wf_qkv"] = np.ascontiguousarray(
        np.asarray(inputs["L0_ffn_wqkv"], f32).reshape(2, 8, 128, 1536).transpose(0, 2, 1, 3))
    sh["wf_wo"] = np.ascontiguousarray(
        np.asarray(inputs["L0_ffn_wo"], f32).reshape(2, 4, 128, D).transpose(0, 2, 1, 3))
    sh["wf_w1"] = np.ascontiguousarray(
        np.asarray(inputs["L0_ffn_w1"], f32).reshape(2, 8, 128, HID2).transpose(0, 2, 1, 3))
    w2p = np.zeros((2, 22 * 128, D), dtype=f32)
    w2p[:, :HID, :] = np.asarray(inputs["L0_ffn_w2"], f32)
    sh["wf_w2"] = np.ascontiguousarray(w2p.reshape(2, 22, 128, D).transpose(0, 2, 1, 3))
    sh["wf_qs"] = np.ascontiguousarray(np.asarray(inputs["L0_ffn_q_scale"], f32)[None])
    sh["wf_ks"] = np.ascontiguousarray(np.asarray(inputs["L0_ffn_k_scale"], f32)[None])
    sh["wf_lng"] = np.ascontiguousarray(
        np.asarray(inputs["L0_ffn_ln_g"], f32).reshape(2, 8, 128).transpose(2, 0, 1))
    sh["wf_lnb"] = np.ascontiguousarray(
        np.asarray(inputs["L0_ffn_ln_b"], f32).reshape(2, 8, 128).transpose(2, 0, 1))
    sh["wf_b1"] = np.ascontiguousarray(np.asarray(inputs["L0_ffn_b1"], f32)[None])
    sh["wf_b2"] = np.ascontiguousarray(np.asarray(inputs["L0_ffn_b2"], f32)[None])
    sh["w1_qT"] = np.ascontiguousarray(
        np.asarray(inputs["L1_attn_Wq"], f32).T.reshape(8, 128, 256).transpose(1, 0, 2))
    sh["w1_kT"] = np.ascontiguousarray(
        np.asarray(inputs["L1_attn_Wk"], f32).T.reshape(8, 128, 256).transpose(1, 0, 2))
    sh["w1_vT"] = np.ascontiguousarray(
        np.asarray(inputs["L1_attn_Wv"], f32).T.reshape(8, 128, D).transpose(1, 0, 2))
    sh["w1_oT"] = np.ascontiguousarray(
        np.asarray(inputs["L1_attn_Wo"], f32).T.reshape(8, 128, D).transpose(1, 0, 2))
    sh["w1_g"] = np.asarray(inputs["L1_attn_gamma"], f32)[None]
    sh["w1_b"] = np.asarray(inputs["L1_attn_beta"], f32)[None]
    sh["we_wT"] = np.ascontiguousarray(
        np.asarray(inputs["L1_ffn_weight"], f32).T.reshape(8, 128, D).transpose(1, 0, 2))
    sh["we_g"] = np.asarray(inputs["L1_ffn_gamma"], f32)[None]
    sh["we_b"] = np.asarray(inputs["L1_ffn_beta"], f32)[None]

    sh["c_ident_bf"] = np.eye(128, dtype=BF)
    sh["c_ident_f"] = np.eye(128, dtype=f32)
    sh["c_ones_bf"] = np.ones((1, 128), dtype=BF)
    sh["c_ones_f"] = np.ones((1, 128), dtype=f32)
    p = np.arange(128)[:, None]
    c256 = np.arange(256)[None, :]
    sh["c_maskA"] = np.where((c256 > p + 64) & (c256 <= p + 128), 0.0, -1e30).astype(BF)
    c128 = np.arange(128)[None, :]
    sh["c_maskA0"] = np.where((c128 <= p) & (c128 > p - 64), 0.0, -1e30).astype(BF)
    c640 = np.arange(640)[None, :]
    sh["c_maskF"] = np.where((c640 >= p) & (c640 <= p + 512), 0.0, -1e30).astype(BF)
    kk = np.arange(128)[:, None]
    qq = np.arange(128)[None, :]
    sh["c_triu"] = (kk <= qq).astype(BF)
    psub = np.zeros((128, 128), dtype=f32)
    psub[np.arange(127), np.arange(1, 128)] = 1.0
    sh["c_psub"] = psub
    e0 = np.zeros((1, 128), dtype=f32)
    e0[0, 0] = 1.0
    sh["c_e0"] = e0
    p127 = np.zeros((128, 128), dtype=f32)
    p127[127, 0] = 1.0
    sh["c_p127"] = p127
    DH, WIN = 64, 512
    half = np.arange(0, DH, 2, dtype=np.float64)
    inv_freq = 1.0 / 10000.0 ** (half / DH)
    tt = np.arange(SEQ, dtype=np.float64)
    freqs = tt[:, None] * inv_freq[None, :]
    freqs = np.concatenate([freqs, freqs], axis=-1)
    base = (half + 0.4 * DH) / (1.4 * DH)
    power = (tt - SEQ // 2) / float(WIN // 2)
    scale = base[None, :] ** power[:, None]
    scale = np.concatenate([scale, scale], axis=-1)
    rot = np.stack([np.cos(freqs) * scale, np.sin(freqs) * scale,
                    np.cos(freqs) / scale, np.sin(freqs) / scale], axis=0).astype(f32)
    sh["c_rot"] = np.ascontiguousarray(rot.reshape(4, NT, 128, 64).transpose(2, 0, 1, 3))

    in_maps = []
    for core in range(NCORES):
        m = dict(sh)
        xb = x[2 * core:2 * core + 2]
        m["x_tok"] = np.stack([tokmaj(xb[0]), tokmaj(xb[1])])
        m["x_feaT"] = np.stack([feamaj(xb[0]), feamaj(xb[1])])
        in_maps.append(m)
    return in_maps


def _run(inputs, stages=ALL_STAGES):
    in_maps = _host_prep(inputs)
    p = P(bass.Bass("TRN2", target_bir_lowering=False, detect_race_conditions=False), stages)
    nc = p.build()
    res = run_bass_kernel_spmd(nc, in_maps, core_ids=list(range(NCORES)))
    outs = []
    for core in range(NCORES):
        o = res.results[core]["out"]
        for b_ in range(NB):
            outs.append(np.ascontiguousarray(o[b_].transpose(1, 0, 2).reshape(SEQ, D)))
    return np.stack(outs).astype(np.float32)


def kernel(**inputs):
    return _run(inputs, ALL_STAGES)


# version 17
# speedup vs baseline: 1.2217x; 1.0021x over previous
import numpy as np
import ml_dtypes
import concourse.bass as bass
import concourse.mybir as mybir
from concourse.bass_utils import run_bass_kernel_spmd
from contextlib import ExitStack

F32 = mybir.dt.float32
BF16 = mybir.dt.bfloat16
AF = mybir.ActivationFunctionType
ALU = mybir.AluOpType
AX = mybir.AxisListType
BF = ml_dtypes.bfloat16

NCORES = 8
NB = 2
NT = 16
D = 1024
SEQ = 2048
HID2 = 5460
HID = 2730
ALL_STAGES = ("A", "B0", "B1", "C", "D", "E")
FFN_ON = True
RELAX_INCW = False
PRECAST_ON = True


class Ctr:
    def __init__(self, h):
        self.h = h
        self.n = 0

    def inc(self, instr, k=1):
        self.n += k
        instr.then_inc(self.h, k)
        return instr

    def wait(self, eng, n=None):
        n = self.n if n is None else n
        if n > 0:
            eng.wait_ge(self.h, n)

    def incx(self, eng, instr, k=1):
        return self.inc(instr, k)

    def incw(self, eng, instr, k=1):
        self.inc(instr, k)
        if not RELAX_INCW:
            eng.wait_ge(self.h, self.n)
        return instr


class Slot:

    def __init__(self, handle):
        self.b = handle
        self.free = None

    def acquire(self, eng):
        if self.free is not None:
            c, n = self.free
            if n > 0:
                eng.wait_ge(c.h, n)

    def release(self, ctr):
        self.free = (ctr, ctr.n)


def bcast_last(ap3, nf):
    return bass.AP(ap3.tensor, ap3.offset, [ap3.ap[0], ap3.ap[1], [0, nf]])


def bcast_mid(ap2, nh):
    return bass.AP(ap2.tensor, ap2.offset, [ap2.ap[0], [0, nh], ap2.ap[1]])


class P:
    def __init__(self, nc, stages=ALL_STAGES):
        self.nc = nc
        self.stages = stages
        self.t, self.v, self.s, self.g, self.y = nc.tensor, nc.vector, nc.scalar, nc.gpsimd, nc.sync
        self.ctx = ExitStack()
        self.mm_i = 0
        self.cc_world = NCORES

    def nt(self, name):
        self.uid = getattr(self, "uid", 0) + 1
        return f"{name}_{self.uid}"

    def sb(self, name, shape, dt):
        return self.ctx.enter_context(self.nc.sbuf_tensor(self.nt(name), shape, dt))

    def next_mm(self):
        m = self.mm[self.mm_i % 4]
        self.mm_i += 1
        return m

    def powx(self, out, in_ap, bias_ap, pw):
        s = self.s
        self.sv.wait(s)
        self.ss.incw(s, s.activation(out=out, in_=in_ap, func=AF.Ln, bias=bias_ap, scale=1.0))
        self.ss.incw(s, s.activation(out=out, in_=out, func=AF.Exp, bias=self.zero1[:], scale=float(pw)))

    def build(self):
        nc = self.nc
        dt = nc.dram_tensor
        self.x_tok = dt("x_tok", [NB, 128, NT, D], F32, kind="ExternalInput").ap()
        self.x_feaT = dt("x_feaT", [NB, 128, 8, SEQ], F32, kind="ExternalInput").ap()
        self.out = dt("out", [NB, 128, NT, D], F32, kind="ExternalOutput").ap()
        self.wa_qkvT = dt("wa_qkvT", [128, 8, 3072], F32, kind="ExternalInput").ap()
        self.wa_outT = dt("wa_outT", [128, 8, D], F32, kind="ExternalInput").ap()
        self.ba_qkv = dt("ba_qkv", [128, 16], F32, kind="ExternalInput").ap()
        self.ba_bv = dt("ba_bv", [128, 8], F32, kind="ExternalInput").ap()
        self.ba_bout = dt("ba_bout", [1, D], F32, kind="ExternalInput").ap()
        self.wf_qkv = dt("wf_qkv", [2, 128, 8, 1536], F32, kind="ExternalInput").ap()
        self.wf_wo = dt("wf_wo", [2, 128, 4, D], F32, kind="ExternalInput").ap()
        self.wf_w1 = dt("wf_w1", [2, 128, 8, HID2], F32, kind="ExternalInput").ap()
        self.wf_w2 = dt("wf_w2", [2, 128, 22, D], F32, kind="ExternalInput").ap()
        self.wf_qs = dt("wf_qs", [1, 2, 64], F32, kind="ExternalInput").ap()
        self.wf_ks = dt("wf_ks", [1, 2, 64], F32, kind="ExternalInput").ap()
        self.wf_lng = dt("wf_lng", [128, 2, 8], F32, kind="ExternalInput").ap()
        self.wf_lnb = dt("wf_lnb", [128, 2, 8], F32, kind="ExternalInput").ap()
        self.wf_b1 = dt("wf_b1", [1, 2, HID2], F32, kind="ExternalInput").ap()
        self.wf_b2 = dt("wf_b2", [1, 2, D], F32, kind="ExternalInput").ap()
        self.w1_qT = dt("w1_qT", [128, 8, 256], F32, kind="ExternalInput").ap()
        self.w1_kT = dt("w1_kT", [128, 8, 256], F32, kind="ExternalInput").ap()
        self.w1_vT = dt("w1_vT", [128, 8, D], F32, kind="ExternalInput").ap()
        self.w1_oT = dt("w1_oT", [128, 8, D], F32, kind="ExternalInput").ap()
        self.w1_g = dt("w1_g", [1, 16], F32, kind="ExternalInput").ap()
        self.w1_b = dt("w1_b", [1, 16], F32, kind="ExternalInput").ap()
        self.we_wT = dt("we_wT", [128, 8, D], F32, kind="ExternalInput").ap()
        self.we_g = dt("we_g", [1, D], F32, kind="ExternalInput").ap()
        self.we_b = dt("we_b", [1, D], F32, kind="ExternalInput").ap()
        self.c_ident_bf = dt("c_ident_bf", [128, 128], BF16, kind="ExternalInput").ap()
        self.c_ident_f = dt("c_ident_f", [128, 128], F32, kind="ExternalInput").ap()
        self.c_ones_bf = dt("c_ones_bf", [1, 128], BF16, kind="ExternalInput").ap()
        self.c_ones_f = dt("c_ones_f", [1, 128], F32, kind="ExternalInput").ap()
        self.c_maskA = dt("c_maskA", [128, 256], BF16, kind="ExternalInput").ap()
        self.c_maskA0 = dt("c_maskA0", [128, 128], BF16, kind="ExternalInput").ap()
        self.c_maskF = dt("c_maskF", [128, 640], BF16, kind="ExternalInput").ap()
        self.c_triu = dt("c_triu", [128, 128], BF16, kind="ExternalInput").ap()
        self.c_psub = dt("c_psub", [128, 128], F32, kind="ExternalInput").ap()
        self.c_e0 = dt("c_e0", [1, 128], F32, kind="ExternalInput").ap()
        self.c_p127 = dt("c_p127", [128, 128], F32, kind="ExternalInput").ap()
        self.c_rot = dt("c_rot", [128, 4, NT, 64], F32, kind="ExternalInput").ap()
        self.x1_spill = dt("x1_spill", [128, NT, D], F32)
        self.w1_bf = dt("w1_bf", [2, 128, 8, 12, 512], BF16)
        self.x3_spill = dt("x3_spill", [128, NT, D], F32)
        self.ar_in = dt("ar_in", [1, 128], F32)
        self.ar_out = dt("ar_out", [1, 128], F32)

        E = self.ctx.enter_context
        self.sd = Ctr(E(nc.semaphore("sd")))
        self.sy_ = Ctr(E(nc.semaphore("sy")))
        self.sv = Ctr(E(nc.semaphore("sv")))
        self.ss = Ctr(E(nc.semaphore("ss")))
        self.st = Ctr(E(nc.semaphore("st")))
        self.scc = Ctr(E(nc.semaphore("scc")))

        self.X = self.sb("X", [128, NT, D], F32)
        self.XT = self.sb("XT", [128, 8, SEQ + 1], BF16)
        self.ident_bf = self.sb("ident_bf", [128, 128], BF16)
        self.ident_f = self.sb("ident_f", [128, 128], F32)
        self.ones_bf = self.sb("ones_bf", [1, 128], BF16)
        self.ones_f = self.sb("ones_f", [1, 128], F32)
        self.zero1 = self.sb("zero1", [128, 1], F32)
        self.eps1 = self.sb("eps1", [128, 1], F32)
        self.magic = self.sb("magic", [128, 1], F32)
        self.nmagic = self.sb("nmagic", [128, 1], F32)
        self.clipmax = self.sb("clipmax", [128, 1], F32)
        self.stats_e = self.sb("stats_e", [128, NB, NT, 2], F32)
        self.gmax = self.sb("gmax", [128, 1], F32)
        self.scale_b = self.sb("scale_b", [128, 1], F32)
        self.iscale_b = self.sb("iscale_b", [128, 1], F32)
        self.wf_lng_sb = self.sb("wf_lng_sb", [128, 2, 8], F32)
        self.wf_lnb_sb = self.sb("wf_lnb_sb", [128, 2, 8], F32)

        self.mm = [Slot(E(nc.psum_tensor(f"mm{i}", [128, 512], F32))) for i in range(4)]
        self.tsp = Slot(E(nc.psum_tensor("tsp", [128, 5, 128], BF16)))
        self.tsp2 = Slot(E(nc.psum_tensor("tsp2", [128, 5, 128], BF16)))
        self.po = Slot(E(nc.psum_tensor("po", [128, 512], F32)))
        self.po2 = Slot(E(nc.psum_tensor("po2", [128, 512], F32)))

        v, s, t, g, y = self.v, self.s, self.t, self.g, self.y
        self.sd.inc(g.dma_start(out=self.ident_bf[:], in_=self.c_ident_bf), 16)
        self.sd.inc(g.dma_start(out=self.ident_f[:], in_=self.c_ident_f), 16)
        self.sd.inc(g.dma_start(out=self.ones_bf[:], in_=self.c_ones_bf), 16)
        self.sd.inc(g.dma_start(out=self.ones_f[:], in_=self.c_ones_f), 16)
        self.sd.inc(g.dma_start(out=self.wf_lng_sb[:], in_=self.wf_lng), 16)
        self.sd.inc(g.dma_start(out=self.wf_lnb_sb[:], in_=self.wf_lnb), 16)
        self.sv.incw(v, v.memset(self.zero1[:], 0.0))
        self.sv.incw(v, v.memset(self.eps1[:], 1e-5))
        self.sv.incw(v, v.memset(self.magic[:], 12582912.0))
        self.sv.incw(v, v.memset(self.nmagic[:], -12582912.0))
        self.sv.incw(v, v.memset(self.clipmax[:], 1e12))
        self.sv.incw(v, v.memset(self.gmax[:], 0.0))
        for b in range(NB):
            self.load_x(b)
            if b == 0 and PRECAST_ON and ("B0" in self.stages or "B1" in self.stages):
                self.precast_w1()
            if "A" in self.stages:
                self.stage_A(b)
            for li, stg in ((0, "B0"), (1, "B1")):
                if stg in self.stages:
                    self.stage_B(b, li)
            if "C" in self.stages:
                self.stage_C(b)
            if "D" in self.stages:
                self.stage_D(b)
                if "E" in self.stages:
                    self.stage_D_eprep(b)
            if "E" in self.stages and b != NB - 1:
                self.ss.wait(y)
                self.sv.wait(y)
                for tc in range(4):
                    self.sy_.inc(y.dma_start(out=self.x3_spill.ap()[:, 4 * tc:4 * tc + 4, :],
                                             in_=self.X[:, 4 * tc:4 * tc + 4, :]), 16)
            if "E" not in self.stages:
                self.ss.wait(y)
                self.sv.wait(y)
                for tc in range(4):
                    self.sy_.inc(y.dma_start(out=self.out[b, :, 4 * tc:4 * tc + 4, :],
                                             in_=self.X[:, 4 * tc:4 * tc + 4, :]), 16)
        if "E" in self.stages:
            self.stage_E_all()
        y.wait_ge(self.sy_.h, self.sy_.n)
        self.ctx.close()
        return nc

    def load_x(self, b):
        y, v, s, t, g = self.y, self.v, self.s, self.t, self.g
        self.sv.wait(y)
        self.ss.wait(y)
        self.st.wait(y)
        self.sv.wait(g)
        self.ss.wait(g)
        self.st.wait(g)
        for tc in range(4):
            self.sy_.inc(y.dma_start(out=self.X[:, 4 * tc:4 * tc + 4, :],
                                     in_=self.x_tok[b, :, 4 * tc:4 * tc + 4, :]), 16)
        self.sy_.wait(v)
        self.sv.incw(v, v.memset(self.XT[:, :, 0:1], 0.0))
        self.sd.inc(g.dma_start(out=self.XT[:, :, 1:SEQ + 1], in_=self.x_feaT[b]), 16)

    def precast_w1(self):
        nc, g, y = self.nc, self.g, self.y
        with ExitStack() as c:
            E = c.enter_context
            scr = [E(nc.sbuf_tensor(self.nt("pc_scr"), [128, 8, 512], BF16)) for _ in range(2)]
            slots = [Slot(s_) for s_ in scr]
            for li in range(2):
                for cc in range(12):
                    p_, half = cc // 2, cc % 2
                    a0 = 512 * p_
                    a1 = min(a0 + 512, HID)
                    w = a1 - a0
                    c0 = a0 if half == 0 else HID + a0
                    i = cc % 2
                    slots[i].acquire(g)
                    self.sd.inc(g.dma_start(out=scr[i][:, :, 0:w], in_=self.wf_w1[li, :, :, c0:c0 + w]), 16)
                    self.sd.wait(y)
                    self.sy_.inc(y.dma_start(out=self.w1_bf.ap()[li, :, :, cc, :], in_=scr[i][:]), 16)
                    slots[i].release(self.sy_)
            g.wait_ge(self.sy_.h, self.sy_.n)
            y.wait_ge(self.sy_.h, self.sy_.n)

    def emit_w2_part(self, it, p_, ti, wch, dacc, w2, hTb, hT_s):
        t, v, s = self.t, self.v, self.s
        nhb = (wch + 127) // 128
        hT_s[it].acquire(t)
        self.sv.wait(t)
        banks = (self.tsp, self.tsp2)

        def evac_mm(hl, n_t):
            hw = min(128, wch - 128 * hl)
            sl = banks[hl % 2]
            if hl % 2 == 0:
                v.wait_ge(self.st.h, n_t)
                self.sv.incx(v, v.tensor_copy(out=hTb[0:hw, it, hl, :], in_=sl.b[0:hw, 2 + hl // 2, :]))
                sl.release(self.sv)
                t.wait_ge(self.sv.h, self.sv.n)
            else:
                s.wait_ge(self.st.h, n_t)
                self.ss.incx(s, s.copy(out=hTb[0:hw, it, hl, :], in_=sl.b[0:hw, 2 + hl // 2, :]))
                sl.release(self.ss)
                t.wait_ge(self.ss.h, self.ss.n)
            hb = 4 * p_ + hl
            for oc in range(2):
                dm = dacc[2 * ti + oc]
                self.st.inc(t.matmul(dm.b[:], hTb[0:hw, it, hl, :], w2[0:hw, hb, 512 * oc:512 * oc + 512],
                                     start=(p_ == 0 and hl == 0), stop=False, skip_group_check=True))

        pend = None
        for hl in range(nhb):
            hw = min(128, wch - 128 * hl)
            sl = banks[hl % 2]
            sl.acquire(t)
            self.st.inc(t.transpose(sl.b[0:hw, 2 + hl // 2, :],
                                    self.hidc_cur[it][:, 128 * hl:128 * hl + hw], self.ident_bf[:]))
            n_t = self.st.n
            if pend is not None:
                evac_mm(*pend)
            pend = (hl, n_t)
        self.hid_s_cur[it].release(self.st)
        if pend is not None:
            evac_mm(*pend)
        hT_s[it].release(self.st)

    def lazy_xt(self, T):
        t, v, s = self.t, self.v, self.s
        xbf = self.xbf_s
        xbf.acquire(s)
        self.sv.wait(s)
        self.ss.incx(s, s.copy(out=xbf.b[:], in_=self.X[:, T, :]))
        for db in range(8):
            self.tsp.acquire(t)
            self.ss.wait(t)
            self.st.inc(t.transpose(self.tsp.b[:, db % 5, :], xbf.b[:, 128 * db:128 * db + 128],
                                    self.ident_bf[:]))
            self.tsp.release(self.st)
            self.st.wait(v)
            self.sv.incx(v, v.tensor_copy(out=self.XT[:, db, 1 + 128 * T:1 + 128 * T + 128],
                                      in_=self.tsp.b[:, db % 5, :]))
            self.tsp.release(self.sv)
        xbf.release(self.st)

    def build_gbc(self, gbc):
        t, v, g = self.t, self.v, self.g
        with self.nc.sbuf_tensor(self.nt("grow"), [1, 2, D], F32) as grow:
            self.sv.wait(g)
            self.sd.inc(g.dma_start(out=grow[0:1, 0, :], in_=self.we_g), 16)
            self.sd.inc(g.dma_start(out=grow[0:1, 1, :], in_=self.we_b), 16)
            for which in range(2):
                for oc in range(2):
                    m = self.next_mm()
                    m.acquire(t)
                    t.wait_ge(self.sd.h, self.sd.n)
                    self.st.inc(t.matmul(m.b[:], self.ones_f[:], grow[0:1, which, 512 * oc:512 * oc + 512],
                                         start=True, stop=True, skip_group_check=True))
                    self.st.wait(v)
                    self.sv.incw(v, v.tensor_copy(out=gbc[:, which, 512 * oc:512 * oc + 512], in_=m.b[:]))
                    m.release(self.sv)

    def ln_combine(self, bn):
        v = self.v
        self.sv.incw(v, v.tensor_tensor(out=bn[:, 16:18], in0=bn[:, 12:14], in1=bn[:, 14:16], op=ALU.add))
        self.sv.incw(v, v.tensor_scalar(out=bn[:, 16:18], in0=bn[:, 16:18], scalar1=0.5, scalar2=None,
                                    op0=ALU.mult, op1=ALU.bypass))
        self.sv.incw(v, v.tensor_tensor(out=bn[:, 18:19], in0=bn[:, 12:13], in1=bn[:, 14:15], op=ALU.subtract))
        self.sv.incw(v, v.tensor_tensor(out=bn[:, 18:19], in0=bn[:, 18:19], in1=bn[:, 18:19], op=ALU.mult))
        self.sv.incw(v, v.scalar_tensor_tensor(out=bn[:, 17:18], in0=bn[:, 18:19], scalar=0.25,
                                           in1=bn[:, 17:18], op0=ALU.mult, op1=ALU.add))

    def ln_apply_tile(self, T):
        v, s = self.v, self.s
        bn = self.bn
        self.ss.wait(v)
        self.sv.incw(v, v.bn_stats(bn[:, 0:6], self.X[:, T, 0:512]))
        self.sv.incw(v, v.bn_stats(bn[:, 6:12], self.X[:, T, 512:1024]))
        self.sv.incw(v, v.bn_aggr(bn[:, 12:14], bn[:, 0:6]))
        self.sv.incw(v, v.bn_aggr(bn[:, 14:16], bn[:, 6:12]))
        self.ln_combine(bn)
        rstd, nmr = bn[:, 19:20], bn[:, 20:21]
        self.powx(rstd, bn[:, 17:18], self.eps1[:], -0.5)
        self.ss.wait(v)
        self.sv.incw(v, v.tensor_tensor(out=nmr, in0=bn[:, 16:17], in1=rstd, op=ALU.mult))
        self.sv.incw(v, v.tensor_scalar(out=nmr, in0=nmr, scalar1=-1.0, scalar2=None,
                                    op0=ALU.mult, op1=ALU.bypass))
        self.sv.wait(s)
        self.ss.incw(s, s.activation(out=self.X[:, T, :], in_=self.X[:, T, :], func=AF.Identity,
                                 bias=nmr, scale=rstd))

    def softmax_block(self, psum_aps, mask_aps, scf_slot, probs_slot, sm, ncols,
                      scf_ap=None, probs_ap=None):
        v, s = self.v, self.s
        scf_b = scf_ap if scf_ap is not None else scf_slot.b[:]
        probs_b = probs_ap if probs_ap is not None else probs_slot.b[:]
        c0 = 0
        for pap, map_ in zip(psum_aps, mask_aps):
            w = pap.shape[-1]
            self.st.wait(v)
            self.sv.incw(v, v.tensor_tensor(out=scf_b[:, c0:c0 + w], in0=pap, in1=map_, op=ALU.add))
            c0 += w
        assert c0 == ncols
        self.sv.incx(v, v.reduce_max(out=sm[:, 0:1], in_=scf_b[:, 0:ncols], axis=AX.X, negate=True))
        self.sv.wait(s)
        self.ss.incx(s, s.activation(out=probs_b[:, 0:ncols], in_=scf_b[:, 0:ncols], func=AF.Exp,
                                 bias=sm[:, 0:1], accum_out=sm[:, 1:2]))
        scf_slot.release(self.ss)
        self.ss.wait(v)
        self.sv.incw(v, v.reciprocal(out=sm[:, 2:3], in_=sm[:, 1:2]))
        self.sv.incx(v, v.tensor_scalar(out=probs_b[:, 0:ncols], in0=probs_b[:, 0:ncols],
                                    scalar1=sm[:, 2:3], scalar2=None, op0=ALU.mult, op1=ALU.bypass))

    def softmax_psum(self, psum_ap, probs_slot, sm, ncols, probs_ap=None):
        v, s = self.v, self.s
        probs_b = probs_ap if probs_ap is not None else probs_slot.b[:]
        self.st.wait(v)
        self.sv.incx(v, v.reduce_max(out=sm[:, 0:1], in_=psum_ap, axis=AX.X, negate=True))
        self.sv.wait(s)
        self.ss.incx(s, s.activation(out=probs_b[:, 0:ncols], in_=psum_ap, func=AF.Exp,
                                     bias=sm[:, 0:1], accum_out=sm[:, 1:2]))
        self.ss.wait(v)
        self.sv.incw(v, v.reciprocal(out=sm[:, 2:3], in_=sm[:, 1:2]))
        self.sv.incx(v, v.tensor_scalar(out=probs_b[:, 0:ncols], in0=probs_b[:, 0:ncols],
                                        scalar1=sm[:, 2:3], scalar2=None, op0=ALU.mult, op1=ALU.bypass))

    def softmax_psum2(self, papA, papB, probs_slot, sm, ncols, probs_ap):
        v, s = self.v, self.s
        if papA is None:
            return self.softmax_psum(papB, probs_slot, sm, ncols, probs_ap)
        wA = papA.shape[-1]
        self.st.wait(v)
        self.sv.incx(v, v.reduce_max(out=sm[:, 0:1], in_=papA, axis=AX.X, negate=True))
        self.sv.incw(v, v.reduce_max(out=sm[:, 3:4], in_=papB, axis=AX.X, negate=True))
        self.sv.incx(v, v.tensor_tensor(out=sm[:, 0:1], in0=sm[:, 0:1], in1=sm[:, 3:4], op=ALU.min))
        self.sv.wait(s)
        self.ss.incx(s, s.activation(out=probs_ap[:, 0:wA], in_=papA, func=AF.Exp,
                                     bias=sm[:, 0:1], accum_out=sm[:, 1:2]))
        self.ss.incx(s, s.activation(out=probs_ap[:, wA:ncols], in_=papB, func=AF.Exp,
                                     bias=sm[:, 0:1], accum_out=sm[:, 3:4]))
        self.ss.wait(v)
        self.sv.incw(v, v.tensor_tensor(out=sm[:, 1:2], in0=sm[:, 1:2], in1=sm[:, 3:4], op=ALU.add))
        self.sv.incw(v, v.reciprocal(out=sm[:, 2:3], in_=sm[:, 1:2]))
        self.sv.incx(v, v.tensor_scalar(out=probs_ap[:, 0:ncols], in0=probs_ap[:, 0:ncols],
                                        scalar1=sm[:, 2:3], scalar2=None, op0=ALU.mult, op1=ALU.bypass))

    def transpose_to(self, src_ap_list, dst_ap_list, ident):
        t, v, s = self.t, self.v, self.s
        banks = (self.tsp, self.tsp2)
        pend = None
        items = list(zip(src_ap_list, dst_ap_list))
        for i, (sap, dap) in enumerate(items):
            b = banks[i % 2]
            sl = (i // 2) % 5
            b.acquire(t)
            self.st.inc(t.transpose(b.b[0:sap.shape[-1], sl, :], sap, ident))
            n_t = self.st.n
            if pend is not None:
                self._tp_evac(*pend)
            pend = (b, (i // 2) % 5, items[i][0].shape[-1], dap, n_t, i)
        if pend is not None:
            self._tp_evac(*pend)

    def _tp_evac(self, b, sl, w, dap, n_t, i):
        v, s = self.v, self.s
        if i % 2 == 0:
            v.wait_ge(self.st.h, n_t)
            self.sv.incx(v, v.tensor_copy(out=dap, in_=b.b[0:w, sl, :]))
            b.release(self.sv)
        else:
            s.wait_ge(self.st.h, n_t)
            self.ss.incx(s, s.copy(out=dap, in_=b.b[0:w, sl, :]))
            b.release(self.ss)

    def stage_A(self, b):
        nc, t, v, s, g, y = self.nc, self.t, self.v, self.s, self.g, self.y
        with ExitStack() as c:
            E = c.enter_context
            wqkv = E(nc.sbuf_tensor(self.nt("a_wqkv"), [128, 8, 3072], BF16))
            wout = E(nc.sbuf_tensor(self.nt("a_wout"), [128, 8, D], BF16))
            bqk = E(nc.sbuf_tensor(self.nt("a_bqk"), [128, 16], F32))
            bv = E(nc.sbuf_tensor(self.nt("a_bv"), [128, 8], BF16))
            brow = E(nc.sbuf_tensor(self.nt("a_brow"), [1, D], BF16))
            browf = E(nc.sbuf_tensor(self.nt("a_browf"), [1, D], F32))
            qk_c = E(nc.sbuf_tensor(self.nt("a_qk"), [128, 16, 512], BF16))
            kcar = E(nc.sbuf_tensor(self.nt("a_kcar"), [128, 8, 128], BF16))
            v_c = E(nc.sbuf_tensor(self.nt("a_v"), [128, 4, D], BF16))
            vcar = E(nc.sbuf_tensor(self.nt("a_vcar"), [128, D], BF16))
            probs = E(nc.sbuf_tensor(self.nt("a_probs"), [128, 3, 256], BF16))
            pT = E(nc.sbuf_tensor(self.nt("a_pT"), [128, 3, 2, 128], BF16))
            oT = E(nc.sbuf_tensor(self.nt("a_oT"), [128, 8, 128], BF16))
            sm = E(nc.sbuf_tensor(self.nt("a_sm"), [128, 3, 4], F32))
            self.bn = E(nc.sbuf_tensor(self.nt("a_bn"), [128, 24], F32))
            maskA = E(nc.sbuf_tensor(self.nt("a_maskA"), [128, 256], BF16))
            maskA0 = E(nc.sbuf_tensor(self.nt("a_maskA0"), [128, 128], BF16))

            qk_s, v_s = Slot(qk_c), Slot(v_c)
            oT_s = Slot(oT)
            probs_s = [Slot(None), Slot(None), Slot(None)]
            pT_s = [Slot(None), Slot(None), Slot(None)]
            kcar_s, vcar_s = Slot(kcar), Slot(vcar)

            self.st.wait(g)
            self.sd.inc(g.dma_start(out=maskA[:], in_=self.c_maskA), 16)
            self.sd.inc(g.dma_start(out=maskA0[:], in_=self.c_maskA0), 16)
            self.sd.inc(g.dma_start(out=wqkv[:], in_=self.wa_qkvT), 16)
            self.sd.inc(g.dma_start(out=wout[:], in_=self.wa_outT), 16)
            self.sd.inc(g.dma_start(out=bqk[:], in_=self.ba_qkv), 16)
            self.sd.inc(g.dma_start(out=bv[:], in_=self.ba_bv), 16)
            self.sd.inc(g.dma_start(out=browf[:], in_=self.ba_bout), 16)

            self.sd.wait(s)
            self.ss.incw(s, s.mul(out=bqk[:, 0:8], in_=bqk[:, 0:8], mul=0.125))

            t.wait_ge(self.sd.h, self.sd.n)
            for oc in range(2):
                m = self.next_mm()
                m.acquire(t)
                for db in range(8):
                    self.st.inc(t.matmul(m.b[0:1, :], bv[:, db:db + 1], wout[:, db, 512 * oc:512 * oc + 512],
                                         start=(db == 0), stop=(db == 7), skip_group_check=True))
                self.st.wait(v)
                self.sv.incw(v, v.tensor_tensor(out=browf[0:1, 512 * oc:512 * oc + 512],
                                            in0=m.b[0:1, :], in1=browf[0:1, 512 * oc:512 * oc + 512], op=ALU.add))
                m.release(self.sv)
            self.sv.incw(v, v.tensor_copy(out=brow[:], in_=browf[:]))

            self.sd.wait(t)
            self.sv.wait(t)
            self.sy_.wait(t)
            self.sy_.wait(v)

            for tc in range(4):
                qk_s.acquire(s)
                qk_s.acquire(t)
                for ob in range(16):
                    m = self.next_mm()
                    m.acquire(t)
                    for db in range(8):
                        self.st.inc(t.matmul(m.b[:], wqkv[:, db, 128 * ob:128 * ob + 128],
                                             self.XT[:, db, 1 + 512 * tc:1 + 512 * tc + 512],
                                             start=(db == 0), stop=(db == 7), skip_group_check=True))
                    self.st.wait(s)
                    self.ss.incx(s, s.activation(out=qk_c[:, ob, :], in_=m.b[:], func=AF.Identity,
                                             bias=bqk[:, ob:ob + 1], scale=(0.125 if ob < 8 else 1.0)))
                    m.release(self.ss)
                v_s.acquire(v)
                for tt in range(4):
                    T = 4 * tc + tt
                    for vc in range(2):
                        m = self.next_mm()
                        m.acquire(t)
                        for db in range(8):
                            self.st.inc(t.matmul(m.b[:], self.XT[:, db, 1 + 128 * T:1 + 128 * T + 128],
                                                 wqkv[:, db, 2048 + 512 * vc:2048 + 512 * vc + 512],
                                                 start=(db == 0), stop=(db == 7), skip_group_check=True))
                        self.st.wait(v)
                        self.sv.incx(v, v.tensor_copy(out=v_c[:, tt, 512 * vc:512 * vc + 512], in_=m.b[:]))
                        m.release(self.sv)

                for tt in range(4):
                    T = 4 * tc + tt
                    ncols = 128 if T == 0 else 256
                    nkt = ncols // 128

                    def a_tail(h, par, sv_n):
                        hp, hh = h // 2, h % 2
                        opo = self.po if hp % 2 == 0 else self.po2
                        pT_s[par].acquire(v)
                        pT_s[par].acquire(s)
                        t.wait_ge(self.sv.h, sv_n)
                        self.transpose_to([probs[:, par, 128 * kt:128 * kt + 128] for kt in range(nkt)],
                                          [pT[:, par, kt, :] for kt in range(nkt)], self.ident_bf[:])
                        probs_s[par].release(self.st)
                        t.wait_ge(self.sv.h, self.sv.n)
                        t.wait_ge(self.ss.h, self.ss.n)
                        if hh == 0:
                            opo.acquire(t)
                        tp = (0, 64 * hh)
                        od = opo.b[64 * hh:64 * hh + 64, 0:128]
                        if T == 0:
                            self.st.inc(t.matmul(od, v_c[:, tt, 64 * h:64 * h + 64], pT[:, par, 0, :],
                                                 start=True, stop=True, tile_position=tp,
                                                 skip_group_check=True))
                        else:
                            vprev = vcar[:, 64 * h:64 * h + 64] if tt == 0 else v_c[:, tt - 1, 64 * h:64 * h + 64]
                            self.st.inc(t.matmul(od, vprev, pT[:, par, 0, :],
                                                 start=True, stop=False, tile_position=tp,
                                                 skip_group_check=True))
                            self.st.inc(t.matmul(od, v_c[:, tt, 64 * h:64 * h + 64], pT[:, par, 1, :],
                                                 start=False, stop=True, tile_position=tp,
                                                 skip_group_check=True))
                        pT_s[par].release(self.st)
                        if hh == 1:
                            oT_s.acquire(v)
                            v.wait_ge(self.st.h, self.st.n)
                            self.sv.incx(v, v.tensor_copy(out=oT[:, hp, :], in_=opo.b[:, 0:128]))
                            opo.release(self.sv)

                    from collections import deque
                    pendq = deque()
                    self.ss.wait(t)
                    self.sv.wait(t)
                    for h in range(16):
                        bp, pbq, pbk = 64 * (h % 2), h // 2, 8 + h // 2
                        par = h % 3
                        msc = self.next_mm()
                        msc.acquire(t)
                        if T == 0:
                            self.st.inc(t.matmul(msc.b[:, 0:128], qk_c[bp:bp + 64, pbq, 0:128],
                                                 qk_c[bp:bp + 64, pbk, 0:128], start=True, stop=False,
                                                 skip_group_check=True))
                        elif tt == 0:
                            self.st.inc(t.matmul(msc.b[:, 0:128], qk_c[bp:bp + 64, pbq, 0:128],
                                                 kcar[bp:bp + 64, h // 2, :], start=True, stop=False,
                                                 skip_group_check=True))
                            self.st.inc(t.matmul(msc.b[:, 128:256], qk_c[bp:bp + 64, pbq, 0:128],
                                                 qk_c[bp:bp + 64, pbk, 0:128], start=False, stop=False,
                                                 skip_group_check=True))
                        else:
                            self.st.inc(t.matmul(msc.b[:, 0:256], qk_c[bp:bp + 64, pbq, 128 * tt:128 * tt + 128],
                                                 qk_c[bp:bp + 64, pbk, 128 * (tt - 1):128 * (tt - 1) + 256],
                                                 start=True, stop=False, skip_group_check=True))
                        mask = maskA0[:, 0:128] if T == 0 else maskA[:, 0:256]
                        self.st.inc(t.matmul(msc.b[:, 0:ncols], self.ident_bf[:], mask,
                                             start=False, stop=True, skip_group_check=True))
                        if len(pendq) >= 2:
                            a_tail(*pendq.popleft())
                        probs_s[par].acquire(s)
                        self.softmax_psum(msc.b[:, 0:ncols], probs_s[par],
                                          sm[:, par, :], ncols, probs[:, par, :])
                        msc.release(self.ss)
                        pendq.append((h, par, self.sv.n))
                    while pendq:
                        a_tail(*pendq.popleft())

                    for oc in range(2):
                        m = self.next_mm()
                        m.acquire(t)
                        self.sv.wait(t)
                        for fb in range(8):
                            self.st.inc(t.matmul(m.b[:], oT[:, fb, :], wout[:, fb, 512 * oc:512 * oc + 512],
                                                 start=(fb == 0), stop=False, skip_group_check=True))
                        self.st.inc(t.matmul(m.b[:], self.ones_bf[:], brow[0:1, 512 * oc:512 * oc + 512],
                                             start=False, stop=True, skip_group_check=True))
                        self.st.wait(v)
                        self.sv.incw(v, v.tensor_tensor(out=self.X[:, T, 512 * oc:512 * oc + 512],
                                                    in0=m.b[:], in1=self.X[:, T, 512 * oc:512 * oc + 512],
                                                    op=ALU.add))
                        m.release(self.sv)
                    oT_s.release(self.st)
                    self.ln_apply_tile(T)
                    if "B0" in self.stages or "C" in self.stages:
                        self.ss.wait(y)
                        self.sy_.inc(y.dma_start(out=self.x1_spill.ap()[:, T, :], in_=self.X[:, T, :]), 16)
                if tc < 3:
                    kcar_s.acquire(v)
                    vcar_s.acquire(v)
                    self.st.wait(v)
                    self.sv.incx(v, v.tensor_copy(out=kcar[:], in_=qk_c[:, 8:16, 384:512]))
                    self.sv.incw(v, v.tensor_copy(out=vcar[:], in_=v_c[:, 3, :]))
                    kcar_s.release(self.sv)
                    vcar_s.release(self.sv)
                qk_s.release(self.st)
                v_s.release(self.st)

    def stage_B(self, b, li):
        nc, t, v, s, g, y = self.nc, self.t, self.v, self.s, self.g, self.y
        with ExitStack() as c:
            E = c.enter_context
            self.bn = E(nc.sbuf_tensor(self.nt("b_bn"), [128, 24], F32))
            with ExitStack() as p1:
                E1 = p1.enter_context
                self.xbf_s = Slot(E1(nc.sbuf_tensor(self.nt("b_xbf"), [128, D], BF16)))
                wqkv = E1(nc.sbuf_tensor(self.nt("b_wqkv"), [128, 8, 1536], BF16))
                wo = E1(nc.sbuf_tensor(self.nt("b_wo"), [128, 4, D], BF16))
                qsb = E1(nc.sbuf_tensor(self.nt("b_qsb"), [128, 64], F32))
                ksb = E1(nc.sbuf_tensor(self.nt("b_ksb"), [128, 64], F32))
                qsrow = E1(nc.sbuf_tensor(self.nt("b_qsrow"), [1, 128], F32))
                kT = E1(nc.sbuf_tensor(self.nt("b_kT"), [128, 4, SEQ], BF16))
                vfull = E1(nc.sbuf_tensor(self.nt("b_v"), [128, NT, 512], BF16))
                qf = E1(nc.sbuf_tensor(self.nt("b_qf"), [128, 512], F32))
                kf = E1(nc.sbuf_tensor(self.nt("b_kf"), [128, 512], F32))
                rtmp = E1(nc.sbuf_tensor(self.nt("b_rtmp"), [128, 512], F32))
                qe = E1(nc.sbuf_tensor(self.nt("b_qe"), [128, 512], BF16))
                ke = E1(nc.sbuf_tensor(self.nt("b_ke"), [128, 512], BF16))
                qT_t = E1(nc.sbuf_tensor(self.nt("b_qT"), [128, 4, 128], BF16))
                scf = E1(nc.sbuf_tensor(self.nt("b_scf"), [128, 1, 640], F32))
                probs = E1(nc.sbuf_tensor(self.nt("b_probs"), [128, 3, 640], BF16))
                pT = E1(nc.sbuf_tensor(self.nt("b_pT"), [128, 3, 5, 128], BF16))
                oT = E1(nc.sbuf_tensor(self.nt("b_oT"), [128, 4, 128], BF16))
                sm = E1(nc.sbuf_tensor(self.nt("b_sm"), [128, 3, 4], F32))
                nrm = E1(nc.sbuf_tensor(self.nt("b_nrm"), [128, 2, 8], F32))
                rot = E1(nc.sbuf_tensor(self.nt("b_rot"), [128, 4, NT, 64], F32))
                rotscr = E1(nc.sbuf_tensor(self.nt("b_rotscr"), [128, 8, 64], F32))
                maskF = E1(nc.sbuf_tensor(self.nt("b_maskF"), [128, 640], BF16))

                self.st.wait(g)
                self.ss.wait(g)
                self.sv.wait(g)
                self.sd.inc(g.dma_start(out=rot[:], in_=self.c_rot), 16)
                self.sd.inc(g.dma_start(out=maskF[:], in_=self.c_maskF), 16)
                self.sd.inc(g.dma_start(out=wqkv[:], in_=self.wf_qkv[li]), 16)
                self.sd.inc(g.dma_start(out=wo[:], in_=self.wf_wo[li]), 16)
                self.sd.inc(g.dma_start(out=qsrow[0:1, 0:64], in_=self.wf_qs[0:1, li, :]), 16)
                self.sd.inc(g.dma_start(out=qsrow[0:1, 64:128], in_=self.wf_ks[0:1, li, :]), 16)
                m = self.next_mm()
                m.acquire(t)
                t.wait_ge(self.sd.h, self.sd.n)
                self.st.inc(t.matmul(m.b[:, 0:128], self.ones_f[:], qsrow[:], start=True, stop=True,
                                     skip_group_check=True))
                self.st.wait(v)
                self.sv.incw(v, v.tensor_scalar(out=qsb[:], in0=m.b[:, 0:64], scalar1=8.0, scalar2=None,
                                            op0=ALU.mult, op1=ALU.bypass))
                self.sv.incw(v, v.tensor_copy(out=ksb[:], in_=m.b[:, 64:128]))
                m.release(self.sv)

                qf_s, kf_s, rtmp_s = Slot(qf), Slot(kf), Slot(rtmp)
                qe_s, ke_s, qT_s = Slot(qe), Slot(ke), Slot(qT_t)
                oT_s = Slot(oT)
                scf_s = [Slot(None), Slot(None)]
                probs_s = [Slot(None), Slot(None), Slot(None)]
                pT_s = [Slot(None), Slot(None), Slot(None)]
                vf_s, kTf_s = Slot(vfull), Slot(kT)

                for T in range(NT):
                    self.lazy_xt(T)

                    def hsl(db, T=T):
                        off = 1 if db < 4 else 0
                        return self.XT[:, db, off + 128 * T: off + 128 * T + 128]

                    for part in range(3):
                        m = self.next_mm()
                        m.acquire(t)
                        self.sv.wait(t)
                        for db in range(8):
                            self.st.inc(t.matmul(m.b[:], hsl(db), wqkv[:, db, 512 * part:512 * part + 512],
                                                 start=(db == 0), stop=(db == 7), skip_group_check=True))
                        if part == 0:
                            qf_s.acquire(s)
                            self.st.wait(s)
                            self.ss.incw(s, s.copy(out=qf[:], in_=m.b[:]))
                            m.release(self.ss)
                        elif part == 1:
                            kf_s.acquire(v)
                            self.st.wait(v)
                            self.sv.incw(v, v.tensor_copy(out=kf[:], in_=m.b[:]))
                            m.release(self.sv)
                        else:
                            vf_s.acquire(v)
                            self.st.wait(v)
                            self.sv.incw(v, v.tensor_copy(out=vfull[:, T, :], in_=m.b[:]))
                            m.release(self.sv)

                    for which in range(2):
                        src = qf if which == 0 else kf
                        src_s = qf_s if which == 0 else kf_s
                        dst = qe if which == 0 else ke
                        dst_s = qe_s if which == 0 else ke_s
                        scl = qsb if which == 0 else ksb
                        ci, si = (0, 1) if which == 0 else (2, 3)
                        src3 = src[:].rearrange("p (h f) -> p h f", h=8)
                        scf_s[0].acquire(v)
                        self.ss.wait(v)
                        sqv = scf[:, 0, 0:512].rearrange("p (h f) -> p h f", h=8)
                        self.sv.incw(v, v.tensor_tensor(out=sqv, in0=src3, in1=src3, op=ALU.mult))
                        self.sv.incw(v, v.reduce_sum(out=nrm[:, 0, :], in_=sqv, axis=AX.X))
                        self.powx(nrm[:, 1, :], nrm[:, 0, :], self.zero1[:], -0.5)
                        self.ss.wait(v)
                        self.sv.incw(v, v.tensor_scalar(out=nrm[:, 1, :], in0=nrm[:, 1, :],
                                                    scalar1=self.clipmax[:, 0:1], scalar2=None,
                                                    op0=ALU.min, op1=ALU.bypass))
                        qh = sqv
                        rb = bcast_last(nrm[:, 1, :].rearrange("p (h a) -> p h a", a=1), 64)
                        self.sv.incw(v, v.tensor_tensor(out=qh, in0=src3, in1=rb, op=ALU.mult))
                        src_s.release(self.sv)
                        self.sv.incw(v, v.tensor_tensor(out=qh, in0=qh, in1=bcast_mid(scl[:], 8), op=ALU.mult))
                        cosb = bcast_mid(rot[:, ci, T, :], 8)
                        sin_lo = bcast_mid(rot[:, si, T, 0:32], 8)
                        sin_hi = bcast_mid(rot[:, si, T, 32:64], 8)
                        rtmp_s.acquire(v)
                        tmp3 = rtmp[:].rearrange("p (h f) -> p h f", h=8)
                        self.sv.incx(v, v.tensor_tensor(out=tmp3, in0=qh, in1=cosb, op=ALU.mult))
                        self.sv.incx(v, v.tensor_tensor(out=rotscr[:, :, 0:32], in0=qh[:, :, 0:32],
                                                    in1=sin_hi, op=ALU.mult))
                        self.sv.incw(v, v.tensor_tensor(out=rotscr[:, :, 32:64], in0=qh[:, :, 32:64],
                                                    in1=sin_lo, op=ALU.mult))
                        scf_s[0].release(self.sv)
                        self.sv.incx(v, v.tensor_tensor(out=tmp3[:, :, 32:64], in0=tmp3[:, :, 32:64],
                                                    in1=rotscr[:, :, 0:32], op=ALU.add))
                        self.sv.incw(v, v.tensor_tensor(out=tmp3[:, :, 0:32], in0=tmp3[:, :, 0:32],
                                                    in1=rotscr[:, :, 32:64], op=ALU.subtract))
                        dst_s.acquire(v)
                        self.sv.incx(v, v.tensor_copy(out=dst[:], in_=rtmp[:]))
                        rtmp_s.release(self.sv)
                        if which == 0:
                            qT_s.acquire(v)
                            qT_s.acquire(s)
                            self.sv.wait(t)
                            self.transpose_to([qe[:, 128 * cc:128 * cc + 128] for cc in range(4)],
                                              [qT_t[:, cc, :] for cc in range(4)], self.ident_bf[:])
                        else:
                            kTf_s.acquire(v)
                            kTf_s.acquire(s)
                            self.sv.wait(t)
                            self.transpose_to([ke[:, 128 * cc:128 * cc + 128] for cc in range(4)],
                                              [kT[:, cc, 128 * T:128 * T + 128] for cc in range(4)],
                                              self.ident_bf[:])
                        dst_s.release(self.st)

                    nk = min(T, 4)
                    ncols = 128 * (nk + 1)

                    def attn_tail(h, par, sv_n):
                        hp, hh = h // 2, h % 2
                        opo = self.po if hp % 2 == 0 else self.po2
                        pT_s[par].acquire(v)
                        pT_s[par].acquire(s)
                        t.wait_ge(self.sv.h, sv_n)
                        self.transpose_to([probs[:, par, 128 * kt:128 * kt + 128] for kt in range(nk + 1)],
                                          [pT[:, par, kt, :] for kt in range(nk + 1)], self.ident_bf[:])
                        probs_s[par].release(self.st)
                        t.wait_ge(self.sv.h, self.sv.n)
                        t.wait_ge(self.ss.h, self.ss.n)
                        if hh == 0:
                            opo.acquire(t)
                        tp = (0, 64 * hh)
                        od = opo.b[64 * hh:64 * hh + 64, 0:128]
                        for kt in range(nk + 1):
                            ktabs = T - nk + kt
                            self.st.inc(t.matmul(od, vfull[:, ktabs, 64 * h:64 * h + 64], pT[:, par, kt, :],
                                                 start=(kt == 0), stop=(kt == nk),
                                                 tile_position=tp, skip_group_check=True))
                        pT_s[par].release(self.st)
                        if hh == 1:
                            oT_s.acquire(v)
                            v.wait_ge(self.st.h, self.st.n)
                            self.sv.incx(v, v.tensor_copy(out=oT[:, hp, :], in_=opo.b[:, 0:128]))
                            opo.release(self.sv)

                    from collections import deque
                    pendq = deque()
                    self.sv.wait(t)
                    self.ss.wait(t)
                    for h in range(8):
                        bp, pb = 64 * (h % 2), h // 2
                        par = h % 3
                        mA = self.next_mm()
                        mB = self.next_mm()
                        mA.acquire(t)
                        mB.acquire(t)
                        if nk > 0:
                            self.st.inc(t.matmul(mA.b[:, 0:128 * nk], qT_t[bp:bp + 64, pb, :],
                                                 kT[bp:bp + 64, pb, 128 * (T - nk):128 * T],
                                                 start=True, stop=False, skip_group_check=True))
                            self.st.inc(t.matmul(mA.b[:, 0:128 * nk], self.ident_bf[:],
                                                 maskF[:, 512 - 128 * nk:512],
                                                 start=False, stop=True, skip_group_check=True))
                        self.st.inc(t.matmul(mB.b[:, 0:128], qT_t[bp:bp + 64, pb, :],
                                             kT[bp:bp + 64, pb, 128 * T:128 * T + 128],
                                             start=True, stop=False, skip_group_check=True))
                        self.st.inc(t.matmul(mB.b[:, 0:128], self.ident_bf[:], maskF[:, 512:640],
                                             start=False, stop=True, skip_group_check=True))
                        if len(pendq) >= 2:
                            attn_tail(*pendq.popleft())
                        probs_s[par].acquire(s)
                        self.softmax_psum2(mA.b[:, 0:128 * nk] if nk > 0 else None, mB.b[:, 0:128],
                                           probs_s[par], sm[:, par, :], ncols, probs[:, par, :])
                        mA.release(self.ss)
                        mB.release(self.ss)
                        pendq.append((h, par, self.sv.n))
                    while pendq:
                        attn_tail(*pendq.popleft())
                    for oc in range(2):
                        m = self.next_mm()
                        m.acquire(t)
                        self.sv.wait(t)
                        for fb in range(4):
                            self.st.inc(t.matmul(m.b[:], oT[:, fb, :], wo[:, fb, 512 * oc:512 * oc + 512],
                                                 start=(fb == 0), stop=(fb == 3), skip_group_check=True))
                        self.st.wait(v)
                        self.sv.incw(v, v.tensor_tensor(out=self.X[:, T, 512 * oc:512 * oc + 512],
                                                    in0=m.b[:], in1=self.X[:, T, 512 * oc:512 * oc + 512],
                                                    op=ALU.add))
                        m.release(self.sv)
                    oT_s.release(self.st)
                    vf_s.release(self.st)
                    kTf_s.release(self.st)
                    qT_s.release(self.st)
            if FFN_ON:
                self.stage_B_ffn(b, li)

    def stage_B_ffn(self, b, li):
        nc, t, v, s, g, y = self.nc, self.t, self.v, self.s, self.g, self.y
        with ExitStack() as c:
            E = c.enter_context
            rows_s = E(nc.sbuf_tensor(self.nt("b_rows_s"), [1, SEQ], BF16))
            rows_nm = E(nc.sbuf_tensor(self.nt("b_rows_nm"), [1, SEQ], BF16))
            rstd_all = E(nc.sbuf_tensor(self.nt("b_rstd"), [128, NT], F32))
            lgw1 = E(nc.sbuf_tensor(self.nt("b_lgw1"), [1, HID2], BF16))
            lb1 = E(nc.sbuf_tensor(self.nt("b_lb1"), [1, HID2], BF16))
            b2r = E(nc.sbuf_tensor(self.nt("b_b2r"), [1, D], BF16))
            lglb = E(nc.sbuf_tensor(self.nt("b_lglb"), [128, 8, 2], BF16))
            histat = E(nc.sbuf_tensor(self.nt("b_histat"), [128, 2, 2], F32))
            psub = E(nc.sbuf_tensor(self.nt("b_psub"), [128, 128], F32))
            p127 = E(nc.sbuf_tensor(self.nt("b_p127"), [128, 128], F32))
            stt3 = E(nc.sbuf_tensor(self.nt("b_stt3"), [128, 2], F32))
            bn = self.bn

            self.st.wait(g)
            self.ss.wait(g)
            self.sv.wait(g)
            self.sd.inc(g.dma_start(out=psub[:], in_=self.c_psub), 16)
            self.sd.inc(g.dma_start(out=p127[:], in_=self.c_p127), 16)
            self.sd.inc(g.dma_start(out=lb1[:], in_=self.wf_b1[0:1, li, :]), 16)
            self.sd.inc(g.dma_start(out=b2r[:], in_=self.wf_b2[0:1, li, :]), 16)
            self.sd.wait(s)
            for db in range(8):
                self.ss.incw(s, s.copy(out=lglb[:, db, 0:1], in_=self.wf_lng_sb[:, li, db:db + 1]))
                self.ss.incw(s, s.copy(out=lglb[:, db, 1:2], in_=self.wf_lnb_sb[:, li, db:db + 1]))

            self.sv.incw(v, v.memset(self.XT[:, :, 0:1], 0.0))
            for T in range(NT):
                for db in range(8):
                    m = self.next_mm()
                    m.acquire(t)
                    self.sv.wait(t)
                    self.st.inc(t.transpose(m.b[:, 0:128], self.X[:, T, 128 * db:128 * db + 128],
                                            self.ident_f[:]))
                    self.st.wait(s)
                    self.ss.incx(s, s.activation(out=self.XT[:, db, 1 + 128 * T:1 + 128 * T + 128],
                                             in_=m.b[:, 0:128], func=AF.Identity,
                                             bias=self.zero1[:], scale=self.wf_lng_sb[:, li, db:db + 1]))
                    m.release(self.ss)

            hist_s, stt3_s = Slot(histat), Slot(stt3)
            for T in range(NT):
                self.ss.wait(v)
                self.sv.incw(v, v.bn_stats(bn[:, 0:6], self.X[:, T, 0:512]))
                self.sv.incw(v, v.bn_stats(bn[:, 6:12], self.X[:, T, 512:1024]))
                self.sv.incw(v, v.bn_aggr(bn[:, 12:14], bn[:, 0:6]))
                hist_s.acquire(v)
                self.sv.incw(v, v.bn_aggr(histat[:, T % 2, :], bn[:, 6:12]))
                self.po.acquire(t)
                self.sv.wait(t)
                self.st.inc(t.matmul(self.po.b[:, 0:2], psub[:], histat[:, T % 2, :],
                                     start=True, stop=(T == 0), skip_group_check=True))
                if T > 0:
                    self.st.inc(t.matmul(self.po.b[:, 0:2], p127[:], histat[:, (T - 1) % 2, :],
                                         start=False, stop=True, skip_group_check=True))
                hist_s.release(self.st)
                self.st.wait(v)
                self.sv.incw(v, v.tensor_copy(out=bn[:, 14:16], in_=self.po.b[:, 0:2]))
                self.po.release(self.sv)
                self.ln_combine(bn)
                stt3_s.acquire(s)
                self.powx(stt3[:, 0:1], bn[:, 17:18], self.eps1[:], +0.5)
                self.ss.wait(v)
                self.sv.incw(v, v.reciprocal(out=rstd_all[:, T:T + 1], in_=stt3[:, 0:1]))
                self.sv.incw(v, v.tensor_scalar(out=stt3[:, 1:2], in0=bn[:, 16:17], scalar1=-1.0, scalar2=None,
                                            op0=ALU.mult, op1=ALU.bypass))
                for which in range(2):
                    m = self.next_mm()
                    m.acquire(t)
                    self.sv.wait(t)
                    self.st.inc(t.transpose(m.b[0:1, 0:128], stt3[:, which:which + 1], self.ident_f[:]))
                    self.st.wait(s)
                    dst = rows_s if which == 0 else rows_nm
                    self.ss.incw(s, s.copy(out=dst[0:1, 128 * T:128 * T + 128], in_=m.b[0:1, 0:128]))
                    m.release(self.ss)
                stt3_s.release(self.st)

            w2 = E(nc.sbuf_tensor(self.nt("b_w2"), [128, 22, D], BF16))
            w1c = [E(nc.sbuf_tensor(self.nt("b_w1c"), [128, 8, 512], BF16)) for _ in range(3)]
            hidc = [E(nc.sbuf_tensor(self.nt("b_hidc"), [128, 512], BF16)) for _ in range(3)]
            hTb = E(nc.sbuf_tensor(self.nt("b_hTb"), [128, 3, 4, 128], BF16))
            gel = E(nc.sbuf_tensor(self.nt("b_gel"), [128, 3, 512], BF16))
            w1_s = [Slot(w) for w in w1c]
            hid_s = [Slot(h) for h in hidc]
            self.hidc_cur = hidc
            self.hid_s_cur = hid_s
            hT_s = [Slot(None), Slot(None), Slot(None)]
            gel_s = [Slot(None), Slot(None), Slot(None)]
            self.sy_.wait(g)
            self.st.wait(g)
            self.sd.inc(g.dma_start(out=w2[:], in_=self.wf_w2[li]), 16)
            w2_ready = self.sd.n
            rows_done = [False] * 6
            CH = [(512 * p_, min(512 * p_ + 512, HID)) for p_ in range(6)]
            for sub in range(8):
                tiles = (2 * sub, 2 * sub + 1)
                dacc = [self.next_mm() for _ in range(4)]
                for dm in dacc:
                    dm.acquire(t)
                from collections import deque
                pendq = deque()
                for p_, (a0, a1) in enumerate(CH):
                    wch = a1 - a0
                    ia, ig = (2 * p_) % 3, (2 * p_ + 1) % 3
                    w1_s[ia].acquire(g)
                    self.sd.inc(g.dma_start(out=w1c[ia][:], in_=self.w1_bf.ap()[li, :, :, 2 * p_, :]), 16)
                    w1_s[ig].acquire(g)
                    self.sd.inc(g.dma_start(out=w1c[ig][:], in_=self.w1_bf.ap()[li, :, :, 2 * p_ + 1, :]), 16)
                    ld_n = self.sd.n
                    if sub == 0 and not rows_done[p_]:
                        t.wait_ge(self.sd.h, ld_n)
                        self.ss.wait(t)
                        for which in range(2):
                            m2 = self.po if which == 0 else self.po2
                            m2.acquire(t)
                            for db in range(8):
                                self.st.inc(t.matmul(m2.b[0:1, 0:wch], lglb[:, db, 0:1],
                                                     w1c[ia if which == 0 else ig][:, db, 0:wch],
                                                     start=(db == 0), stop=(db == 7), skip_group_check=True))
                        self.st.wait(v)
                        self.sv.incx(v, v.tensor_copy(out=lgw1[0:1, a0:a0 + wch], in_=self.po.b[0:1, 0:wch]))
                        self.sv.incx(v, v.tensor_copy(out=lgw1[0:1, HID + a0:HID + a0 + wch],
                                                      in_=self.po2.b[0:1, 0:wch]))
                        self.po.release(self.sv)
                        self.po2.release(self.sv)
                        for which in range(2):
                            m2 = self.po if which == 0 else self.po2
                            m2.acquire(t)
                            for db in range(8):
                                self.st.inc(t.matmul(m2.b[0:1, 0:wch], lglb[:, db, 1:2],
                                                     w1c[ia if which == 0 else ig][:, db, 0:wch],
                                                     start=(db == 0), stop=(db == 7), skip_group_check=True))
                        self.st.wait(v)
                        self.sv.incx(v, v.tensor_tensor(out=lb1[0:1, a0:a0 + wch], in0=self.po.b[0:1, 0:wch],
                                                        in1=lb1[0:1, a0:a0 + wch], op=ALU.add))
                        self.sv.incx(v, v.tensor_tensor(out=lb1[0:1, HID + a0:HID + a0 + wch],
                                                        in0=self.po2.b[0:1, 0:wch],
                                                        in1=lb1[0:1, HID + a0:HID + a0 + wch], op=ALU.add))
                        self.po.release(self.sv)
                        self.po2.release(self.sv)
                        rows_done[p_] = True
                    for ti, T in enumerate(tiles):
                        it = (2 * p_ + ti) % 3

                        def hsl2(db, T=T):
                            off = 1 if db < 4 else 0
                            return self.XT[:, db, off + 128 * T: off + 128 * T + 128]
                        self.po.acquire(t)
                        self.po2.acquire(t)
                        t.wait_ge(self.sd.h, ld_n)
                        self.sv.wait(t)
                        self.ss.wait(t)
                        for which, mmx in ((0, self.po), (1, self.po2)):
                            c0 = a0 if which == 0 else HID + a0
                            for db in range(8):
                                self.st.inc(t.matmul(mmx.b[:, 0:wch], hsl2(db),
                                                     w1c[ia if which == 0 else ig][:, db, 0:wch],
                                                     start=(db == 0), stop=False, skip_group_check=True))
                            self.st.inc(t.matmul(mmx.b[:, 0:wch], rows_nm[0:1, 128 * T:128 * T + 128],
                                                 lgw1[0:1, c0:c0 + wch], start=False, stop=False,
                                                 skip_group_check=True))
                            self.st.inc(t.matmul(mmx.b[:, 0:wch], rows_s[0:1, 128 * T:128 * T + 128],
                                                 lb1[0:1, c0:c0 + wch], start=False, stop=True,
                                                 skip_group_check=True))
                        st_w1 = self.st.n
                        if len(pendq) > 1:
                            self.emit_w2_part(*pendq.popleft(), dacc, w2, hTb, hT_s)
                        gel_s[it].acquire(s)
                        s.wait_ge(self.st.h, st_w1)
                        self.ss.incx(s, s.activation(out=gel[:, it, 0:wch], in_=self.po2.b[:, 0:wch],
                                                     func=AF.Gelu, bias=self.zero1[:],
                                                     scale=rstd_all[:, T:T + 1]))
                        self.po2.release(self.ss)
                        hid_s[it].acquire(v)
                        self.ss.wait(v)
                        self.sv.incx(v, v.scalar_tensor_tensor(out=hidc[it][:, 0:wch], in0=self.po.b[:, 0:wch],
                                                               scalar=rstd_all[:, T:T + 1], in1=gel[:, it, 0:wch],
                                                               op0=ALU.mult, op1=ALU.mult))
                        self.po.release(self.sv)
                        gel_s[it].release(self.sv)
                        pendq.append((it, p_, ti, wch))
                    w1_s[ia].release(self.st)
                    w1_s[ig].release(self.st)
                while pendq:
                    self.emit_w2_part(*pendq.popleft(), dacc, w2, hTb, hT_s)
                for ti, T in enumerate(tiles):
                    for oc in range(2):
                        dm = dacc[2 * ti + oc]
                        self.st.inc(t.matmul(dm.b[:], self.ones_bf[:], b2r[0:1, 512 * oc:512 * oc + 512],
                                             start=False, stop=True, skip_group_check=True))
                        self.st.wait(v)
                        self.sv.incw(v, v.tensor_tensor(out=self.X[:, T, 512 * oc:512 * oc + 512],
                                                        in0=dm.b[:], in1=self.X[:, T, 512 * oc:512 * oc + 512],
                                                        op=ALU.add))
                        dm.release(self.sv)

    def stage_C(self, b):
        nc, t, v, s, g, y = self.nc, self.t, self.v, self.s, self.g, self.y
        with ExitStack() as c:
            E = c.enter_context
            x1t = E(nc.sbuf_tensor(self.nt("c_x1"), [128, 4, D], F32))
            self.bn = E(nc.sbuf_tensor(self.nt("c_bn"), [128, 24], F32))
            x1_s = Slot(x1t)
            self.sv.wait(y)
            self.ss.wait(y)
            self.st.wait(y)
            for tc in range(4):
                x1_s.acquire(y)
                self.sy_.inc(y.dma_start(out=x1t[:], in_=self.x1_spill.ap()[:, 4 * tc:4 * tc + 4, :]), 16)
                self.sy_.wait(v)
                for tt in range(4):
                    T = 4 * tc + tt
                    self.ss.wait(v)
                    self.sv.incw(v, v.tensor_tensor(out=self.X[:, T, :], in0=self.X[:, T, :],
                                                in1=x1t[:, tt, :], op=ALU.add))
                    self.ln_apply_tile(T)
                x1_s.release(self.sv)

    def stage_D(self, b):
        nc, t, v, s, g, y = self.nc, self.t, self.v, self.s, self.g, self.y
        with ExitStack() as c:
            E = c.enter_context
            self.bn = E(nc.sbuf_tensor(self.nt("d_bn"), [128, 24], F32))
            self.xbf_s = Slot(E(nc.sbuf_tensor(self.nt("d_xbf"), [128, D], BF16)))
            qT_pad = E(nc.sbuf_tensor(self.nt("d_qT"), [128, 4, SEQ], BF16))
            kT_pad = E(nc.sbuf_tensor(self.nt("d_kT"), [128, 4, SEQ], BF16))
            v65 = E(nc.sbuf_tensor(self.nt("d_v65"), [128, NT, 16, 65], BF16))
            zz = E(nc.sbuf_tensor(self.nt("d_zz"), [128, 2], F32))
            eu = E(nc.sbuf_tensor(self.nt("d_eu"), [128, D], F32))
            eu_s = Slot(eu)
            qTp_s, kTp_s, v65_s = Slot(qT_pad), Slot(kT_pad), Slot(v65)

            self.st.wait(g)
            self.ss.wait(g)
            self.sv.wait(g)
            with ExitStack() as dproj:
                E1 = dproj.enter_context
                gb = E1(nc.sbuf_tensor(self.nt("d_gb"), [128, 4, 16], F32))
                gbrow = E1(nc.sbuf_tensor(self.nt("d_gbrow"), [1, 128], F32))
                qf = E1(nc.sbuf_tensor(self.nt("d_qf"), [128, 256], F32))
                q2 = E1(nc.sbuf_tensor(self.nt("d_q2"), [128, 256], F32))
                fmst = E1(nc.sbuf_tensor(self.nt("d_fmst"), [128, 4, 16], F32))
                qpad = E1(nc.sbuf_tensor(self.nt("d_qpad"), [128, 16, 32], BF16))
                wq_sb = E1(nc.sbuf_tensor(self.nt("d_wq"), [128, 8, 256], BF16))
                wk_sb = E1(nc.sbuf_tensor(self.nt("d_wk"), [128, 8, 256], BF16))
                wv_sb = E1(nc.sbuf_tensor(self.nt("d_wv"), [128, 8, D], BF16))
                self.sd.inc(g.dma_start(out=wq_sb[:], in_=self.w1_qT), 16)
                self.sd.inc(g.dma_start(out=wk_sb[:], in_=self.w1_kT), 16)
                self.sd.inc(g.dma_start(out=wv_sb[:], in_=self.w1_vT), 16)
                self.sd.inc(g.dma_start(out=gbrow[0:1, 0:16], in_=self.w1_g), 16)
                self.sd.inc(g.dma_start(out=gbrow[0:1, 16:32], in_=self.w1_b), 16)
                m = self.next_mm()
                m.acquire(t)
                t.wait_ge(self.sd.h, self.sd.n)
                self.st.inc(t.matmul(m.b[:, 0:32], self.ones_f[:], gbrow[0:1, 0:32], start=True, stop=True,
                                     skip_group_check=True))
                self.st.wait(v)
                self.sv.incw(v, v.tensor_scalar(out=gb[:, 0, :], in0=m.b[:, 0:16], scalar1=0.25, scalar2=None,
                                            op0=ALU.mult, op1=ALU.bypass))
                self.sv.incw(v, v.tensor_scalar(out=gb[:, 1, :], in0=m.b[:, 16:32], scalar1=0.25, scalar2=None,
                                            op0=ALU.mult, op1=ALU.bypass))
                self.sv.incw(v, v.tensor_copy(out=gb[:, 2, :], in_=m.b[:, 0:16]))
                self.sv.incw(v, v.tensor_copy(out=gb[:, 3, :], in_=m.b[:, 16:32]))
                m.release(self.sv)
                self.sv.incw(v, v.memset(qpad[:], 0.0))
                self.sv.incw(v, v.memset(v65[:], 1.0))

                qf_s, q2_s, qp_s = Slot(qf), Slot(q2), Slot(qpad)

                for T in range(NT):
                    self.lazy_xt(T)
                    for which in range(2):
                        wsb = wq_sb if which == 0 else wk_sb
                        m = self.next_mm()
                        m.acquire(t)
                        self.sv.wait(t)
                        for db in range(8):
                            self.st.inc(t.matmul(m.b[:, 0:256], self.XT[:, db, 1 + 128 * T:1 + 128 * T + 128],
                                                 wsb[:, db, :], start=(db == 0), stop=(db == 7),
                                                 skip_group_check=True))
                        qf_s.acquire(v)
                        self.st.wait(v)
                        self.sv.incw(v, v.tensor_copy(out=qf[:], in_=m.b[:, 0:256]))
                        m.release(self.sv)
                        q3 = qf[:].rearrange("p (h f) -> p h f", h=16)
                        q2_s.acquire(s)
                        self.sv.wait(s)
                        self.ss.incw(s, s.square(out=q2[:], in_=qf[:]))
                        q23 = q2[:].rearrange("p (h f) -> p h f", h=16)
                        self.ss.wait(v)
                        self.sv.incx(v, v.reduce_sum(out=fmst[:, 0, :], in_=q3, axis=AX.X))
                        self.sv.incw(v, v.reduce_sum(out=fmst[:, 1, :], in_=q23, axis=AX.X))
                        q2_s.release(self.sv)
                        self.sv.incw(v, v.tensor_scalar(out=fmst[:, 0, :], in0=fmst[:, 0, :], scalar1=1.0 / 16,
                                                    scalar2=None, op0=ALU.mult, op1=ALU.bypass))
                        self.sv.incw(v, v.tensor_scalar(out=fmst[:, 1, :], in0=fmst[:, 1, :], scalar1=1.0 / 16,
                                                    scalar2=None, op0=ALU.mult, op1=ALU.bypass))
                        self.sv.incw(v, v.tensor_tensor(out=fmst[:, 2, :], in0=fmst[:, 0, :], in1=fmst[:, 0, :],
                                                    op=ALU.mult))
                        self.sv.incw(v, v.tensor_tensor(out=fmst[:, 2, :], in0=fmst[:, 1, :], in1=fmst[:, 2, :],
                                                    op=ALU.subtract))
                        self.powx(fmst[:, 3, :], fmst[:, 2, :], self.eps1[:], -0.5)
                        mu_b = bcast_last(fmst[:, 0, :].rearrange("p (h a) -> p h a", a=1), 16)
                        rs_b = bcast_last(fmst[:, 3, :].rearrange("p (h a) -> p h a", a=1), 16)
                        g_b = bcast_mid(gb[:, 0 if which == 0 else 2, :], 16)
                        b_b = bcast_mid(gb[:, 1 if which == 0 else 3, :], 16)
                        self.ss.wait(v)
                        self.sv.incw(v, v.tensor_tensor(out=q3, in0=q3, in1=mu_b, op=ALU.subtract))
                        self.sv.incw(v, v.tensor_tensor(out=q3, in0=q3, in1=rs_b, op=ALU.mult))
                        self.sv.incw(v, v.tensor_tensor(out=q3, in0=q3, in1=g_b, op=ALU.mult))
                        qp_s.acquire(v)
                        self.sv.incw(v, v.tensor_tensor(out=qpad[:, :, 0:16], in0=q3, in1=b_b, op=ALU.add))
                        qf_s.release(self.sv)
                        qpf = qpad[:].rearrange("p h f -> p (h f)")
                        dstT = qT_pad if which == 0 else kT_pad
                        (qTp_s if which == 0 else kTp_s).acquire(v)
                        (qTp_s if which == 0 else kTp_s).acquire(s)
                        self.sv.wait(t)
                        self.transpose_to([qpf[:, 128 * cc:128 * cc + 128] for cc in range(4)],
                                          [dstT[:, cc, 128 * T:128 * T + 128] for cc in range(4)],
                                          self.ident_bf[:])
                        qp_s.release(self.st)
                    v65_s.acquire(v)
                    for vc in range(2):
                        m = self.next_mm()
                        m.acquire(t)
                        self.sv.wait(t)
                        for db in range(8):
                            self.st.inc(t.matmul(m.b[:], self.XT[:, db, 1 + 128 * T:1 + 128 * T + 128],
                                                 wv_sb[:, db, 512 * vc:512 * vc + 512],
                                                 start=(db == 0), stop=(db == 7), skip_group_check=True))
                        self.st.wait(v)
                        dst = v65[:, T, 8 * vc:8 * vc + 8, 0:64]
                        self.sv.incw(v, v.tensor_copy(out=dst, in_=m.b[:].rearrange("p (h f) -> p h f", h=8)))
                        m.release(self.sv)

            s2T = E(nc.sbuf_tensor(self.nt("d_s2T"), [128, 3, 512], BF16))
            triu = E(nc.sbuf_tensor(self.nt("d_triu"), [128, 128], BF16))
            onorm = E(nc.sbuf_tensor(self.nt("d_on"), [128, 4, 16, 64], BF16))
            oev = E(nc.sbuf_tensor(self.nt("d_oev"), [128, 4, 65], F32))
            rz4 = E(nc.sbuf_tensor(self.nt("d_rz4"), [128, 4, 1], F32))
            oTd = E(nc.sbuf_tensor(self.nt("d_oT"), [128, 8, 128], BF16))
            wo_sb = E(nc.sbuf_tensor(self.nt("d_wo"), [128, 8, D], BF16))
            s2_s = [Slot(None), Slot(None), Slot(None)]
            on_s, oT_s, oev_s = Slot(onorm), Slot(oTd), Slot(oev)
            self.st.wait(g)
            self.sd.inc(g.dma_start(out=triu[:], in_=self.c_triu), 16)
            self.sd.inc(g.dma_start(out=wo_sb[:], in_=self.w1_oT), 16)
            self.sd.wait(t)
            for qc in range(4):
                on_s.acquire(v)
                self.sv.wait(t)
                self.ss.wait(t)
                for h in range(16):
                    bp, pb = 32 * (h % 4), h // 4
                    nkt = 4 * qc + 4
                    opo = [self.mm[j] for j in range(4)]
                    for j in range(4):
                        opo[j].acquire(t)
                    def omm_d(kt, was_diag, used_v=False):
                        sb_i = kt % 3
                        if was_diag or used_v:
                            self.sv.wait(t)
                        else:
                            self.ss.wait(t)
                        for j in range(4):
                            qtile = 4 * qc + j
                            if kt > qtile:
                                continue
                            self.st.inc(t.matmul(opo[j].b[:, 0:65], s2T[:, sb_i, 128 * j:128 * j + 128],
                                                 v65[:, kt, h, :], start=(kt == 0), stop=(kt == qtile),
                                                 skip_group_check=True))
                        s2_s[sb_i].release(self.st)

                    from collections import deque
                    pq = deque()
                    for kt in range(nkt):
                        sps = self.po if kt % 2 == 0 else self.po2
                        sps.acquire(t)
                        self.st.inc(t.matmul(sps.b[:], kT_pad[bp:bp + 32, pb, 128 * kt:128 * kt + 128],
                                             qT_pad[bp:bp + 32, pb, 512 * qc:512 * qc + 512],
                                             start=True, stop=True, tile_position=(bp, 0),
                                             skip_group_check=True))
                        sb_i = kt % 3
                        jd = kt - 4 * qc
                        is_diag = 0 <= jd < 4
                        c0 = 128 * jd if is_diag else 0
                        s2_s[sb_i].acquire(s)
                        self.st.wait(s)
                        self.ss.incx(s, s.square(out=s2T[:, sb_i, c0:512], in_=sps.b[:, c0:512]))
                        sps.release(self.ss)
                        if is_diag:
                            self.ss.wait(v)
                            self.sv.incx(v, v.tensor_tensor(out=s2T[:, sb_i, 128 * jd:128 * jd + 128],
                                                            in0=s2T[:, sb_i, 128 * jd:128 * jd + 128],
                                                            in1=triu[:], op=ALU.mult))
                        pq.append((kt, is_diag, False))
                        if len(pq) > 2:
                            omm_d(*pq.popleft())
                    while pq:
                        omm_d(*pq.popleft())
                    oev_s.acquire(v)
                    self.st.wait(v)
                    for j in range(4):
                        if j < 3:
                            self.sv.incx(v, v.tensor_copy(out=oev[:, j, :], in_=opo[j].b[:, 0:65]))
                        else:
                            self.sv.incw(v, v.tensor_copy(out=oev[:, j, :], in_=opo[j].b[:, 0:65]))
                        opo[j].release(self.sv)
                    self.sv.incw(v, v.tensor_scalar(out=rz4[:], in0=oev[:, :, 64:65], scalar1=1e-5,
                                                    scalar2=None, op0=ALU.add, op1=ALU.bypass))
                    self.sv.incw(v, v.reciprocal(out=rz4[:], in_=rz4[:]))
                    self.sv.incx(v, v.tensor_tensor(out=onorm[:, :, h, :], in0=oev[:, :, 0:64],
                                                    in1=bcast_last(rz4[:], 64), op=ALU.mult))
                    oev_s.release(self.sv)
                for j in range(4):
                    T = 4 * qc + j
                    onf = onorm[:, j].rearrange("p h f -> p (h f)")
                    oT_s.acquire(v)
                    oT_s.acquire(s)
                    self.sv.wait(t)
                    self.transpose_to([onf[:, 128 * cc:128 * cc + 128] for cc in range(8)],
                                      [oTd[:, cc, :] for cc in range(8)], self.ident_bf[:])
                    for oc in range(2):
                        m = self.next_mm()
                        m.acquire(t)
                        self.sv.wait(t)
                        self.ss.wait(t)
                        for fb in range(8):
                            self.st.inc(t.matmul(m.b[:], oTd[:, fb, :], wo_sb[:, fb, 512 * oc:512 * oc + 512],
                                                 start=(fb == 0), stop=(fb == 7), skip_group_check=True))
                        self.st.wait(v)
                        self.sv.incw(v, v.tensor_tensor(out=self.X[:, T, 512 * oc:512 * oc + 512],
                                                        in0=m.b[:], in1=self.X[:, T, 512 * oc:512 * oc + 512],
                                                        op=ALU.add))
                        m.release(self.sv)
                    oT_s.release(self.st)
                    self.ln_apply_tile(T)
                on_s.release(self.st)

    def stage_D_eprep(self, b):
        nc, t, v, s, g, y = self.nc, self.t, self.v, self.s, self.g, self.y
        with ExitStack() as c:
            E = c.enter_context
            self.bn = E(nc.sbuf_tensor(self.nt("de_bn"), [128, 24], F32))
            gbc = E(nc.sbuf_tensor(self.nt("de_gbc"), [128, 2, D], F32))
            eu = E(nc.sbuf_tensor(self.nt("de_eu"), [128, D], F32))
            zz = E(nc.sbuf_tensor(self.nt("de_zz"), [128, 2], F32))
            self.sv.wait(g)
            self.ss.wait(g)
            self.st.wait(g)
            self.build_gbc(gbc)
            eu_s = Slot(eu)
            bn = self.bn
            for T in range(NT):
                self.ss.wait(v)
                self.sv.incw(v, v.bn_stats(bn[:, 0:6], self.X[:, T, 0:512]))
                self.sv.incw(v, v.bn_stats(bn[:, 6:12], self.X[:, T, 512:1024]))
                self.sv.incw(v, v.bn_aggr(bn[:, 12:14], bn[:, 0:6]))
                self.sv.incw(v, v.bn_aggr(bn[:, 14:16], bn[:, 6:12]))
                self.ln_combine(bn)
                rstd3 = self.stats_e[:, b, T, 0:1]
                nmr3 = self.stats_e[:, b, T, 1:2]
                self.powx(rstd3, bn[:, 17:18], self.eps1[:], -0.5)
                self.ss.wait(v)
                self.sv.incw(v, v.tensor_tensor(out=nmr3, in0=bn[:, 16:17], in1=rstd3, op=ALU.mult))
                self.sv.incw(v, v.tensor_scalar(out=nmr3, in0=nmr3, scalar1=-1.0, scalar2=None,
                                            op0=ALU.mult, op1=ALU.bypass))
                eu_s.acquire(s)
                self.sv.wait(s)
                self.ss.incw(s, s.activation(out=eu[:], in_=self.X[:, T, :], func=AF.Identity,
                                         bias=nmr3, scale=rstd3))
                self.ss.wait(v)
                self.sv.incw(v, v.tensor_tensor(out=eu[:], in0=eu[:], in1=gbc[:, 0, :], op=ALU.mult))
                self.sv.incw(v, v.tensor_tensor(out=eu[:], in0=eu[:], in1=gbc[:, 1, :], op=ALU.add))
                self.sv.incw(v, v.reduce_max(out=zz[:, 0:1], in_=eu[:], axis=AX.X, apply_absolute_value=True))
                eu_s.release(self.sv)
                self.sv.incw(v, v.tensor_tensor(out=self.gmax[:], in0=self.gmax[:], in1=zz[:, 0:1], op=ALU.max))

    def stage_E_all(self):
        nc, t, v, s, g, y = self.nc, self.t, self.v, self.s, self.g, self.y
        with ExitStack() as c:
            E = c.enter_context
            self.bn = E(nc.sbuf_tensor(self.nt("e_bn"), [128, 24], F32))
            wtf = E(nc.sbuf_tensor(self.nt("e_wtf"), [128, 8, D], F32))
            absc = E(nc.sbuf_tensor(self.nt("e_absc"), [128, D], F32))
            maskc = E(nc.sbuf_tensor(self.nt("e_maskc"), [128, D], F32))
            sgn = E(nc.sbuf_tensor(self.nt("e_sgn"), [128, D], F32))
            sums = E(nc.sbuf_tensor(self.nt("e_sums"), [128, 8], F32))
            sm2 = E(nc.sbuf_tensor(self.nt("e_sm2"), [128, 2], F32))
            smt = E(nc.sbuf_tensor(self.nt("e_smt"), [2, 1], F32))
            al = E(nc.sbuf_tensor(self.nt("e_al"), [1, 4], F32))
            alb = E(nc.sbuf_tensor(self.nt("e_alb"), [128, 2], F32))
            u = E(nc.sbuf_tensor(self.nt("e_u"), [128, D], F32))
            xq = E(nc.sbuf_tensor(self.nt("e_xq"), [128, D], BF16))
            xqT = E(nc.sbuf_tensor(self.nt("e_xqT"), [128, 8, 128], BF16))
            arr = E(nc.sbuf_tensor(self.nt("e_arr"), [1, 128], F32))
            tmpc = E(nc.sbuf_tensor(self.nt("e_tmpc"), [128, 1], F32))
            self.wq_ns = E(nc.sbuf_tensor(self.nt("wq_ns"), [128, 8, D], BF16))

            self.st.wait(g)
            self.ss.wait(g)
            self.sv.wait(g)
            self.gbc = E(nc.sbuf_tensor(self.nt("e_gbc"), [128, 2, D], F32))
            self.build_gbc(self.gbc)
            self.sd.inc(g.dma_start(out=wtf[:], in_=self.we_wT), 16)
            self.sd.wait(s)
            self.sd.wait(v)

            for db in range(8):
                self.sv.wait(s)
                self.ss.incw(s, s.activation(out=absc[:], in_=wtf[:, db, :], func=AF.Abs,
                                         bias=self.zero1[:], scale=1.0))
                self.ss.wait(v)
                self.sv.incw(v, v.reduce_sum(out=sums[:, db:db + 1], in_=absc[:], axis=AX.X))
            self.sv.incw(v, v.reduce_sum(out=sm2[:, 0:1], in_=sums[:], axis=AX.X))
            m = self.next_mm()
            m.acquire(t)
            self.sv.wait(t)
            self.st.inc(t.transpose(m.b[0:1, 0:128], sm2[:, 0:1], self.ident_f[:]))
            self.st.wait(v)
            self.sv.incw(v, v.reduce_sum(out=al[0:1, 0:1], in_=m.b[0:1, 0:128], axis=AX.X))
            m.release(self.sv)
            self.sv.incw(v, v.tensor_scalar(out=al[0:1, 0:1], in0=al[0:1, 0:1],
                                        scalar1=0.7 / (1024.0 * 1024.0), scalar2=None,
                                        op0=ALU.mult, op1=ALU.bypass))
            m = self.next_mm()
            m.acquire(t)
            self.sv.wait(t)
            self.st.inc(t.matmul(m.b[:, 0:1], self.ones_f[:], al[0:1, 0:1], start=True, stop=True,
                                 skip_group_check=True))
            self.st.wait(v)
            self.sv.incw(v, v.tensor_copy(out=alb[:, 0:1], in_=m.b[:, 0:1]))
            m.release(self.sv)
            self.sv.incw(v, v.memset(sm2[:], 0.0))
            for db in range(8):
                self.sv.wait(s)
                self.ss.incw(s, s.activation(out=absc[:], in_=wtf[:, db, :], func=AF.Abs,
                                         bias=self.zero1[:], scale=1.0))
                self.ss.wait(v)
                self.sv.incw(v, v.tensor_scalar(out=maskc[:], in0=absc[:], scalar1=alb[:, 0:1], scalar2=None,
                                            op0=ALU.is_gt, op1=ALU.bypass))
                self.sv.incw(v, v.scalar_tensor_tensor(out=absc[:], in0=absc[:], scalar=1.0, in1=maskc[:],
                                                   op0=ALU.mult, op1=ALU.mult, accum_out=tmpc[:]))
                self.sv.incw(v, v.tensor_tensor(out=sm2[:, 0:1], in0=sm2[:, 0:1], in1=tmpc[:], op=ALU.add))
                self.sv.incw(v, v.tensor_scalar(out=maskc[:], in0=maskc[:], scalar1=1.0, scalar2=0.0,
                                            op0=ALU.mult, op1=ALU.add, accum_out=tmpc[:]))
                self.sv.incw(v, v.tensor_tensor(out=sm2[:, 1:2], in0=sm2[:, 1:2], in1=tmpc[:], op=ALU.add))
            m = self.next_mm()
            m.acquire(t)
            self.sv.wait(t)
            self.st.inc(t.transpose(m.b[0:2, 0:128], sm2[:], self.ident_f[:]))
            self.st.wait(v)
            self.sv.incw(v, v.reduce_sum(out=smt[0:2, 0:1], in_=m.b[0:2, 0:128], axis=AX.X))
            m.release(self.sv)
            m = self.next_mm()
            m.acquire(t)
            self.sv.wait(t)
            self.st.inc(t.transpose(m.b[0:1, 0:2], smt[0:2, 0:1], self.ident_f[0:2, 0:2]))
            self.st.wait(v)
            self.sv.incw(v, v.tensor_scalar(out=al[0:1, 1:2], in0=m.b[0:1, 1:2], scalar1=1.0,
                                        scalar2=None, op0=ALU.max, op1=ALU.bypass))
            self.sv.incw(v, v.reciprocal(out=al[0:1, 1:2], in_=al[0:1, 1:2]))
            self.sv.incw(v, v.tensor_tensor(out=al[0:1, 2:3], in0=m.b[0:1, 0:1], in1=al[0:1, 1:2],
                                        op=ALU.mult))
            m.release(self.sv)
            m = self.next_mm()
            m.acquire(t)
            self.sv.wait(t)
            self.st.inc(t.matmul(m.b[:, 0:1], self.ones_f[:], al[0:1, 2:3], start=True, stop=True,
                                 skip_group_check=True))
            self.st.wait(v)
            self.sv.incw(v, v.tensor_copy(out=alb[:, 1:2], in_=m.b[:, 0:1]))
            m.release(self.sv)
            for db in range(8):
                self.sv.wait(s)
                self.ss.incw(s, s.activation(out=absc[:], in_=wtf[:, db, :], func=AF.Abs,
                                         bias=self.zero1[:], scale=1.0))
                self.ss.incw(s, s.sign(out=sgn[:], in_=wtf[:, db, :]))
                self.ss.wait(v)
                self.sv.incw(v, v.tensor_scalar(out=maskc[:], in0=absc[:], scalar1=alb[:, 0:1], scalar2=None,
                                            op0=ALU.is_gt, op1=ALU.bypass))
                self.sv.incw(v, v.tensor_tensor(out=sgn[:], in0=sgn[:], in1=maskc[:], op=ALU.mult))
                self.sv.incw(v, v.tensor_scalar(out=self.wq_ns[:, db, :], in0=sgn[:], scalar1=alb[:, 1:2],
                                            scalar2=None, op0=ALU.mult, op1=ALU.bypass))

            m = self.next_mm()
            m.acquire(t)
            self.sv.wait(t)
            self.st.inc(t.transpose(m.b[0:1, 0:128], self.gmax[:], self.ident_f[:]))
            self.st.wait(v)
            self.sv.incw(v, v.memset(arr[:], 0.0))
            self.sv.incw(v, v.reduce_max(out=arr[0:1, 0:1], in_=m.b[0:1, 0:128], axis=AX.X))
            m.release(self.sv)
            self.sv.wait(g)
            self.sd.inc(g.dma_start(out=self.ar_in.ap(), in_=arr[:]), 16)
            self.sd.wait(g)
            self.scc.inc(g.collective_compute("AllReduce", ALU.max, replica_groups=[list(range(self.cc_world))],
                                              ins=[self.ar_in.ap().opt()], outs=[self.ar_out.ap().opt()]))
            self.scc.wait(g)
            self.sd.inc(g.dma_start(out=arr[0:1, 0:1], in_=self.ar_out.ap()[0:1, 0:1]), 16)
            self.sd.wait(v)
            self.sv.incw(v, v.tensor_scalar(out=al[0:1, 3:4], in0=arr[0:1, 0:1], scalar1=1e-8,
                                        scalar2=1.0 / 127.0, op0=ALU.max, op1=ALU.mult))
            m = self.next_mm()
            m.acquire(t)
            self.sv.wait(t)
            self.st.inc(t.matmul(m.b[:, 0:1], self.ones_f[:], al[0:1, 3:4], start=True, stop=True,
                                 skip_group_check=True))
            self.st.wait(v)
            self.sv.incw(v, v.tensor_copy(out=self.scale_b[:], in_=m.b[:, 0:1]))
            m.release(self.sv)
            self.sv.incw(v, v.reciprocal(out=self.iscale_b[:], in_=self.scale_b[:]))

            u_s, xq_s, xqT_s = Slot(u), Slot(xq), Slot(xqT)
            for b in range(NB - 1, -1, -1):
                if b != NB - 1:
                    self.sv.wait(y)
                    self.ss.wait(y)
                    self.st.wait(y)
                    for tc in range(4):
                        self.sy_.inc(y.dma_start(out=self.X[:, 4 * tc:4 * tc + 4, :],
                                                 in_=self.x3_spill.ap()[:, 4 * tc:4 * tc + 4, :]), 16)
                    self.sy_.wait(v)
                    self.sy_.wait(s)
                for T in range(NT):
                    st_ap = self.stats_e[:, b, T, :]
                    u_s.acquire(s)
                    self.sv.wait(s)
                    self.ss.incw(s, s.activation(out=u[:], in_=self.X[:, T, :], func=AF.Identity,
                                             bias=st_ap[:, 1:2], scale=st_ap[:, 0:1]))
                    self.ss.wait(v)
                    self.sv.incw(v, v.tensor_tensor(out=u[:], in0=u[:], in1=self.gbc[:, 0, :], op=ALU.mult))
                    self.sv.incw(v, v.tensor_tensor(out=u[:], in0=u[:], in1=self.gbc[:, 1, :], op=ALU.add))
                    self.sv.incw(v, v.tensor_scalar(out=u[:], in0=u[:], scalar1=self.iscale_b[:, 0:1],
                                                scalar2=None, op0=ALU.mult, op1=ALU.bypass))
                    self.sv.incw(v, v.tensor_scalar(out=u[:], in0=u[:], scalar1=127.0, scalar2=-127.0,
                                                op0=ALU.min, op1=ALU.max))
                    xq_s.acquire(s)
                    self.sv.wait(s)
                    self.ss.incw(s, s.activation(out=u[:], in_=u[:], func=AF.Identity,
                                             bias=self.magic[:], scale=1.0))
                    self.ss.incw(s, s.activation(out=xq[:], in_=u[:], func=AF.Identity,
                                             bias=self.nmagic[:], scale=1.0))
                    u_s.release(self.ss)
                    xqT_s.acquire(v)
                    xqT_s.acquire(s)
                    self.ss.wait(t)
                    self.transpose_to([xq[:, 128 * db:128 * db + 128] for db in range(8)],
                                      [xqT[:, db, :] for db in range(8)], self.ident_bf[:])
                    xq_s.release(self.st)
                    for oc in range(2):
                        m = self.next_mm()
                        m.acquire(t)
                        self.sv.wait(t)
                        self.ss.wait(t)
                        for db in range(8):
                            self.st.inc(t.matmul(m.b[:], xqT[:, db, :],
                                                 self.wq_ns[:, db, 512 * oc:512 * oc + 512],
                                                 start=(db == 0), stop=(db == 7), skip_group_check=True))
                        self.st.wait(v)
                        self.sv.incw(v, v.scalar_tensor_tensor(out=self.X[:, T, 512 * oc:512 * oc + 512],
                                                           in0=m.b[:], scalar=self.scale_b[:, 0:1],
                                                           in1=self.X[:, T, 512 * oc:512 * oc + 512],
                                                           op0=ALU.mult, op1=ALU.add))
                        m.release(self.sv)
                    xqT_s.release(self.st)
                    self.ln_apply_tile(T)
                    self.ss.wait(y)
                    self.sy_.inc(y.dma_start(out=self.out[b, :, T, :], in_=self.X[:, T, :]), 16)


def _host_prep(inputs):
    f32 = np.float32
    x = np.ascontiguousarray(inputs["L0_attn_x"], dtype=f32)

    def tokmaj(xb):
        return np.ascontiguousarray(xb.reshape(NT, 128, D).transpose(1, 0, 2))

    def feamaj(xb):
        return np.ascontiguousarray(xb.T.reshape(8, 128, SEQ).transpose(1, 0, 2))

    sh = {}
    sh["wa_qkvT"] = np.ascontiguousarray(
        np.asarray(inputs["L0_attn_Wqkv"], f32).T.reshape(8, 128, 3072).transpose(1, 0, 2))
    sh["wa_outT"] = np.ascontiguousarray(
        np.asarray(inputs["L0_attn_Wout"], f32).T.reshape(8, 128, D).transpose(1, 0, 2))
    bqkv = np.asarray(inputs["L0_attn_bqkv"], f32)
    sh["ba_qkv"] = np.ascontiguousarray(bqkv[:2048].reshape(16, 128).T)
    sh["ba_bv"] = np.ascontiguousarray(bqkv[2048:].reshape(8, 128).T)
    sh["ba_bout"] = np.asarray(inputs["L0_attn_bout"], f32)[None, :]
    sh["wf_qkv"] = np.ascontiguousarray(
        np.asarray(inputs["L0_ffn_wqkv"], f32).reshape(2, 8, 128, 1536).transpose(0, 2, 1, 3))
    sh["wf_wo"] = np.ascontiguousarray(
        np.asarray(inputs["L0_ffn_wo"], f32).reshape(2, 4, 128, D).transpose(0, 2, 1, 3))
    sh["wf_w1"] = np.ascontiguousarray(
        np.asarray(inputs["L0_ffn_w1"], f32).reshape(2, 8, 128, HID2).transpose(0, 2, 1, 3))
    w2p = np.zeros((2, 22 * 128, D), dtype=f32)
    w2p[:, :HID, :] = np.asarray(inputs["L0_ffn_w2"], f32)
    sh["wf_w2"] = np.ascontiguousarray(w2p.reshape(2, 22, 128, D).transpose(0, 2, 1, 3))
    sh["wf_qs"] = np.ascontiguousarray(np.asarray(inputs["L0_ffn_q_scale"], f32)[None])
    sh["wf_ks"] = np.ascontiguousarray(np.asarray(inputs["L0_ffn_k_scale"], f32)[None])
    sh["wf_lng"] = np.ascontiguousarray(
        np.asarray(inputs["L0_ffn_ln_g"], f32).reshape(2, 8, 128).transpose(2, 0, 1))
    sh["wf_lnb"] = np.ascontiguousarray(
        np.asarray(inputs["L0_ffn_ln_b"], f32).reshape(2, 8, 128).transpose(2, 0, 1))
    sh["wf_b1"] = np.ascontiguousarray(np.asarray(inputs["L0_ffn_b1"], f32)[None])
    sh["wf_b2"] = np.ascontiguousarray(np.asarray(inputs["L0_ffn_b2"], f32)[None])
    sh["w1_qT"] = np.ascontiguousarray(
        np.asarray(inputs["L1_attn_Wq"], f32).T.reshape(8, 128, 256).transpose(1, 0, 2))
    sh["w1_kT"] = np.ascontiguousarray(
        np.asarray(inputs["L1_attn_Wk"], f32).T.reshape(8, 128, 256).transpose(1, 0, 2))
    sh["w1_vT"] = np.ascontiguousarray(
        np.asarray(inputs["L1_attn_Wv"], f32).T.reshape(8, 128, D).transpose(1, 0, 2))
    sh["w1_oT"] = np.ascontiguousarray(
        np.asarray(inputs["L1_attn_Wo"], f32).T.reshape(8, 128, D).transpose(1, 0, 2))
    sh["w1_g"] = np.asarray(inputs["L1_attn_gamma"], f32)[None]
    sh["w1_b"] = np.asarray(inputs["L1_attn_beta"], f32)[None]
    sh["we_wT"] = np.ascontiguousarray(
        np.asarray(inputs["L1_ffn_weight"], f32).T.reshape(8, 128, D).transpose(1, 0, 2))
    sh["we_g"] = np.asarray(inputs["L1_ffn_gamma"], f32)[None]
    sh["we_b"] = np.asarray(inputs["L1_ffn_beta"], f32)[None]

    sh["c_ident_bf"] = np.eye(128, dtype=BF)
    sh["c_ident_f"] = np.eye(128, dtype=f32)
    sh["c_ones_bf"] = np.ones((1, 128), dtype=BF)
    sh["c_ones_f"] = np.ones((1, 128), dtype=f32)
    p = np.arange(128)[:, None]
    c256 = np.arange(256)[None, :]
    sh["c_maskA"] = np.where((c256 > p + 64) & (c256 <= p + 128), 0.0, -1e30).astype(BF)
    c128 = np.arange(128)[None, :]
    sh["c_maskA0"] = np.where((c128 <= p) & (c128 > p - 64), 0.0, -1e30).astype(BF)
    c640 = np.arange(640)[None, :]
    sh["c_maskF"] = np.where((c640 >= p) & (c640 <= p + 512), 0.0, -1e30).astype(BF)
    kk = np.arange(128)[:, None]
    qq = np.arange(128)[None, :]
    sh["c_triu"] = (kk <= qq).astype(BF)
    psub = np.zeros((128, 128), dtype=f32)
    psub[np.arange(127), np.arange(1, 128)] = 1.0
    sh["c_psub"] = psub
    e0 = np.zeros((1, 128), dtype=f32)
    e0[0, 0] = 1.0
    sh["c_e0"] = e0
    p127 = np.zeros((128, 128), dtype=f32)
    p127[127, 0] = 1.0
    sh["c_p127"] = p127
    DH, WIN = 64, 512
    half = np.arange(0, DH, 2, dtype=np.float64)
    inv_freq = 1.0 / 10000.0 ** (half / DH)
    tt = np.arange(SEQ, dtype=np.float64)
    freqs = tt[:, None] * inv_freq[None, :]
    freqs = np.concatenate([freqs, freqs], axis=-1)
    base = (half + 0.4 * DH) / (1.4 * DH)
    power = (tt - SEQ // 2) / float(WIN // 2)
    scale = base[None, :] ** power[:, None]
    scale = np.concatenate([scale, scale], axis=-1)
    rot = np.stack([np.cos(freqs) * scale, np.sin(freqs) * scale,
                    np.cos(freqs) / scale, np.sin(freqs) / scale], axis=0).astype(f32)
    sh["c_rot"] = np.ascontiguousarray(rot.reshape(4, NT, 128, 64).transpose(2, 0, 1, 3))

    in_maps = []
    for core in range(NCORES):
        m = dict(sh)
        xb = x[2 * core:2 * core + 2]
        m["x_tok"] = np.stack([tokmaj(xb[0]), tokmaj(xb[1])])
        m["x_feaT"] = np.stack([feamaj(xb[0]), feamaj(xb[1])])
        in_maps.append(m)
    return in_maps


def _run(inputs, stages=ALL_STAGES):
    in_maps = _host_prep(inputs)
    p = P(bass.Bass("TRN2", target_bir_lowering=False, detect_race_conditions=False), stages)
    nc = p.build()
    res = run_bass_kernel_spmd(nc, in_maps, core_ids=list(range(NCORES)))
    outs = []
    for core in range(NCORES):
        o = res.results[core]["out"]
        for b_ in range(NB):
            outs.append(np.ascontiguousarray(o[b_].transpose(1, 0, 2).reshape(SEQ, D)))
    return np.stack(outs).astype(np.float32)


def kernel(**inputs):
    return _run(inputs, ALL_STAGES)
